# version 04
# baseline (speedup 1.0000x reference)
import contextlib
import numpy as np
import ml_dtypes
import concourse.bass as bass
import concourse.mybir as mybir
from concourse.bass_utils import run_bass_kernel_spmd

F32 = mybir.dt.float32
BF16 = mybir.dt.bfloat16
FP8 = mybir.dt.float8e4
AF = mybir.ActivationFunctionType
ALU = mybir.AluOpType
BF = ml_dtypes.bfloat16

T = 1024
HALO = 512
E = 2048
E2 = 4096
DF = 8192
NH = 16
D = 128
NE = E // 128
NT = T // 128
NTH = (T + HALO) // 128
NCH = E2 // 128
NF = DF // 128
TH = NTH * 128
SCALE = float(1.0 / np.sqrt(128.0))
NEG = -1.0e30

ENGS = ("sync", "gpsimd", "vector", "scalar", "tensor")


class Ctr:
    __slots__ = ("v",)
    def __init__(self):
        self.v = 0
    def add(self, n=1):
        self.v += n
        return self.v


class P:
    def __init__(self, nc, stack):
        self.nc = nc
        self.stack = stack
        self.q = {k: [] for k in ENGS}
        self.sems = {}
        self.ctr = {}
        self.nbar = 0
        for name in ("bar", "yL", "gL", "cL", "vC", "sC", "tC", "yS", "gS"):
            self.sems[name] = stack.enter_context(nc.semaphore(name))
            self.ctr[name] = Ctr()

    def on(self, eng, fn):
        self.q[eng].append(fn)

    def wait(self, eng, sem, val):
        h = self.sems[sem]
        self.on(eng, lambda e, h=h, v=val: e.wait_ge(h, v))

    def barrier(self):
        self.nbar += 1
        tgt = 5 * self.nbar
        h = self.sems["bar"]
        for eng in ENGS:
            self.on(eng, lambda e, h=h: e.sem_inc(h, 1))
            self.on(eng, lambda e, h=h, t=tgt: e.wait_ge(h, t))

    def sb(self, name, shape, dtype, stack=None):
        return (stack or self.stack).enter_context(self.nc.sbuf_tensor(name, shape, dtype))

    def ps(self, name, shape, dtype=F32, stack=None):
        return (stack or self.stack).enter_context(self.nc.psum_tensor(name, shape, dtype))

    def emit(self):
        nc = self.nc
        with nc.Block() as block:
            @block.sync
            def _(e):
                for f in self.q["sync"]:
                    f(e)
            @block.gpsimd
            def _(e):
                for f in self.q["gpsimd"]:
                    f(e)
            @block.vector
            def _(e):
                for f in self.q["vector"]:
                    f(e)
            @block.scalar
            def _(e):
                for f in self.q["scalar"]:
                    f(e)
            @block.tensor
            def _(e):
                for f in self.q["tensor"]:
                    f(e)


def _dma(p, eng, sem, out, in_, **kw):
    h = p.sems[sem]
    p.on(eng, lambda e, o=out, i=in_, h=h, kw=kw: e.dma_start(out=o, in_=i, **kw).then_inc(h, 16))
    return p.ctr[sem].add(16)


def _dma_tr(p, sem, out, in_):
    h = p.sems[sem]
    p.on("scalar", lambda e, o=out, i=in_, h=h: e.dma_start(out=o, in_=i, transpose=True).then_inc(h, 16))
    return p.ctr[sem].add(16)


def emit_ln(p, xbuf, dst, sc, scale_ap=None):
    hv, hs = p.sems["vC"], p.sems["sC"]
    vC, sC = p.ctr["vC"], p.ctr["sC"]
    for c in range(4):
        if c < 3:
            p.on("vector", lambda e, c=c: e.bn_stats(out=sc["bn24"][:, 6 * c:6 * (c + 1)],
                                                     in_=xbuf[:, 512 * c:512 * (c + 1)]))
        else:
            p.on("vector", lambda e, c=c, h=hv: e.bn_stats(out=sc["bn24"][:, 6 * c:6 * (c + 1)],
                                                           in_=xbuf[:, 512 * c:512 * (c + 1)]).then_inc(h, 1))
    t0 = vC.add(1)
    p.wait("vector", "vC", t0)
    p.on("vector", lambda e, h=hv: e.bn_aggr(out=sc["mv"][:], in_=sc["bn24"][:]).then_inc(h, 1))
    t1 = vC.add(1)
    p.wait("scalar", "vC", t1)
    p.on("scalar", lambda e, h=hs: e.activation(out=sc["sd"][:], in_=sc["mv"][:, 1:2],
                                                func=AF.Sqrt, bias=sc["eps"][:]).then_inc(h, 1))
    t2 = sC.add(1)
    p.wait("vector", "sC", t2)
    p.on("vector", lambda e, h=hv: e.reciprocal(out=sc["rstd"][:], in_=sc["sd"][:]).then_inc(h, 1))
    tr0 = vC.add(1)
    p.wait("vector", "vC", tr0)
    if scale_ap is not None:
        p.on("vector", lambda e, m=scale_ap, h=hv: e.tensor_single_scalar(
            out=sc["rstd"][:], in_=sc["rstd"][:], scalar=m, op=ALU.mult).then_inc(h, 1))
        tr1 = vC.add(1)
        p.wait("vector", "vC", tr1)
    p.on("vector", lambda e, h=hv: e.tensor_scalar(out=sc["negmr"][:], in0=sc["mv"][:, 0:1],
                                                   scalar1=sc["rstd"][:], scalar2=-1.0,
                                                   op0=ALU.mult, op1=ALU.mult).then_inc(h, 1))
    t3 = vC.add(1)
    p.wait("scalar", "vC", t3)
    p.on("scalar", lambda e, d=dst, h=hs: e.activation(out=d, in_=xbuf[:], func=AF.Identity,
                                                       scale=sc["rstd"][:], bias=sc["negmr"][:]).then_inc(h, 1))
    return sC.add(1), t3


def emit_transpose_tiles(p, src, dst_view, pst, ident, ncols, wait_s=None):
    tC, vC = p.ctr["tC"], p.ctr["vC"]
    ht, hv = p.sems["tC"], p.sems["vC"]
    if wait_s is not None:
        p.wait("tensor", wait_s[0], wait_s[1])
    for j in range(ncols):
        if j < ncols - 1:
            p.on("tensor", lambda e, j=j: e.transpose(pst[:, 128 * j:128 * (j + 1)],
                                                      src[:, 128 * j:128 * (j + 1)], ident[:]))
        else:
            p.on("tensor", lambda e, j=j, h=ht: e.transpose(pst[:, 128 * j:128 * (j + 1)],
                                                            src[:, 128 * j:128 * (j + 1)], ident[:]).then_inc(h, 1))
    t = tC.add(1)
    p.wait("vector", "tC", t)
    p.on("vector", lambda e, h=hv: e.tensor_copy(
        out=dst_view, in_=pst[:, 0:128 * ncols].rearrange("p (t c) -> p t c", c=128)).then_inc(h, 1))
    return vC.add(1), t


def build_A():
    nc = bass.Bass("TRN2", target_bir_lowering=False)
    stack = contextlib.ExitStack()
    p = P(nc, stack)

    x0 = nc.dram_tensor("x0", [T, E], F32, kind="ExternalInput").ap()
    xh = nc.dram_tensor("xh", [HALO, E], F32, kind="ExternalInput").ap()
    wqkv = nc.dram_tensor("wqkv", [3 * E, E], F32, kind="ExternalInput").ap()
    wout = nc.dram_tensor("wout", [E, E], F32, kind="ExternalInput").ap()
    in2 = nc.dram_tensor("in2", [E, E], F32, kind="ExternalInput").ap()
    wk1 = nc.dram_tensor("wk1", [E, D], F32, kind="ExternalInput").ap()
    wq1 = nc.dram_tensor("wq1", [E, D], F32, kind="ExternalInput").ap()
    wk2 = nc.dram_tensor("wk2", [D, E2], F32, kind="ExternalInput").ap()
    wv = nc.dram_tensor("wv", [E, E2], F32, kind="ExternalInput").ap()
    identf = nc.dram_tensor("identf", [128, 128], F32, kind="ExternalInput").ap()
    identb = nc.dram_tensor("identb", [128, 128], BF16, kind="ExternalInput").ap()
    amask = nc.dram_tensor("amask", [40, 128, 128], BF16, kind="ExternalInput").ap()
    sclb = nc.dram_tensor("sclb", [128, 1], F32, kind="ExternalInput").ap()
    scl1b = nc.dram_tensor("scl1b", [128, 1], F32, kind="ExternalInput").ap()

    x2 = nc.dram_tensor("x2", [T, E], F32, kind="ExternalOutput").ap()
    acum = nc.dram_tensor("acum", [E2, T], BF16, kind="ExternalOutput").ap()
    hloc = nc.dram_tensor("hloc", [E2, T], BF16, kind="ExternalOutput").ap()
    q1o = nc.dram_tensor("q1o", [T, D], F32, kind="ExternalOutput").ap()

    scrq = nc.dram_tensor("scrq", [3 * E, E], BF16, kind="Internal").ap()
    scro = nc.dram_tensor("scro", [E, E], BF16, kind="Internal").ap()
    q2h = nc.dram_tensor("q2h", [E, E], BF16, kind="Internal").ap()
    xa = nc.dram_tensor("xa", [T, E], F32, kind="Internal").ap()

    ident_f = p.sb("ident_f", [128, 128], F32)
    ident_b = p.sb("ident_b", [128, 128], BF16)
    eps5 = p.sb("eps5", [128, 1], F32)
    scl_c = p.sb("scl_c", [128, 1], F32)
    scl1_c = p.sb("scl1_c", [128, 1], F32)
    dq_c = p.sb("dq_c", [128, 1], F32)
    xbuf = [p.sb(f"xbuf{i}", [128, E], F32) for i in range(2)]
    xln_f = p.sb("xln_f", [128, E], F32)
    sc = {
        "bn24": p.sb("bn24", [128, 24], F32),
        "mv": p.sb("mv", [128, 2], F32),
        "sd": p.sb("sd", [128, 1], F32),
        "rstd": p.sb("rstd", [128, 1], F32),
        "negmr": p.sb("negmr", [128, 1], F32),
        "eps": eps5,
    }
    PSA = p.ps("PSA", [128, 2048], F32)
    PSB = p.ps("PSB", [128, 1024], F32)
    PSTB = p.ps("PSTB", [128, 2048], BF16)

    vC, sC, tC = p.ctr["vC"], p.ctr["sC"], p.ctr["tC"]
    hv, hs, ht = p.sems["vC"], p.sems["sC"], p.sems["tC"]

    _dma(p, "sync", "yL", ident_f[:], identf)
    _dma(p, "sync", "yL", ident_b[:], identb)
    _dma(p, "sync", "yL", scl_c[:], sclb)
    t_consts = _dma(p, "sync", "yL", scl1_c[:], scl1b)
    p.wait("tensor", "yL", t_consts)
    p.wait("scalar", "yL", t_consts)
    p.on("vector", lambda e: e.memset(eps5[:], 1e-5))
    p.wait("vector", "yL", t_consts)
    p.on("vector", lambda e, h=hv: e.tensor_mul(out=dq_c[:], in0=scl_c[:], in1=scl1_c[:]).then_inc(h, 1))
    p.wait("vector", "vC", p.ctr["vC"].add(1))
    p.on("vector", lambda e, h=hv: e.reciprocal(out=dq_c[:], in_=dq_c[:]).then_inc(h, 1))
    p.wait("vector", "vC", p.ctr["vC"].add(1))

    for j in range(4):
        _dma(p, "gpsimd", "gL", scrq[1536 * j:1536 * (j + 1), :], wqkv[1536 * j:1536 * (j + 1), :])
    for j in range(2):
        _dma(p, "gpsimd", "gL", scro[1024 * j:1024 * (j + 1), :], wout[1024 * j:1024 * (j + 1), :])
    t_bounce = p.ctr["gL"].v

    s_q2 = contextlib.ExitStack()
    q2f = p.sb("q2f", [128, NE * 512], F32, stack=s_q2)
    q2q = p.sb("q2q", [128, NE * 512], FP8, stack=s_q2)
    q2b = p.sb("q2b", [128, NE * 512], BF16, stack=s_q2)
    in2r = in2.rearrange("(t p) n -> p t n", p=128)
    q2hr = q2h.rearrange("(t p) n -> p t n", p=128)
    for ns in range(4):
        tld = _dma(p, "gpsimd", "gL", q2f[:].rearrange("p (t n) -> p t n", n=512),
                   in2r[:, :, 512 * ns:512 * (ns + 1)])
        p.wait("vector", "gL", tld)
        p.on("vector", lambda e, h=hv: e.tensor_single_scalar(out=q2f[:], in_=q2f[:], scalar=scl1_c[:],
                                                               op=ALU.mult).then_inc(h, 1))
        p.wait("vector", "vC", vC.add(1))
        p.on("vector", lambda e, h=hv: e.tensor_copy(out=q2q[:], in_=q2f[:]).then_inc(h, 1))
        p.wait("vector", "vC", vC.add(1))
        p.on("vector", lambda e, h=hv: e.tensor_copy(out=q2b[:], in_=q2q[:]).then_inc(h, 1))
        tq = vC.add(1)
        p.wait("gpsimd", "vC", tq)
        tst = _dma(p, "gpsimd", "gS", q2hr[:, :, 512 * ns:512 * (ns + 1)],
                   q2b[:].rearrange("p (t n) -> p t n", n=512))
        p.wait("gpsimd", "gS", tst)
    t_q2done = p.ctr["gS"].v
    s_q2.close()

    s1 = contextlib.ExitStack()
    xlnT = p.sb("xlnT", [128, NE * TH], BF16, stack=s1)

    pstr = PSA[:, 0:2048]
    lasts = {}
    last_tr = 0
    for i in range(NTH):
        src = xh[128 * i:128 * (i + 1), :] if i < 4 else x0[128 * (i - 4):128 * (i - 3), :]
        buf = xbuf[i % 2]
        if i >= 2:
            p.wait("sync", "sC", lasts[i - 2])
        tld = _dma(p, "sync", "yL", buf[:], src)
        p.wait("vector", "yL", tld)
        if i > 0:
            p.wait("scalar", "tC", last_tr)
        t_ap, _ = emit_ln(p, buf, xln_f[:], sc)
        lasts[i] = t_ap
        dstv = xlnT[:].rearrange("p (t c) -> p t c", c=TH)[:, :, 128 * i:128 * (i + 1)]
        ev, ttr = emit_transpose_tiles(p, xln_f, dstv, pstr, ident_f, NE, wait_s=("sC", t_ap))
        last_tr = ttr
        p.wait("tensor", "vC", ev)
    p.barrier()

    osb = p.sb("osb", [128, NT * E], BF16, stack=s1)
    s1a = contextlib.ExitStack()
    wT = p.sb("wT", [128, NE * 512], BF16, stack=s1a)
    qT = p.sb("qT", [128, 4 * T], BF16, stack=s1a)
    kT = p.sb("kT", [128, 4 * TH], BF16, stack=s1a)
    VW = 4 * 129
    vsb = p.sb("vsb", [128, NTH * VW], BF16, stack=s1a)
    et2 = [p.sb(f"et2_{j}", [128, 640], BF16, stack=s1a) for j in range(2)]
    amask_sb = p.sb("amask_sb", [128, 40 * 128], BF16, stack=s1a)
    rcol = p.sb("rcol", [128, 1], F32, stack=s1a)
    psq = PSA[:, 0:1024]
    psk = PSA[:, 0:1536]
    psv = PSA[:, 0:512]

    tam = _dma(p, "sync", "yL", amask_sb[:], amask.rearrange("s j i -> j s i"))
    p.wait("tensor", "yL", tam)
    p.wait("scalar", "gL", t_bounce)
    for hg in range(4):
        for et in range(NE):
            tw = _dma_tr(p, "cL", wT[:, 512 * et:512 * (et + 1)],
                         scrq[512 * hg:512 * (hg + 1), 128 * et:128 * (et + 1)])
        p.wait("tensor", "cL", tw)
        for tt in range(NTH):
            for hh in range(4):
                p.on("vector", lambda e, tt=tt, hh=hh: e.memset(
                    vsb[:, VW * tt + 129 * hh + 128:VW * tt + 129 * hh + 129], 1.0))
        p.on("vector", lambda e, h=hv: e.engine_nop().then_inc(h, 1))
        t_ones = vC.add(1)

        for hh in range(4):
            for half in range(2):
                for et in range(NE):
                    last = (et == NE - 1)
                    def mmq(e, hh=hh, half=half, et=et, last=last):
                        r = e.matmul(psq[:, 512 * half:512 * (half + 1)],
                                     wT[:, 512 * et + 128 * hh:512 * et + 128 * (hh + 1)],
                                     xlnT[:, TH * et + HALO + 512 * half:TH * et + HALO + 512 * (half + 1)],
                                     start=(et == 0), stop=last)
                        if last and half == 1:
                            r.then_inc(ht, 1)
                    p.on("tensor", mmq)
            tq_ = tC.add(1)
            p.wait("vector", "tC", tq_)
            p.on("vector", lambda e, hh=hh, h=hv: e.tensor_single_scalar(
                out=qT[:, T * hh:T * (hh + 1)], in_=psq, scalar=SCALE,
                op=ALU.mult).then_inc(h, 1))
            t_qev = vC.add(1)
            p.on("tensor", lambda e, t=t_qev, h=hv: e.wait_ge(h, t))
        p.wait("scalar", "tC", tC.v)
        for et in range(NE):
            tw = _dma_tr(p, "cL", wT[:, 512 * et:512 * (et + 1)],
                         scrq[E + 512 * hg:E + 512 * (hg + 1), 128 * et:128 * (et + 1)])
        p.wait("tensor", "cL", tw)
        for hh in range(4):
            for third in range(3):
                for et in range(NE):
                    last = (et == NE - 1)
                    def mmk(e, hh=hh, third=third, et=et, last=last):
                        r = e.matmul(psk[:, 512 * third:512 * (third + 1)],
                                     wT[:, 512 * et + 128 * hh:512 * et + 128 * (hh + 1)],
                                     xlnT[:, TH * et + 512 * third:TH * et + 512 * (third + 1)],
                                     start=(et == 0), stop=last)
                        if last and third == 2:
                            r.then_inc(ht, 1)
                    p.on("tensor", mmk)
            tk_ = tC.add(1)
            p.wait("vector", "tC", tk_)
            p.on("vector", lambda e, hh=hh, h=hv: e.tensor_copy(
                out=kT[:, TH * hh:TH * (hh + 1)], in_=psk).then_inc(h, 1))
            t_kev = vC.add(1)
            p.on("tensor", lambda e, t=t_kev, h=hv: e.wait_ge(h, t))
        p.wait("scalar", "tC", tC.v)
        for et in range(NE):
            tw = _dma_tr(p, "cL", wT[:, 512 * et:512 * (et + 1)],
                         scrq[2 * E + 512 * hg:2 * E + 512 * (hg + 1), 128 * et:128 * (et + 1)])
        p.wait("tensor", "cL", tw)
        p.wait("tensor", "vC", t_ones)
        for tt in range(NTH):
            for et in range(NE):
                last = (et == NE - 1)
                def mmv(e, tt=tt, et=et, last=last):
                    r = e.matmul(psv, xlnT[:, TH * et + 128 * tt:TH * et + 128 * (tt + 1)],
                                 wT[:, 512 * et:512 * (et + 1)], start=(et == 0), stop=last)
                    if last:
                        r.then_inc(ht, 1)
                p.on("tensor", mmv)
            tv_ = tC.add(1)
            p.wait("vector", "tC", tv_)
            p.on("vector", lambda e, tt=tt, h=hv: e.tensor_copy(
                out=vsb[:].rearrange("p (t h c) -> p t h c", t=NTH, h=4)[:, tt, :, 0:128],
                in_=psv.rearrange("p (h c) -> p h c", h=4)).then_inc(h, 1))
            t_vev = vC.add(1)
            p.on("tensor", lambda e, t=t_vev, h=hv: e.wait_ge(h, t))

        psst_ar = [PSB[:, 0:640], PSA[:, 0:640]]
        pso_ar = [PSB[:, 768:900], PSA[:, 768:900]]
        st_h, ex_h, vm_h, evac_h = {}, {}, {}, {}
        iters = [(hh, r) for hh in range(4) for r in range(NT)]
        def emit_vmm(i):
            hh, r = iters[i]
            p.wait("tensor", "sC", ex_h[i])
            for m in range(5):
                lastm = (m == 4)
                def mmo(e, hh=hh, r=r, m=m, i=i, lastm=lastm):
                    rr = e.matmul(pso_ar[i % 2][:, 0:129], et2[i % 2][:, 128 * m:128 * (m + 1)],
                                  vsb[:, VW * (r + m) + 129 * hh:VW * (r + m) + 129 * (hh + 1)],
                                  start=(m == 0), stop=lastm)
                    if lastm:
                        rr.then_inc(ht, 1)
                p.on("tensor", mmo)
            vm_h[i] = tC.add(1)
        def emit_evac(i):
            hh, r = iters[i]
            p.wait("vector", "tC", vm_h[i])
            p.on("vector", lambda e, i=i, h=hv: e.reciprocal(out=rcol[:], in_=pso_ar[i % 2][:, 128:129]).then_inc(h, 1))
            p.wait("vector", "vC", vC.add(1))
            p.on("vector", lambda e, hg=hg, hh=hh, r=r, i=i, h=hv: e.tensor_single_scalar(
                out=osb[:, E * r + 128 * (4 * hg + hh):E * r + 128 * (4 * hg + hh + 1)],
                in_=pso_ar[i % 2][:, 0:128], scalar=rcol[:], op=ALU.mult).then_inc(h, 1))
            evac_h[i] = vC.add(1)
        for i, (hh, r) in enumerate(iters):
            if i >= 2:
                p.wait("tensor", "sC", ex_h[i - 2])
                p.on("tensor", lambda e, t=evac_h[i - 2], h=hv: e.wait_ge(h, t))
            slots = [0, 4] + [m for m in range(1, 4) if m < 4 - r]
            for m in range(5):
                p.on("tensor", lambda e, hh=hh, r=r, m=m, i=i: e.matmul(
                    psst_ar[i % 2][:, 128 * m:128 * (m + 1)],
                    kT[:, TH * hh + 128 * (r + m):TH * hh + 128 * (r + m + 1)],
                    qT[:, T * hh + 128 * r:T * hh + 128 * (r + 1)],
                    start=True, stop=(m not in slots)))
                if m in slots:
                    lastm = (m == max(slots))
                    def mmask(e, r=r, m=m, i=i, lastm=lastm):
                        rr = e.matmul(psst_ar[i % 2][:, 128 * m:128 * (m + 1)], ident_b[:],
                                      amask_sb[:, 128 * (5 * r + m):128 * (5 * r + m + 1)],
                                      start=False, stop=True)
                        if lastm:
                            rr.then_inc(ht, 1)
                    p.on("tensor", mmask)
            st_h[i] = tC.add(1)
            if i >= 1:
                emit_vmm(i - 1)
            p.wait("scalar", "tC", st_h[i])
            p.on("scalar", lambda e, i=i, h=hs: e.activation(out=et2[i % 2][:], in_=psst_ar[i % 2],
                                                             func=AF.Exp).then_inc(h, 1))
            ex_h[i] = sC.add(1)
            if i >= 1:
                emit_evac(i - 1)
        emit_vmm(len(iters) - 1)
        emit_evac(len(iters) - 1)
        p.barrier()

    s1a.close()
    oT = p.sb("oT", [128, NE * T], BF16, stack=s1)
    for r in range(NT):
        ev, ttr = emit_transpose_tiles(
            p, osb[:, E * r:E * (r + 1)],
            oT[:].rearrange("p (t c) -> p t c", c=T)[:, :, 128 * r:128 * (r + 1)],
            PSTB, ident_b, NE)
        p.wait("tensor", "vC", ev)
    p.barrier()

    woT = p.sb("woT", [128, NE * 512], BF16, stack=s1)
    psm = [PSA[:, 0:512], PSA[:, 512:1024]]
    for n in range(4):
        if n > 0:
            p.wait("scalar", "tC", p.ctr["tC"].v)
        for et in range(NE):
            tw = _dma_tr(p, "cL", woT[:, 512 * et:512 * (et + 1)],
                         scro[512 * n:512 * (n + 1), 128 * et:128 * (et + 1)])
        p.wait("tensor", "cL", tw)
        tevs = {}
        for r in range(NT):
            buf = xbuf[r % 2]
            tld = _dma(p, "sync", "yL", buf[:, 512 * n:512 * (n + 1)],
                       x0[128 * r:128 * (r + 1), 512 * n:512 * (n + 1)])
            pb = psm[r % 2]
            if r >= 2:
                p.on("tensor", lambda e, t=tevs[r - 2], h=hv: e.wait_ge(h, t))
            for et in range(NE):
                last = (et == NE - 1)
                def mmw(e, r=r, et=et, pb=pb, last=last):
                    rr = e.matmul(pb, oT[:, T * et + 128 * r:T * et + 128 * (r + 1)],
                                  woT[:, 512 * et:512 * (et + 1)], start=(et == 0), stop=last)
                    if last:
                        rr.then_inc(ht, 1)
                p.on("tensor", mmw)
            tm_ = tC.add(1)
            p.wait("vector", "tC", tm_)
            p.wait("vector", "yL", tld)
            p.on("vector", lambda e, pb=pb, buf=buf, n=n, h=hv: e.scalar_tensor_tensor(
                out=buf[:, 512 * n:512 * (n + 1)], in0=pb, scalar=1.0,
                in1=buf[:, 512 * n:512 * (n + 1)], op0=ALU.mult, op1=ALU.add).then_inc(h, 1))
            tev = vC.add(1)
            tevs[r] = tev
            p.wait("sync", "vC", tev)
            tst = _dma(p, "sync", "yS", xa[128 * r:128 * (r + 1), 512 * n:512 * (n + 1)],
                       buf[:, 512 * n:512 * (n + 1)])
        p.wait("tensor", "vC", tev)
    p.wait("sync", "yS", p.ctr["yS"].v)
    p.barrier()
    s1.close()

    s2 = contextlib.ExitStack()
    qf8 = p.sb("qf8", [128, E], FP8, stack=s2)
    qfb = p.sb("qfb", [128, E], BF16, stack=s2)
    qfTa = p.sb("qfTa", [128, NT * E], BF16, stack=s2)
    q2s = [p.sb(f"q2s{j}", [128, NE * 512], BF16, stack=s2) for j in range(2)]
    x2a = p.sb("x2a", [128, NT * E], F32, stack=s2)
    psm = [PSA[:, 0:512], PSA[:, 512:1024]]
    p.wait("gpsimd", "gS", t_q2done)
    last_tr = 0
    for r in range(NT):
        xsl = x2a[:, E * r:E * (r + 1)]
        tld = _dma(p, "sync", "yL", xsl, xa[128 * r:128 * (r + 1), :])
        p.wait("vector", "yL", tld)
        if r > 0:
            p.wait("scalar", "tC", last_tr)
        t_ap, _ = emit_ln(p, xsl, xln_f[:], sc, scale_ap=scl_c[:])
        p.wait("vector", "sC", t_ap)
        p.on("vector", lambda e, h=hv: e.tensor_copy(out=qf8[:], in_=xln_f[:]).then_inc(h, 1))
        p.wait("vector", "vC", vC.add(1))
        p.on("vector", lambda e, h=hv: e.tensor_copy(out=qfb[:], in_=qf8[:]).then_inc(h, 1))
        tq_ = vC.add(1)
        ev, last_tr = emit_transpose_tiles(
            p, qfb, qfTa[:].rearrange("p (r t c) -> p (r t) c", r=NT, c=128)[:, NE * r:NE * (r + 1), :],
            PSTB, ident_b, NE, wait_s=("vC", tq_))
        p.wait("tensor", "vC", ev)
    p.barrier()
    mm_hist = {}
    ev_hist = {}
    gi = 0
    for ns in range(4):
        bufw = q2s[ns % 2]
        if ns >= 2:
            p.wait("gpsimd", "tC", mm_hist[(ns - 2) * NT + NT - 1])
        tw = _dma(p, "gpsimd", "gL", bufw[:].rearrange("p (t n) -> p t n", n=512),
                  q2hr[:, :, 512 * ns:512 * (ns + 1)])
        p.wait("tensor", "gL", tw)
        for r in range(NT):
            pb = psm[gi % 2]
            if gi >= 2:
                p.on("tensor", lambda e, t=ev_hist[gi - 2], h=hv: e.wait_ge(h, t))
            for et in range(NE):
                last = (et == NE - 1)
                def mmf(e, r=r, et=et, pb=pb, bufw=bufw, last=last):
                    rr = e.matmul(pb, qfTa[:, E * r + 128 * et:E * r + 128 * (et + 1)],
                                  bufw[:, 512 * et:512 * (et + 1)], start=(et == 0), stop=last)
                    if last:
                        rr.then_inc(ht, 1)
                p.on("tensor", mmf)
            tm_ = tC.add(1)
            mm_hist[gi] = tm_
            p.wait("vector", "tC", tm_)
            p.on("vector", lambda e, pb=pb, r=r, ns=ns, h=hv: e.scalar_tensor_tensor(
                out=x2a[:, E * r + 512 * ns:E * r + 512 * (ns + 1)], in0=pb, scalar=dq_c[:],
                in1=x2a[:, E * r + 512 * ns:E * r + 512 * (ns + 1)],
                op0=ALU.mult, op1=ALU.add).then_inc(h, 1))
            ev_hist[gi] = vC.add(1)
            gi += 1
    p.wait("sync", "vC", p.ctr["vC"].v)
    for r in range(NT):
        _dma(p, "sync", "yS", x2[128 * r:128 * (r + 1), :], x2a[:, E * r:E * (r + 1)])
    p.wait("sync", "yS", p.ctr["yS"].v)
    p.barrier()
    s2.close()

    s3 = contextlib.ExitStack()
    xlnT2 = p.sb("xlnT2", [128, NE * T], BF16, stack=s3)
    wk1b = p.sb("wk1b", [128, NE * 128], BF16, stack=s3)
    wq1b = p.sb("wq1b", [128, NE * 128], BF16, stack=s3)
    wk2b = p.sb("wk2b", [128, E2], BF16, stack=s3)
    wvb = [p.sb(f"wvb{j}", [128, NE * 128], BF16, stack=s3) for j in range(2)]
    k1T = p.sb("k1T", [128, T], BF16, stack=s3)
    k1tmp = p.sb("k1tmp", [128, 128], BF16, stack=s3)
    q1sb = p.sb("q1sb", [128, D], F32, stack=s3)
    f_sb = p.sb("f_sb", [128, T], F32, stack=s3)
    a_sb = [p.sb(f"a_sb{j}", [128, T], F32, stack=s3) for j in range(2)]
    kg_sb = [p.sb(f"kg_sb{j}", [128, T], F32, stack=s3) for j in range(2)]
    v_f = p.sb("v_f", [128, T], F32, stack=s3)
    b_sb = p.sb("b_sb", [128, T], F32, stack=s3)
    ac_sb = [p.sb(f"ac_sb{j}", [128, T], BF16, stack=s3) for j in range(2)]
    hl_sb = [p.sb(f"hl_sb{j}", [128, T], BF16, stack=s3) for j in range(2)]
    pssm = PSB[:, 0:128]
    pstb1 = PSTB[:, 0:128]

    wk1r = wk1.rearrange("(t p) n -> p t n", p=128)
    wq1r = wq1.rearrange("(t p) n -> p t n", p=128)
    wvr = wv.rearrange("(t p) n -> p t n", p=128)
    _dma(p, "gpsimd", "gL", wk1b[:].rearrange("p (t n) -> p t n", n=128), wk1r)
    _dma(p, "gpsimd", "gL", wq1b[:].rearrange("p (t n) -> p t n", n=128), wq1r)
    twk = _dma(p, "gpsimd", "gL", wk2b[:], wk2)

    last_tr = 0
    for r in range(NT):
        buf = xbuf[0]
        tld = _dma(p, "sync", "yL", buf[:], x2[128 * r:128 * (r + 1), :])
        p.wait("vector", "yL", tld)
        if r > 0:
            p.wait("scalar", "tC", last_tr)
        t_ap, _ = emit_ln(p, buf, xln_f[:], sc)
        ev, last_tr = emit_transpose_tiles(
            p, xln_f, xlnT2[:].rearrange("p (t c) -> p t c", c=T)[:, :, 128 * r:128 * (r + 1)],
            PSA[:, 0:2048], ident_f, NE, wait_s=("sC", t_ap))
        p.wait("tensor", "vC", ev)
        p.wait("sync", "vC", ev)
    p.barrier()

    p.wait("tensor", "gL", twk)
    for r in range(NT):
        for et in range(NE):
            last = (et == NE - 1)
            def mk1(e, r=r, et=et, last=last):
                rr = e.matmul(pssm, xlnT2[:, T * et + 128 * r:T * et + 128 * (r + 1)],
                              wk1b[:, 128 * et:128 * (et + 1)], start=(et == 0), stop=last)
                if last:
                    rr.then_inc(ht, 1)
            p.on("tensor", mk1)
        t1_ = tC.add(1)
        p.wait("vector", "tC", t1_)
        p.on("vector", lambda e, h=hv: e.tensor_copy(out=k1tmp[:], in_=pssm).then_inc(h, 1))
        tev = vC.add(1)
        p.wait("tensor", "vC", tev)
        p.on("tensor", lambda e, h=ht: e.transpose(pstb1, k1tmp[:], ident_b[:]).then_inc(h, 1))
        t2_ = tC.add(1)
        p.wait("vector", "tC", t2_)
        p.on("vector", lambda e, r=r, h=hv: e.tensor_copy(out=k1T[:, 128 * r:128 * (r + 1)],
                                                          in_=pstb1).then_inc(h, 1))
        tev2 = vC.add(1)
        p.on("tensor", lambda e, t=tev2, h=hv: e.wait_ge(h, t))
        for et in range(NE):
            last = (et == NE - 1)
            def mq1(e, r=r, et=et, last=last):
                rr = e.matmul(pssm, xlnT2[:, T * et + 128 * r:T * et + 128 * (r + 1)],
                              wq1b[:, 128 * et:128 * (et + 1)], start=(et == 0), stop=last)
                if last:
                    rr.then_inc(ht, 1)
            p.on("tensor", mq1)
        t3_ = tC.add(1)
        p.wait("vector", "tC", t3_)
        p.on("vector", lambda e, h=hv: e.tensor_copy(out=q1sb[:], in_=pssm).then_inc(h, 1))
        tev3 = vC.add(1)
        p.wait("sync", "vC", tev3)
        tst = _dma(p, "sync", "yS", q1o[128 * r:128 * (r + 1), :], q1sb[:])
        p.wait("sync", "yS", tst)
        p.on("tensor", lambda e, t=tev3, h=hv: e.wait_ge(h, t))
    p.barrier()

    psraw = PSB[:, 0:1024]
    psv1 = [PSA[:, 0:1024], PSA[:, 1024:2048]]
    raw_h, sigf_h, sig_h, vmm_h, vf_h, scan_h, st_h = {}, {}, {}, {}, {}, {}, {}
    for ch in range(NCH):
        bufw = wvb[ch % 2]
        if ch >= 2:
            p.wait("gpsimd", "tC", vmm_h[ch - 2])
        tw = _dma(p, "gpsimd", "gL", bufw[:].rearrange("p (t n) -> p t n", n=128),
                  wvr[:, :, 128 * ch:128 * (ch + 1)])
        if ch >= 1:
            p.wait("tensor", "sC", sigf_h[ch - 1])
        for half in range(2):
            lasth = (half == 1)
            def mraw(e, ch=ch, half=half, lasth=lasth):
                rr = e.matmul(psraw[:, 512 * half:512 * (half + 1)], wk2b[:, 128 * ch:128 * (ch + 1)],
                              k1T[:, 512 * half:512 * (half + 1)], start=True, stop=True)
                if lasth:
                    rr.then_inc(ht, 1)
            p.on("tensor", mraw)
        raw_h[ch] = tC.add(1)
        p.wait("tensor", "gL", tw)
        if ch >= 2:
            p.wait("tensor", "vC", vf_h[ch - 2])
        for half in range(2):
            for et in range(NE):
                last = (et == NE - 1)
                def mv2(e, ch=ch, et=et, half=half, bufw=bufw, last=last, lasth=(half == 1)):
                    rr = e.matmul(psv1[ch % 2][:, 512 * half:512 * (half + 1)],
                                  bufw[:, 128 * et:128 * (et + 1)],
                                  xlnT2[:, T * et + 512 * half:T * et + 512 * (half + 1)],
                                  start=(et == 0), stop=last)
                    if last and lasth:
                        rr.then_inc(ht, 1)
                p.on("tensor", mv2)
        vmm_h[ch] = tC.add(1)
        p.wait("scalar", "tC", raw_h[ch])
        if ch >= 2:
            p.wait("scalar", "vC", scan_h[ch - 2])
        p.on("scalar", lambda e, ch=ch, h=hs: e.activation(out=f_sb[:], in_=psraw,
                                                           func=AF.Sigmoid).then_inc(h, 1))
        sigf_h[ch] = sC.add(1)
        p.wait("scalar", "sC", sigf_h[ch])
        p.on("scalar", lambda e, ch=ch: e.activation(out=a_sb[ch % 2][:], in_=f_sb[:], func=AF.Sigmoid))
        p.on("scalar", lambda e, ch=ch, h=hs: e.activation(out=kg_sb[ch % 2][:], in_=f_sb[:],
                                                           func=AF.Sigmoid, scale=-1.0).then_inc(h, 1))
        sig_h[ch] = sC.add(1)
        p.wait("vector", "tC", vmm_h[ch])
        p.on("vector", lambda e, ch=ch, h=hv: e.tensor_copy(out=v_f[:], in_=psv1[ch % 2]).then_inc(h, 1))
        vf_h[ch] = vC.add(1)
        p.wait("vector", "vC", vf_h[ch])
        p.wait("vector", "sC", sig_h[ch])
        p.on("vector", lambda e, ch=ch, h=hv: e.tensor_mul(out=b_sb[:], in0=kg_sb[ch % 2][:],
                                                           in1=v_f[:]).then_inc(h, 1))
        tb_ = vC.add(1)
        p.wait("vector", "vC", tb_)
        if ch >= 2:
            p.wait("vector", "yS", st_h[ch - 2])
        p.on("vector", lambda e, ch=ch: e.tensor_tensor_scan(
            out=ac_sb[ch % 2][:], data0=a_sb[ch % 2][:], data1=a_sb[ch % 2][:],
            initial=1.0, op0=ALU.mult, op1=ALU.bypass))
        p.on("vector", lambda e, ch=ch, h=hv: e.tensor_tensor_scan(
            out=hl_sb[ch % 2][:], data0=a_sb[ch % 2][:], data1=b_sb[:],
            initial=0.0, op0=ALU.mult, op1=ALU.add).then_inc(h, 1))
        scan_h[ch] = vC.add(1)
        p.wait("sync", "vC", scan_h[ch])
        _dma(p, "sync", "yS", acum[128 * ch:128 * (ch + 1), :], ac_sb[ch % 2][:])
        st_h[ch] = _dma(p, "sync", "yS", hloc[128 * ch:128 * (ch + 1), :], hl_sb[ch % 2][:])

    p.wait("sync", "yS", p.ctr["yS"].v)
    p.wait("gpsimd", "gS", p.ctr["gS"].v)
    s3.close()

    p.emit()
    stack.close()
    return nc



def build_B():
    nc = bass.Bass("TRN2", target_bir_lowering=False)
    stack = contextlib.ExitStack()
    p = P(nc, stack)

    x2 = nc.dram_tensor("x2", [T, E], F32, kind="ExternalInput").ap()
    acum = nc.dram_tensor("acum", [E2, T], BF16, kind="ExternalInput").ap()
    hloc = nc.dram_tensor("hloc", [E2, T], BF16, kind="ExternalInput").ap()
    q1t = nc.dram_tensor("q1t", [D, T], F32, kind="ExternalInput").ap()
    predA = nc.dram_tensor("predA", [128, 96], F32, kind="ExternalInput").ap()
    predH = nc.dram_tensor("predH", [128, 96], F32, kind="ExternalInput").ap()
    wq2 = nc.dram_tensor("wq2", [D, E2], F32, kind="ExternalInput").ap()
    wo = nc.dram_tensor("wo", [E2, E], F32, kind="ExternalInput").ap()
    nwr = nc.dram_tensor("nwr", [128, NCH], F32, kind="ExternalInput").ap()
    wrr = nc.dram_tensor("wrr", [128, NE], F32, kind="ExternalInput").ap()
    brr = nc.dram_tensor("brr", [1, 1], F32, kind="ExternalInput").ap()
    identf = nc.dram_tensor("identf", [128, 128], F32, kind="ExternalInput").ap()

    x3 = nc.dram_tensor("x3", [T, E], F32, kind="ExternalOutput").ap()
    xlnT3o = nc.dram_tensor("xlnT3o", [128, NE * T], BF16, kind="ExternalOutput").ap()
    scso = nc.dram_tensor("scso", [1, T], F32, kind="ExternalOutput").ap()

    ident_f = p.sb("ident_f", [128, 128], F32)
    eps5 = p.sb("eps5", [128, 1], F32)
    xbuf = [p.sb(f"xbuf{i}", [128, E], F32) for i in range(2)]
    xln_f = p.sb("xln_f", [128, E], F32)
    sc = {
        "bn24": p.sb("bn24", [128, 24], F32),
        "mv": p.sb("mv", [128, 2], F32),
        "sd": p.sb("sd", [128, 1], F32),
        "rstd": p.sb("rstd", [128, 1], F32),
        "negmr": p.sb("negmr", [128, 1], F32),
        "eps": eps5,
    }
    PSA = p.ps("PSA", [128, 2048], F32)
    PSB = p.ps("PSB", [128, 1024], F32)

    hv, hs, ht = p.sems["vC"], p.sems["sC"], p.sems["tC"]
    vC, sC, tC = p.ctr["vC"], p.ctr["sC"], p.ctr["tC"]

    pA = p.sb("pA", [128, 96], F32)
    pH = p.sb("pH", [128, 96], F32)
    hin = p.sb("hin", [128, NCH], F32)
    q1f = p.sb("q1f", [128, T], F32)
    q1b = p.sb("q1b", [128, T], BF16)
    wq2b = p.sb("wq2b", [128, E2], BF16)
    nwc = p.sb("nwc", [128, NCH], F32)
    wrb = p.sb("wrb", [128, NE], BF16)
    brc = p.sb("brc", [1, 1], F32)
    onesb = p.sb("onesb", [128, 1], BF16)
    onesrow = p.sb("onesrow", [1, 128], BF16)
    acb = [p.sb(f"acb{i}", [128, T], BF16) for i in range(2)]
    hlb = [p.sb(f"hlb{i}", [128, T], BF16) for i in range(2)]
    h_sb = p.sb("h_sb", [128, T], BF16)
    qg_sb = p.sb("qg_sb", [128, T], BF16)
    o2_sb = p.sb("o2_sb", [128, T], BF16)
    out_sb = p.sb("out_sb", [128, NCH * T], BF16)
    ssqf = p.sb("ssqf", [1, T], F32)
    ssqb = p.sb("ssqb", [1, T], BF16)
    var_c = p.sb("var_c", [128, NT], F32)
    sd_c = p.sb("sd_c", [128, NT], F32)
    rstd_c = p.sb("rstd_c", [128, NT], F32)

    _dma(p, "sync", "yL", ident_f[:], identf)
    _dma(p, "sync", "yL", pA[:], predA)
    _dma(p, "sync", "yL", pH[:], predH)
    _dma(p, "sync", "yL", q1f[:], q1t)
    _dma(p, "sync", "yL", nwc[:], nwr)
    t_br = _dma(p, "sync", "yL", brc[:], brr)
    _dma(p, "gpsimd", "gL", wq2b[:], wq2)
    t_wr = _dma(p, "gpsimd", "gL", wrb[:], wrr)
    p.on("vector", lambda e: e.memset(eps5[:], 1e-5))
    p.on("vector", lambda e: e.memset(onesb[:], 1.0))
    p.on("vector", lambda e: e.memset(onesrow[:], 1.0))
    p.wait("vector", "yL", t_br)
    p.wait("vector", "gL", t_wr)
    p.wait("tensor", "yL", t_br)
    p.wait("tensor", "gL", t_wr)
    p.on("vector", lambda e, h=hv: e.tensor_copy(out=q1b[:], in_=q1f[:]).then_inc(h, 1))
    p.wait("vector", "vC", vC.add(1))
    p.on("vector", lambda e, h=hv: e.memset(hin[:], 0.0).then_inc(h, 1))
    p.wait("vector", "vC", vC.add(1))
    for j in range(3):
        p.on("vector", lambda e, j=j, h=hv: e.tensor_mul(out=hin[:], in0=pA[:, 32 * j:32 * (j + 1)],
                                                         in1=hin[:]).then_inc(h, 1))
        p.wait("vector", "vC", vC.add(1))
        p.on("vector", lambda e, j=j, h=hv: e.tensor_add(out=hin[:], in0=hin[:],
                                                         in1=pH[:, 32 * j:32 * (j + 1)]).then_inc(h, 1))
        p.wait("vector", "vC", vC.add(1))
    p.barrier()

    psg = PSA[:, 0:1024]
    psr1 = PSB[0:1, 0:512]
    psr2 = PSB[0:1, 512:1024]
    t_ldprev = 0
    for ch in range(NCH):
        ab = acb[ch % 2]
        hb = hlb[ch % 2]
        if ch >= 2:
            p.wait("sync", "vC", t_hdone)
        _dma(p, "sync", "yL", ab[:], acum[128 * ch:128 * (ch + 1), :])
        tld = _dma(p, "sync", "yL", hb[:], hloc[128 * ch:128 * (ch + 1), :])
        if ch > 0:
            p.wait("tensor", "sC", t_silu)
        for half in range(2):
            lasth = (half == 1)
            def mmg(e, ch=ch, half=half, lasth=lasth):
                rr = e.matmul(psg[:, 512 * half:512 * (half + 1)], wq2b[:, 128 * ch:128 * (ch + 1)],
                              q1b[:, 512 * half:512 * (half + 1)], start=True, stop=True)
                if lasth:
                    rr.then_inc(ht, 1)
            p.on("tensor", mmg)
        tqg = tC.add(1)
        if ch > 0:
            p.wait("scalar", "vC", t_out)
        p.wait("scalar", "tC", tqg)
        p.on("scalar", lambda e, h=hs: e.activation(out=qg_sb[:], in_=psg, func=AF.Silu).then_inc(h, 1))
        t_silu = sC.add(1)
        p.wait("vector", "yL", tld)
        p.on("vector", lambda e, ch=ch, ab=ab, hb=hb, h=hv: e.scalar_tensor_tensor(
            out=h_sb[:], in0=ab[:], scalar=hin[:, ch:ch + 1], in1=hb[:],
            op0=ALU.mult, op1=ALU.add).then_inc(h, 1))
        t_hdone = vC.add(1)
        p.wait("vector", "vC", t_hdone)
        p.wait("vector", "sC", t_silu)
        p.on("vector", lambda e, ch=ch, h=hv: e.tensor_mul(
            out=out_sb[:, T * ch:T * (ch + 1)], in0=qg_sb[:], in1=h_sb[:]).then_inc(h, 1))
        t_out = vC.add(1)
        p.wait("scalar", "vC", t_out)
        p.on("scalar", lambda e, ch=ch, h=hs: e.activation(
            out=o2_sb[:], in_=out_sb[:, T * ch:T * (ch + 1)], func=AF.Square).then_inc(h, 1))
        t_o2 = sC.add(1)
        p.wait("tensor", "sC", t_o2)
        for half in range(2):
            lasth = (half == 1)
            def mms(e, ch=ch, half=half, lasth=lasth):
                rr = e.matmul((psr1 if half == 0 else psr2), onesb[:],
                              o2_sb[:, 512 * half:512 * (half + 1)],
                              start=(ch == 0), stop=(ch == NCH - 1))
                if lasth:
                    rr.then_inc(ht, 1)
            p.on("tensor", mms)
        t_ssq = tC.add(1)
        p.on("scalar", lambda e, t=t_ssq, h=ht: e.wait_ge(h, t))
    p.barrier()

    p.on("vector", lambda e, h=hv: e.tensor_copy(out=ssqf[0:1, 0:512], in_=psr1).then_inc(h, 1))
    p.on("vector", lambda e, h=hv: e.tensor_copy(out=ssqf[0:1, 512:1024], in_=psr2).then_inc(h, 1))
    p.wait("vector", "vC", vC.add(2))
    p.on("vector", lambda e, h=hv: e.tensor_copy(out=ssqb[:], in_=ssqf[:]).then_inc(h, 1))
    tssb = vC.add(1)
    p.wait("tensor", "vC", tssb)
    pscol = PSB[:, 0:128]
    for r in range(NT):
        p.on("tensor", lambda e, r=r, h=ht: e.matmul(pscol[:, 0:1], ssqb[0:1, 128 * r:128 * (r + 1)],
                                                     onesb[0:1, 0:1], start=True, stop=True).then_inc(h, 1))
        tcol = tC.add(1)
        p.wait("vector", "tC", tcol)
        p.on("vector", lambda e, r=r, h=hv: e.tensor_scalar(
            out=var_c[:, r:r + 1], in0=pscol[:, 0:1], scalar1=float(1.0 / E2), scalar2=1e-6,
            op0=ALU.mult, op1=ALU.add).then_inc(h, 1))
        tvar = vC.add(1)
        p.on("tensor", lambda e, t=tvar, h=hv: e.wait_ge(h, t))
    p.wait("scalar", "vC", vC.v)
    p.on("scalar", lambda e, h=hs: e.activation(out=sd_c[:], in_=var_c[:], func=AF.Sqrt).then_inc(h, 1))
    p.wait("vector", "sC", sC.add(1))
    p.on("vector", lambda e, h=hv: e.reciprocal(out=rstd_c[:], in_=sd_c[:]).then_inc(h, 1))
    trst = vC.add(1)
    p.barrier()

    wos = p.sb("wos", [128, NCH * 512], BF16)
    wor = wo.rearrange("(t p) n -> p t n", p=128)
    psm = [PSA[:, 0:512], PSA[:, 512:1024]]
    for n in range(4):
        if n > 0:
            p.wait("gpsimd", "tC", tC.v)
        tw = _dma(p, "gpsimd", "gL", wos[:].rearrange("p (t n) -> p t n", n=512),
                  wor[:, :, 512 * n:512 * (n + 1)])
        p.wait("vector", "gL", tw)
        for c in range(NCH):
            lastc = (c == NCH - 1)
            def scw(e, c=c, lastc=lastc):
                rr = e.tensor_single_scalar(out=wos[:, 512 * c:512 * (c + 1)],
                                            in_=wos[:, 512 * c:512 * (c + 1)],
                                            scalar=nwc[:, c:c + 1], op=ALU.mult)
                if lastc:
                    rr.then_inc(hv, 1)
            p.on("vector", scw)
        tnw = vC.add(1)
        p.wait("tensor", "vC", tnw)
        tevs = {}
        for r in range(NT):
            buf = xbuf[r % 2]
            tld = _dma(p, "sync", "yL", buf[:, 512 * n:512 * (n + 1)],
                       x2[128 * r:128 * (r + 1), 512 * n:512 * (n + 1)])
            pb = psm[r % 2]
            if r >= 2:
                p.on("tensor", lambda e, t=tevs[r - 2], h=hv: e.wait_ge(h, t))
            for c in range(NCH):
                lastc = (c == NCH - 1)
                def mmo(e, r=r, c=c, pb=pb, lastc=lastc):
                    rr = e.matmul(pb, out_sb[:, T * c + 128 * r:T * c + 128 * (r + 1)],
                                  wos[:, 512 * c:512 * (c + 1)], start=(c == 0), stop=lastc)
                    if lastc:
                        rr.then_inc(ht, 1)
                p.on("tensor", mmo)
            tm_ = tC.add(1)
            p.wait("vector", "tC", tm_)
            p.wait("vector", "yL", tld)
            p.on("vector", lambda e, pb=pb, buf=buf, n=n, r=r, h=hv: e.scalar_tensor_tensor(
                out=buf[:, 512 * n:512 * (n + 1)], in0=pb, scalar=rstd_c[:, r:r + 1],
                in1=buf[:, 512 * n:512 * (n + 1)], op0=ALU.mult, op1=ALU.add).then_inc(h, 1))
            tev = vC.add(1)
            tevs[r] = tev
            p.wait("sync", "vC", tev)
            tst = _dma(p, "sync", "yS", x3[128 * r:128 * (r + 1), 512 * n:512 * (n + 1)],
                       buf[:, 512 * n:512 * (n + 1)])
        p.wait("tensor", "vC", tev)
    p.wait("sync", "yS", p.ctr["yS"].v)
    p.barrier()

    xlnT3 = p.sb("xlnT3", [128, NE * T], BF16)
    last_tr = 0
    for r in range(NT):
        buf = xbuf[0]
        tld = _dma(p, "sync", "yL", buf[:], x3[128 * r:128 * (r + 1), :])
        p.wait("vector", "yL", tld)
        if r > 0:
            p.wait("scalar", "tC", last_tr)
        t_ap, _ = emit_ln(p, buf, xln_f[:], sc)
        ev, last_tr = emit_transpose_tiles(
            p, xln_f, xlnT3[:].rearrange("p (t c) -> p t c", c=T)[:, :, 128 * r:128 * (r + 1)],
            PSA[:, 0:2048], ident_f, NE, wait_s=("sC", t_ap))
        p.wait("tensor", "vC", ev)
        p.wait("sync", "vC", ev)
    tso = _dma(p, "sync", "yS", xlnT3o, xlnT3[:])
    for half in range(2):
        for et in range(NE):
            laste = (et == NE - 1)
            def mmsc(e, half=half, et=et, laste=laste):
                rr = e.matmul((psr1 if half == 0 else psr2), wrb[:, et:et + 1],
                              xlnT3[:, T * et + 512 * half:T * et + 512 * (half + 1)],
                              start=(et == 0), stop=laste)
                if laste and half == 1:
                    rr.then_inc(ht, 1)
            p.on("tensor", mmsc)
    tsc_ = tC.add(1)
    p.wait("vector", "tC", tsc_)
    p.on("vector", lambda e, h=hv: e.tensor_single_scalar(out=ssqf[0:1, 0:512], in_=psr1,
                                                          scalar=brc[0:1, 0:1], op=ALU.add).then_inc(h, 1))
    p.on("vector", lambda e, h=hv: e.tensor_single_scalar(out=ssqf[0:1, 512:1024], in_=psr2,
                                                          scalar=brc[0:1, 0:1], op=ALU.add).then_inc(h, 1))
    tsf = vC.add(2)
    p.wait("sync", "vC", tsf)
    _dma(p, "sync", "yS", scso, ssqf[:])
    p.wait("sync", "yS", p.ctr["yS"].v)
    p.emit()
    stack.close()
    return nc


def build_C():
    nc = bass.Bass("TRN2", target_bir_lowering=False)
    stack = contextlib.ExitStack()
    p = P(nc, stack)

    x3 = nc.dram_tensor("x3", [T, E], F32, kind="ExternalInput").ap()
    xlnT3 = nc.dram_tensor("xlnT3", [128, NE * T], BF16, kind="ExternalInput").ap()
    scall = nc.dram_tensor("scall", [1, 4096], F32, kind="ExternalInput").ap()
    msc = nc.dram_tensor("msc", [128, NT], F32, kind="ExternalInput").ap()
    w1 = nc.dram_tensor("w1", [E, DF], F32, kind="ExternalInput").ap()
    b1r = nc.dram_tensor("b1r", [128, NF], F32, kind="ExternalInput").ap()
    w2 = nc.dram_tensor("w2", [DF, E], F32, kind="ExternalInput").ap()
    b2r = nc.dram_tensor("b2r", [1, E], F32, kind="ExternalInput").ap()
    y = nc.dram_tensor("y", [T, E], F32, kind="ExternalOutput").ap()

    PSA = p.ps("PSA", [128, 2048], F32)
    PSB = p.ps("PSB", [128, 1024], F32)
    hv, hs, ht = p.sems["vC"], p.sems["sC"], p.sems["tC"]
    vC, sC, tC = p.ctr["vC"], p.ctr["sC"], p.ctr["tC"]

    xlt = p.sb("xlt", [128, NE * T], BF16)
    w1b = [p.sb(f"w1b{i}", [128, NE * 128], BF16) for i in range(2)]
    xbuf = [p.sb(f"xbuf{i}", [128, E], F32) for i in range(2)]
    tmp = p.sb("tmp", [128, 512], F32)
    b1c = p.sb("b1c", [128, NF], F32)
    b2b = p.sb("b2b", [1, E], BF16)
    b2f = p.sb("b2f", [1, E], F32)
    mscb = p.sb("mscb", [128, NT], F32)
    cnt = p.sb("cnt", [128, NT], F32)
    msel = p.sb("msel", [128, NT], F32)
    onesf = p.sb("onesf", [1, 128], F32)
    onesrow = p.sb("onesrow", [1, 128], BF16)
    sC1 = contextlib.ExitStack()
    scrow = p.sb("scrow", [1, 4096], F32, stack=sC1)
    J = p.sb("J", [128, 4096], F32, stack=sC1)
    junk = p.sb("junk", [128, 4096], F32, stack=sC1)

    tld0 = _dma(p, "sync", "yL", xlt[:], xlnT3)
    _dma(p, "sync", "yL", scrow[:], scall)
    _dma(p, "sync", "yL", mscb[:], msc)
    _dma(p, "sync", "yL", b1c[:], b1r)
    t_c0 = _dma(p, "sync", "yL", b2f[:], b2r)
    p.on("vector", lambda e: e.memset(onesf[:], 1.0))
    p.on("vector", lambda e: e.memset(onesrow[:], 1.0))
    p.wait("vector", "yL", t_c0)
    p.on("vector", lambda e, h=hv: e.tensor_copy(out=b2b[:], in_=b2f[:]).then_inc(h, 1))
    p.wait("vector", "vC", vC.add(1))

    psj = PSB[:, 0:512]
    for s8 in range(8):
        p.on("tensor", lambda e, s8=s8, h=ht: e.matmul(psj, onesf[:], scrow[0:1, 512 * s8:512 * (s8 + 1)],
                                                       start=True, stop=True).then_inc(h, 1))
        tj = tC.add(1)
        p.wait("vector", "tC", tj)
        p.on("vector", lambda e, s8=s8, h=hv: e.tensor_copy(out=J[:, 512 * s8:512 * (s8 + 1)],
                                                            in_=psj).then_inc(h, 1))
        tjc = vC.add(1)
        p.on("tensor", lambda e, t=tjc, h=hv: e.wait_ge(h, t))
    p.wait("vector", "vC", vC.v)
    for r in range(NT):
        p.on("vector", lambda e, r=r, h=hv: e.tensor_scalar(
            out=junk[:], in0=J[:], scalar1=mscb[:, r:r + 1], scalar2=0.0,
            op0=ALU.is_gt, op1=ALU.add, accum_out=cnt[:, r:r + 1]).then_inc(h, 1))
    p.wait("vector", "vC", vC.add(NT))
    p.on("vector", lambda e, h=hv: e.tensor_single_scalar(out=msel[:], in_=cnt[:], scalar=2048.0,
                                                          op=ALU.is_lt).then_inc(h, 1))
    tmsel = vC.add(1)
    p.barrier()
    sC1.close()

    h1g = p.sb("h1g", [128, 32 * T], BF16)
    oacc = p.sb("oacc", [128, NT * E], BF16)
    w2b = [p.sb(f"w2b{j}", [128, 32 * 256], BF16) for j in range(2)]
    w1r = w1.rearrange("(t p) f -> p t f", p=128)
    w2r = w2.rearrange("(t p) n -> p t n", p=128)
    psh = [PSA[:, 0:1024], PSA[:, 1024:2048]]
    psw = [PSB[:, 0:256], PSB[:, 512:768]]
    th_hist = {}
    gelu_hist = {}
    for fh in range(2):
        for ft in range(32):
            fg = 32 * fh + ft
            bufw = w1b[ft % 2]
            if fg >= 2:
                p.wait("gpsimd", "tC", th_hist[fg - 2])
            tw = _dma(p, "gpsimd", "gL", bufw[:].rearrange("p (t n) -> p t n", n=128),
                      w1r[:, :, 128 * fg:128 * (fg + 1)])
            p.wait("tensor", "gL", tw)
            if fg == 0:
                p.wait("tensor", "yL", tld0)
            if fg >= 2:
                p.wait("tensor", "sC", gelu_hist[fg - 2])
            ph = psh[fg % 2]
            for half in range(2):
                for et in range(NE):
                    laste = (et == NE - 1)
                    def mmh(e, half=half, et=et, bufw=bufw, ph=ph, laste=laste, lasth=(half == 1)):
                        rr = e.matmul(ph[:, 512 * half:512 * (half + 1)],
                                      bufw[:, 128 * et:128 * (et + 1)],
                                      xlt[:, T * et + 512 * half:T * et + 512 * (half + 1)],
                                      start=(et == 0), stop=laste)
                        if laste and lasth:
                            rr.then_inc(ht, 1)
                    p.on("tensor", mmh)
            th_ = tC.add(1)
            th_hist[fg] = th_
            p.wait("scalar", "tC", th_)
            p.on("scalar", lambda e, ft=ft, fg=fg, ph=ph, h=hs: e.activation(
                out=h1g[:, T * ft:T * (ft + 1)], in_=ph, func=AF.Gelu_apprx_tanh,
                bias=b1c[:, fg:fg + 1]).then_inc(h, 1))
            gelu_hist[fg] = sC.add(1)
        t_gelu_all = gelu_hist[32 * fh + 31]
        w2mm_hist = {}
        ev_hist = {}
        gi = 0
        for n in range(8):
            bufw2 = w2b[n % 2]
            if n >= 2:
                p.wait("gpsimd", "tC", w2mm_hist[n - 2])
            tw2 = _dma(p, "gpsimd", "gL", bufw2[:].rearrange("p (t n) -> p t n", n=256),
                       w2r[:, 32 * fh:32 * (fh + 1), 256 * n:256 * (n + 1)])
            p.wait("tensor", "gL", tw2)
            if n == 0:
                p.wait("tensor", "sC", t_gelu_all)
            for r in range(NT):
                pb = psw[gi % 2]
                if fh == 1:
                    buf = xbuf[r % 2]
                    tld = _dma(p, "sync", "yL", buf[:, 256 * n:256 * (n + 1)],
                               x3[128 * r:128 * (r + 1), 256 * n:256 * (n + 1)])
                if gi >= 2:
                    p.on("tensor", lambda e, t=ev_hist[gi - 2], h=hv: e.wait_ge(h, t))
                for c in range(32):
                    lastc = (c == 31)
                    stop = lastc and (fh == 0)
                    def mmw2(e, r=r, c=c, pb=pb, bufw2=bufw2, lastc=lastc, stop=stop, fh=fh):
                        rr = e.matmul(pb, h1g[:, T * c + 128 * r:T * c + 128 * (r + 1)],
                                      bufw2[:, 256 * c:256 * (c + 1)], start=(c == 0), stop=stop)
                        if lastc and fh == 0:
                            rr.then_inc(ht, 1)
                    p.on("tensor", mmw2)
                if fh == 1:
                    def mmb2(e, pb=pb, n=n, h=ht):
                        e.matmul(pb, onesrow[:], b2b[0:1, 256 * n:256 * (n + 1)],
                                 start=False, stop=True).then_inc(h, 1)
                    p.on("tensor", mmb2)
                tm_ = tC.add(1)
                w2mm_hist[n] = tm_
                p.wait("vector", "tC", tm_)
                if fh == 0:
                    p.on("vector", lambda e, r=r, n=n, pb=pb, h=hv: e.tensor_copy(
                        out=oacc[:, E * r + 256 * n:E * r + 256 * (n + 1)], in_=pb).then_inc(h, 1))
                    tev = vC.add(1)
                else:
                    p.wait("vector", "yL", tld)
                    p.on("vector", lambda e, r=r, n=n, pb=pb, h=hv: e.tensor_tensor(
                        out=tmp[:, 0:256], in0=pb, in1=oacc[:, E * r + 256 * n:E * r + 256 * (n + 1)],
                        op=ALU.add).then_inc(h, 1))
                    p.wait("vector", "vC", vC.add(1))
                    p.on("vector", lambda e, r=r, n=n, buf=buf, h=hv: e.scalar_tensor_tensor(
                        out=buf[:, 256 * n:256 * (n + 1)], in0=tmp[:, 0:256], scalar=msel[:, r:r + 1],
                        in1=buf[:, 256 * n:256 * (n + 1)], op0=ALU.mult, op1=ALU.add).then_inc(h, 1))
                    tev = vC.add(1)
                    p.wait("sync", "vC", tev)
                    tst = _dma(p, "sync", "yS", y[128 * r:128 * (r + 1), 256 * n:256 * (n + 1)],
                               buf[:, 256 * n:256 * (n + 1)])
                ev_hist[gi] = tev
                gi += 1
    p.wait("sync", "yS", p.ctr["yS"].v)
    p.emit()
    stack.close()
    return nc


_CACHE = {}


def _consts():
    identf = np.eye(128, dtype=np.float32)
    identb = np.eye(128, dtype=BF)
    return identf, identb, None, None


def _amask(is_start):
    am = np.zeros((40, 128, 128), dtype=np.float32)
    jj, ii = np.mgrid[0:128, 0:128]
    for r in range(8):
        am[r * 5 + 0][jj < ii] = NEG
        am[r * 5 + 4][jj > ii] = NEG
        if is_start:
            for m in range(5):
                if m < 4 - r:
                    am[r * 5 + m][:, :] = NEG
    return np.ascontiguousarray(am.astype(BF))


def run_A(X, inputs):
    identf, identb, _, _ = _consts()
    am_st, am_n = _amask(True), _amask(False)
    if "A" not in _CACHE:
        _CACHE["A"] = build_A()
    sclb = np.full((128, 1), np.float32(inputs["L0_ffn_scale"]), np.float32)
    scl1b = np.full((128, 1), np.float32(inputs["L0_ffn_scale1"]), np.float32)
    in_maps = []
    for c in range(8):
        b, k = c // 4, c % 4
        x0 = np.ascontiguousarray(X[b, T * k:T * (k + 1)])
        xhv = X[b, T * k - HALO:T * k] if k > 0 else np.zeros((HALO, E), np.float32)
        in_maps.append({
            "x0": x0, "xh": np.ascontiguousarray(xhv),
            "wqkv": np.ascontiguousarray(np.asarray(inputs["L0_attn_w_qkv"], np.float32)),
            "wout": np.ascontiguousarray(np.asarray(inputs["L0_attn_w_out"], np.float32)),
            "in2": np.ascontiguousarray(np.asarray(inputs["L0_ffn_input2"], np.float32)),
            "wk1": np.ascontiguousarray(np.asarray(inputs["L1_attn_Wk1"], np.float32)),
            "wq1": np.ascontiguousarray(np.asarray(inputs["L1_attn_Wq1"], np.float32)),
            "wk2": np.ascontiguousarray(np.asarray(inputs["L1_attn_Wk2"], np.float32)),
            "wv": np.ascontiguousarray(np.asarray(inputs["L1_attn_Wv"], np.float32)),
            "identf": identf, "identb": identb, "amask": am_st if k == 0 else am_n,
            "sclb": sclb, "scl1b": scl1b,
        })
    res = run_bass_kernel_spmd(_CACHE["A"], in_maps, core_ids=list(range(8)))
    return res.results


def run_B(outA, inputs):
    if "B" not in _CACHE:
        _CACHE["B"] = build_B()
    identf, _, _, _ = _consts()
    wq2 = np.ascontiguousarray(np.asarray(inputs["L1_attn_Wq2"], np.float32))
    wo = np.ascontiguousarray(np.asarray(inputs["L1_attn_Wo"], np.float32))
    nwr = np.ascontiguousarray(np.asarray(inputs["L1_attn_norm_w"], np.float32).reshape(NCH, 128).T)
    wrr = np.ascontiguousarray(np.asarray(inputs["L1_ffn_w_r"], np.float32).reshape(NE, 128).T)
    brr = np.full((1, 1), np.float32(inputs["L1_ffn_b_r"]), np.float32)
    in_maps = []
    for c in range(8):
        b, k = c // 4, c % 4
        predA = np.zeros((128, 96), np.float32)
        predH = np.zeros((128, 96), np.float32)
        predA[:, 0:32 * (3 - k)] = 1.0
        for idx, j in enumerate(range(k)):
            col = 32 * (3 - k + idx)
            aj = np.asarray(outA[4 * b + j]["acum"])[:, T - 1].astype(np.float32)
            hj = np.asarray(outA[4 * b + j]["hloc"])[:, T - 1].astype(np.float32)
            predA[:, col:col + 32] = aj.reshape(NCH, 128).T
            predH[:, col:col + 32] = hj.reshape(NCH, 128).T
        in_maps.append({
            "x2": np.ascontiguousarray(outA[c]["x2"]),
            "acum": np.ascontiguousarray(outA[c]["acum"]),
            "hloc": np.ascontiguousarray(outA[c]["hloc"]),
            "q1t": np.ascontiguousarray(outA[c]["q1o"].T),
            "predA": predA, "predH": predH,
            "wq2": wq2, "wo": wo, "nwr": nwr, "wrr": wrr, "brr": brr,
            "identf": identf,
        })
    res = run_bass_kernel_spmd(_CACHE["B"], in_maps, core_ids=list(range(8)))
    return res.results


def run_C(outB, inputs):
    if "C" not in _CACHE:
        _CACHE["C"] = build_C()
    w1 = np.ascontiguousarray(np.asarray(inputs["L1_ffn_w1"], np.float32))
    w2 = np.ascontiguousarray(np.asarray(inputs["L1_ffn_w2"], np.float32))
    b1r = np.ascontiguousarray(np.asarray(inputs["L1_ffn_b1"], np.float32).reshape(NF, 128).T)
    b2r = np.ascontiguousarray(np.asarray(inputs["L1_ffn_b2"], np.float32).reshape(1, E))
    in_maps = []
    for c in range(8):
        b, k = c // 4, c % 4
        scall = np.concatenate([outB[4 * b + j]["scso"][0] for j in range(4)]).reshape(1, 4096)
        msc = np.ascontiguousarray(outB[c]["scso"][0].reshape(NT, 128).T)
        in_maps.append({
            "x3": np.ascontiguousarray(outB[c]["x3"]),
            "xlnT3": np.ascontiguousarray(outB[c]["xlnT3o"]),
            "scall": np.ascontiguousarray(scall), "msc": msc,
            "w1": w1, "b1r": b1r, "w2": w2, "b2r": b2r,
        })
    res = run_bass_kernel_spmd(_CACHE["C"], in_maps, core_ids=list(range(8)))
    return res.results


def kernel(L0_attn_x, L0_attn_w_qkv, L0_attn_w_out, L0_ffn_input2, L0_ffn_scale,
           L0_ffn_scale1, L1_attn_Wq1, L1_attn_Wq2, L1_attn_Wk1, L1_attn_Wk2,
           L1_attn_Wv, L1_attn_Wo, L1_attn_norm_w, L1_ffn_w_r, L1_ffn_b_r,
           L1_ffn_w1, L1_ffn_b1, L1_ffn_w2, L1_ffn_b2):
    inputs = dict(L0_attn_w_qkv=L0_attn_w_qkv, L0_attn_w_out=L0_attn_w_out,
                  L0_ffn_input2=L0_ffn_input2, L0_ffn_scale=L0_ffn_scale,
                  L0_ffn_scale1=L0_ffn_scale1, L1_attn_Wq1=L1_attn_Wq1,
                  L1_attn_Wq2=L1_attn_Wq2, L1_attn_Wk1=L1_attn_Wk1,
                  L1_attn_Wk2=L1_attn_Wk2, L1_attn_Wv=L1_attn_Wv,
                  L1_attn_Wo=L1_attn_Wo, L1_attn_norm_w=L1_attn_norm_w,
                  L1_ffn_w_r=L1_ffn_w_r, L1_ffn_b_r=L1_ffn_b_r,
                  L1_ffn_w1=L1_ffn_w1, L1_ffn_b1=L1_ffn_b1,
                  L1_ffn_w2=L1_ffn_w2, L1_ffn_b2=L1_ffn_b2)
    X = np.ascontiguousarray(np.asarray(L0_attn_x, dtype=np.float32))
    B, S, _ = X.shape
    outA = run_A(X, inputs)
    outB = run_B(outA, inputs)
    outC = run_C(outB, inputs)
    out = np.empty((B, S, E), np.float32)
    for c in range(8):
        b, k = c // 4, c % 4
        out[b, T * k:T * (k + 1)] = outC[c]["y"]
    return out


# version 05
# speedup vs baseline: 1.0099x; 1.0099x over previous
import contextlib
import numpy as np
import ml_dtypes
import concourse.bass as bass
import concourse.mybir as mybir
from concourse.bass_utils import run_bass_kernel_spmd

F32 = mybir.dt.float32
BF16 = mybir.dt.bfloat16
FP8 = mybir.dt.float8e4
AF = mybir.ActivationFunctionType
ALU = mybir.AluOpType
BF = ml_dtypes.bfloat16

T = 1024
HALO = 512
E = 2048
E2 = 4096
DF = 8192
NH = 16
D = 128
NE = E // 128
NT = T // 128
NTH = (T + HALO) // 128
NCH = E2 // 128
NF = DF // 128
TH = NTH * 128
SCALE = float(1.0 / np.sqrt(128.0))
NEG = -1.0e30

ENGS = ("sync", "gpsimd", "vector", "scalar", "tensor")


class Ctr:
    __slots__ = ("v",)
    def __init__(self):
        self.v = 0
    def add(self, n=1):
        self.v += n
        return self.v


class P:
    def __init__(self, nc, stack):
        self.nc = nc
        self.stack = stack
        self.q = {k: [] for k in ENGS}
        self.sems = {}
        self.ctr = {}
        self.nbar = 0
        for name in ("bar", "yL", "gL", "cL", "vC", "sC", "tC", "yS", "gS"):
            self.sems[name] = stack.enter_context(nc.semaphore(name))
            self.ctr[name] = Ctr()

    def on(self, eng, fn):
        self.q[eng].append(fn)

    def wait(self, eng, sem, val):
        h = self.sems[sem]
        self.on(eng, lambda e, h=h, v=val: e.wait_ge(h, v))

    def barrier(self):
        self.nbar += 1
        tgt = 5 * self.nbar
        h = self.sems["bar"]
        for eng in ENGS:
            self.on(eng, lambda e, h=h: e.sem_inc(h, 1))
            self.on(eng, lambda e, h=h, t=tgt: e.wait_ge(h, t))

    def sb(self, name, shape, dtype, stack=None):
        return (stack or self.stack).enter_context(self.nc.sbuf_tensor(name, shape, dtype))

    def ps(self, name, shape, dtype=F32, stack=None):
        return (stack or self.stack).enter_context(self.nc.psum_tensor(name, shape, dtype))

    def emit(self):
        nc = self.nc
        with nc.Block() as block:
            @block.sync
            def _(e):
                for f in self.q["sync"]:
                    f(e)
            @block.gpsimd
            def _(e):
                for f in self.q["gpsimd"]:
                    f(e)
            @block.vector
            def _(e):
                for f in self.q["vector"]:
                    f(e)
            @block.scalar
            def _(e):
                for f in self.q["scalar"]:
                    f(e)
            @block.tensor
            def _(e):
                for f in self.q["tensor"]:
                    f(e)


def _dma(p, eng, sem, out, in_, **kw):
    h = p.sems[sem]
    p.on(eng, lambda e, o=out, i=in_, h=h, kw=kw: e.dma_start(out=o, in_=i, **kw).then_inc(h, 16))
    return p.ctr[sem].add(16)


def _dma_tr(p, sem, out, in_):
    h = p.sems[sem]
    p.on("scalar", lambda e, o=out, i=in_, h=h: e.dma_start(out=o, in_=i, transpose=True).then_inc(h, 16))
    return p.ctr[sem].add(16)



def emit_ln2(p, xbuf, dst, sc2, par, scale_ap=None):
    hv, hs = p.sems["vC"], p.sems["sC"]
    vC, sC = p.ctr["vC"], p.ctr["sC"]
    bn24, mv, negm, u, rstd = (sc2[k][par] for k in ("bn24", "mv", "negm", "u", "rstd"))
    for c in range(4):
        if c < 3:
            p.on("vector", lambda e, c=c, bn24=bn24, xbuf=xbuf: e.bn_stats(
                out=bn24[:, 6 * c:6 * (c + 1)], in_=xbuf[:, 512 * c:512 * (c + 1)]))
        else:
            p.on("vector", lambda e, c=c, bn24=bn24, xbuf=xbuf, h=hv: e.bn_stats(
                out=bn24[:, 6 * c:6 * (c + 1)], in_=xbuf[:, 512 * c:512 * (c + 1)]).then_inc(h, 1))
    p.wait("vector", "vC", vC.add(1))
    p.on("vector", lambda e, bn24=bn24, mv=mv, h=hv: e.bn_aggr(out=mv[:], in_=bn24[:]).then_inc(h, 1))
    p.wait("vector", "vC", vC.add(1))
    p.on("vector", lambda e, mv=mv, negm=negm, h=hv: e.tensor_single_scalar(
        out=negm[:], in_=mv[:, 0:1], scalar=-1.0, op=ALU.mult).then_inc(h, 1))
    tv = vC.add(1)
    p.wait("scalar", "vC", tv)
    p.on("scalar", lambda e, mv=mv, u=u, h=hs: e.activation(
        out=u[:], in_=mv[:, 1:2], func=AF.Ln, bias=sc2["eps"][:]).then_inc(h, 1))
    p.wait("scalar", "sC", p.ctr["sC"].add(1))
    p.on("scalar", lambda e, u=u, rstd=rstd, h=hs: e.activation(
        out=rstd[:], in_=u[:], func=AF.Exp, scale=-0.5).then_inc(h, 1))
    p.wait("scalar", "sC", p.ctr["sC"].add(1))
    if scale_ap is not None:
        p.on("scalar", lambda e, rstd=rstd, m=scale_ap, h=hs: e.activation(
            out=rstd[:], in_=rstd[:], func=AF.Identity, scale=m).then_inc(h, 1))
        p.wait("scalar", "sC", p.ctr["sC"].add(1))
    p.on("scalar", lambda e, xbuf=xbuf, d=dst, negm=negm, h=hs: e.activation(
        out=d, in_=xbuf[:], func=AF.Identity, bias=negm[:]).then_inc(h, 1))
    p.wait("scalar", "sC", p.ctr["sC"].add(1))
    p.on("scalar", lambda e, d=dst, rstd=rstd, h=hs: e.activation(
        out=d, in_=d, func=AF.Identity, scale=rstd[:]).then_inc(h, 1))
    return p.ctr["sC"].add(1)


def mk_sc2(p, name, stack=None):
    return {
        "bn24": [p.sb(f"{name}bn{j}", [128, 24], F32, stack=stack) for j in range(2)],
        "mv": [p.sb(f"{name}mv{j}", [128, 2], F32, stack=stack) for j in range(2)],
        "negm": [p.sb(f"{name}nm{j}", [128, 1], F32, stack=stack) for j in range(2)],
        "u": [p.sb(f"{name}u{j}", [128, 1], F32, stack=stack) for j in range(2)],
        "rstd": [p.sb(f"{name}rs{j}", [128, 1], F32, stack=stack) for j in range(2)],
        "eps": p.sb(f"{name}eps", [128, 1], F32, stack=stack),
    }

def emit_ln(p, xbuf, dst, sc, scale_ap=None):
    hv, hs = p.sems["vC"], p.sems["sC"]
    vC, sC = p.ctr["vC"], p.ctr["sC"]
    for c in range(4):
        if c < 3:
            p.on("vector", lambda e, c=c: e.bn_stats(out=sc["bn24"][:, 6 * c:6 * (c + 1)],
                                                     in_=xbuf[:, 512 * c:512 * (c + 1)]))
        else:
            p.on("vector", lambda e, c=c, h=hv: e.bn_stats(out=sc["bn24"][:, 6 * c:6 * (c + 1)],
                                                           in_=xbuf[:, 512 * c:512 * (c + 1)]).then_inc(h, 1))
    t0 = vC.add(1)
    p.wait("vector", "vC", t0)
    p.on("vector", lambda e, h=hv: e.bn_aggr(out=sc["mv"][:], in_=sc["bn24"][:]).then_inc(h, 1))
    t1 = vC.add(1)
    p.wait("scalar", "vC", t1)
    p.on("scalar", lambda e, h=hs: e.activation(out=sc["sd"][:], in_=sc["mv"][:, 1:2],
                                                func=AF.Sqrt, bias=sc["eps"][:]).then_inc(h, 1))
    t2 = sC.add(1)
    p.wait("vector", "sC", t2)
    p.on("vector", lambda e, h=hv: e.reciprocal(out=sc["rstd"][:], in_=sc["sd"][:]).then_inc(h, 1))
    tr0 = vC.add(1)
    p.wait("vector", "vC", tr0)
    if scale_ap is not None:
        p.on("vector", lambda e, m=scale_ap, h=hv: e.tensor_single_scalar(
            out=sc["rstd"][:], in_=sc["rstd"][:], scalar=m, op=ALU.mult).then_inc(h, 1))
        tr1 = vC.add(1)
        p.wait("vector", "vC", tr1)
    p.on("vector", lambda e, h=hv: e.tensor_scalar(out=sc["negmr"][:], in0=sc["mv"][:, 0:1],
                                                   scalar1=sc["rstd"][:], scalar2=-1.0,
                                                   op0=ALU.mult, op1=ALU.mult).then_inc(h, 1))
    t3 = vC.add(1)
    p.wait("scalar", "vC", t3)
    p.on("scalar", lambda e, d=dst, h=hs: e.activation(out=d, in_=xbuf[:], func=AF.Identity,
                                                       scale=sc["rstd"][:], bias=sc["negmr"][:]).then_inc(h, 1))
    return sC.add(1), t3


def emit_transpose_tiles(p, src, dst_view, pst, ident, ncols, wait_s=None):
    tC, vC = p.ctr["tC"], p.ctr["vC"]
    ht, hv = p.sems["tC"], p.sems["vC"]
    if wait_s is not None:
        p.wait("tensor", wait_s[0], wait_s[1])
    for j in range(ncols):
        if j < ncols - 1:
            p.on("tensor", lambda e, j=j: e.transpose(pst[:, 128 * j:128 * (j + 1)],
                                                      src[:, 128 * j:128 * (j + 1)], ident[:]))
        else:
            p.on("tensor", lambda e, j=j, h=ht: e.transpose(pst[:, 128 * j:128 * (j + 1)],
                                                            src[:, 128 * j:128 * (j + 1)], ident[:]).then_inc(h, 1))
    t = tC.add(1)
    p.wait("vector", "tC", t)
    p.on("vector", lambda e, h=hv: e.tensor_copy(
        out=dst_view, in_=pst[:, 0:128 * ncols].rearrange("p (t c) -> p t c", c=128)).then_inc(h, 1))
    return vC.add(1), t


def build_A():
    nc = bass.Bass("TRN2", target_bir_lowering=False)
    stack = contextlib.ExitStack()
    p = P(nc, stack)

    x0 = nc.dram_tensor("x0", [T, E], F32, kind="ExternalInput").ap()
    xh = nc.dram_tensor("xh", [HALO, E], F32, kind="ExternalInput").ap()
    wqkv = nc.dram_tensor("wqkv", [3 * E, E], F32, kind="ExternalInput").ap()
    wout = nc.dram_tensor("wout", [E, E], F32, kind="ExternalInput").ap()
    in2 = nc.dram_tensor("in2", [E, E], F32, kind="ExternalInput").ap()
    wk1 = nc.dram_tensor("wk1", [E, D], F32, kind="ExternalInput").ap()
    wq1 = nc.dram_tensor("wq1", [E, D], F32, kind="ExternalInput").ap()
    wk2 = nc.dram_tensor("wk2", [D, E2], F32, kind="ExternalInput").ap()
    wv = nc.dram_tensor("wv", [E, E2], F32, kind="ExternalInput").ap()
    identf = nc.dram_tensor("identf", [128, 128], F32, kind="ExternalInput").ap()
    identb = nc.dram_tensor("identb", [128, 128], BF16, kind="ExternalInput").ap()
    amask = nc.dram_tensor("amask", [40, 128, 128], BF16, kind="ExternalInput").ap()
    sclb = nc.dram_tensor("sclb", [128, 1], F32, kind="ExternalInput").ap()
    scl1b = nc.dram_tensor("scl1b", [128, 1], F32, kind="ExternalInput").ap()

    x2 = nc.dram_tensor("x2", [T, E], F32, kind="ExternalOutput").ap()
    acum = nc.dram_tensor("acum", [E2, T], BF16, kind="ExternalOutput").ap()
    hloc = nc.dram_tensor("hloc", [E2, T], BF16, kind="ExternalOutput").ap()
    q1o = nc.dram_tensor("q1o", [T, D], F32, kind="ExternalOutput").ap()

    scrq = nc.dram_tensor("scrq", [3 * E, E], BF16, kind="Internal").ap()
    scro = nc.dram_tensor("scro", [E, E], BF16, kind="Internal").ap()
    q2h = nc.dram_tensor("q2h", [E, E], BF16, kind="Internal").ap()
    xa = nc.dram_tensor("xa", [T, E], F32, kind="Internal").ap()

    ident_f = p.sb("ident_f", [128, 128], F32)
    ident_b = p.sb("ident_b", [128, 128], BF16)
    eps5 = p.sb("eps5", [128, 1], F32)
    scl_c = p.sb("scl_c", [128, 1], F32)
    scl1_c = p.sb("scl1_c", [128, 1], F32)
    dq_c = p.sb("dq_c", [128, 1], F32)
    xbuf = [p.sb(f"xbuf{i}", [128, E], F32) for i in range(2)]
    xln_f = p.sb("xln_f", [128, E], F32)
    xln_f2 = p.sb("xln_f2", [128, E], F32)
    sc2 = mk_sc2(p, "l0")
    sc = {
        "bn24": p.sb("bn24", [128, 24], F32),
        "mv": p.sb("mv", [128, 2], F32),
        "sd": p.sb("sd", [128, 1], F32),
        "rstd": p.sb("rstd", [128, 1], F32),
        "negmr": p.sb("negmr", [128, 1], F32),
        "eps": eps5,
    }
    PSA = p.ps("PSA", [128, 2048], F32)
    PSB = p.ps("PSB", [128, 1024], F32)
    PSTB = p.ps("PSTB", [128, 2048], BF16)

    vC, sC, tC = p.ctr["vC"], p.ctr["sC"], p.ctr["tC"]
    hv, hs, ht = p.sems["vC"], p.sems["sC"], p.sems["tC"]

    _dma(p, "sync", "yL", ident_f[:], identf)
    _dma(p, "sync", "yL", ident_b[:], identb)
    _dma(p, "sync", "yL", scl_c[:], sclb)
    t_consts = _dma(p, "sync", "yL", scl1_c[:], scl1b)
    p.wait("tensor", "yL", t_consts)
    p.wait("scalar", "yL", t_consts)
    p.on("vector", lambda e: e.memset(eps5[:], 1e-5))
    p.wait("vector", "yL", t_consts)
    p.on("vector", lambda e, h=hv: e.tensor_mul(out=dq_c[:], in0=scl_c[:], in1=scl1_c[:]).then_inc(h, 1))
    p.wait("vector", "vC", p.ctr["vC"].add(1))
    p.on("vector", lambda e, h=hv: e.reciprocal(out=dq_c[:], in_=dq_c[:]).then_inc(h, 1))
    p.wait("vector", "vC", p.ctr["vC"].add(1))

    for j in range(4):
        _dma(p, "gpsimd", "gL", scrq[1536 * j:1536 * (j + 1), :], wqkv[1536 * j:1536 * (j + 1), :])
    for j in range(2):
        _dma(p, "gpsimd", "gL", scro[1024 * j:1024 * (j + 1), :], wout[1024 * j:1024 * (j + 1), :])
    t_bounce = p.ctr["gL"].v

    s_q2 = contextlib.ExitStack()
    q2f = p.sb("q2f", [128, NE * 512], F32, stack=s_q2)
    q2q = p.sb("q2q", [128, NE * 512], FP8, stack=s_q2)
    q2b = p.sb("q2b", [128, NE * 512], BF16, stack=s_q2)
    in2r = in2.rearrange("(t p) n -> p t n", p=128)
    q2hr = q2h.rearrange("(t p) n -> p t n", p=128)
    for ns in range(4):
        tld = _dma(p, "gpsimd", "gL", q2f[:].rearrange("p (t n) -> p t n", n=512),
                   in2r[:, :, 512 * ns:512 * (ns + 1)])
        p.wait("vector", "gL", tld)
        p.on("vector", lambda e, h=hv: e.tensor_single_scalar(out=q2f[:], in_=q2f[:], scalar=scl1_c[:],
                                                               op=ALU.mult).then_inc(h, 1))
        p.wait("vector", "vC", vC.add(1))
        p.on("vector", lambda e, h=hv: e.tensor_copy(out=q2q[:], in_=q2f[:]).then_inc(h, 1))
        p.wait("vector", "vC", vC.add(1))
        p.on("vector", lambda e, h=hv: e.tensor_copy(out=q2b[:], in_=q2q[:]).then_inc(h, 1))
        tq = vC.add(1)
        p.wait("gpsimd", "vC", tq)
        tst = _dma(p, "gpsimd", "gS", q2hr[:, :, 512 * ns:512 * (ns + 1)],
                   q2b[:].rearrange("p (t n) -> p t n", n=512))
        p.wait("gpsimd", "gS", tst)
    t_q2done = p.ctr["gS"].v
    s_q2.close()

    s1 = contextlib.ExitStack()
    xlnT = p.sb("xlnT", [128, NE * TH], BF16, stack=s1)

    xln2 = [xln_f, xln_f2]
    p.on("vector", lambda e: e.memset(sc2["eps"][:], 1e-5))
    pst_half = [PSA[:, 0:1024], PSA[:, 1024:2048]]
    ap_h, tr_h, ev_h = {}, {}, {}
    def ln_tr_tile(i, srcap, dstv_fn, n_et=NE):
        par = i % 2
        buf = xbuf[par]
        if i >= 2:
            p.wait("sync", "sC", ap_h[i - 2])
        tld = _dma(p, "sync", "yL", buf[:], srcap)
        p.wait("vector", "yL", tld)
        p.wait("scalar", "yL", tld)
        if i >= 2:
            p.wait("scalar", "tC", tr_h[i - 2])
        ap = emit_ln2(p, buf, xln2[par][:], sc2, par)
        ap_h[i] = ap
        p.wait("tensor", "sC", ap)
        for hb in range(2):
            ph = pst_half[hb]
            if i >= 1:
                p.on("tensor", lambda e, t=ev_h[(i - 1, hb)], h=hv: e.wait_ge(h, t))
            for j in range(8):
                et = 8 * hb + j
                lastj = (j == 7)
                def tr1(e, et=et, j=j, par=par, ph=ph, lastj=lastj):
                    rr = e.transpose(ph[:, 128 * j:128 * (j + 1)],
                                     xln2[par][:, 128 * et:128 * (et + 1)], ident_f[:])
                    if lastj:
                        rr.then_inc(ht, 1)
                p.on("tensor", tr1)
            t = tC.add(1)
            p.wait("vector", "tC", t)
            p.on("vector", lambda e, i=i, hb=hb, ph=ph, dv=dstv_fn(i, hb), h=hv: e.tensor_copy(
                out=dv, in_=ph.rearrange("p (t c) -> p t c", c=128)).then_inc(h, 1))
            ev_h[(i, hb)] = vC.add(1)
        tr_h[i] = t
    for i in range(NTH):
        srcap = xh[128 * i:128 * (i + 1), :] if i < 4 else x0[128 * (i - 4):128 * (i - 3), :]
        ln_tr_tile(i, srcap,
                   lambda i, hb: xlnT[:].rearrange("p (t c) -> p t c", c=TH)[:, 8 * hb:8 * (hb + 1), 128 * i:128 * (i + 1)])
    p.barrier()

    osb = p.sb("osb", [128, NT * E], BF16, stack=s1)
    s1a = contextlib.ExitStack()
    wT = p.sb("wT", [128, NE * 512], BF16, stack=s1a)
    qT = p.sb("qT", [128, 4 * T], BF16, stack=s1a)
    kT = p.sb("kT", [128, 4 * TH], BF16, stack=s1a)
    VW = 4 * 129
    vsb = p.sb("vsb", [128, NTH * VW], BF16, stack=s1a)
    et2 = [p.sb(f"et2_{j}", [128, 640], BF16, stack=s1a) for j in range(2)]
    amask_sb = p.sb("amask_sb", [128, 40 * 128], BF16, stack=s1a)
    rcol = p.sb("rcol", [128, 1], F32, stack=s1a)
    psq = PSA[:, 0:1024]
    psk = PSA[:, 0:1536]
    psv = PSA[:, 0:512]

    tam = _dma(p, "sync", "yL", amask_sb[:], amask.rearrange("s j i -> j s i"))
    p.wait("tensor", "yL", tam)
    p.wait("scalar", "gL", t_bounce)
    for hg in range(4):
        for et in range(NE):
            tw = _dma_tr(p, "cL", wT[:, 512 * et:512 * (et + 1)],
                         scrq[512 * hg:512 * (hg + 1), 128 * et:128 * (et + 1)])
        p.wait("tensor", "cL", tw)
        for tt in range(NTH):
            for hh in range(4):
                p.on("vector", lambda e, tt=tt, hh=hh: e.memset(
                    vsb[:, VW * tt + 129 * hh + 128:VW * tt + 129 * hh + 129], 1.0))
        p.on("vector", lambda e, h=hv: e.engine_nop().then_inc(h, 1))
        t_ones = vC.add(1)

        for hh in range(4):
            for half in range(2):
                for et in range(NE):
                    last = (et == NE - 1)
                    def mmq(e, hh=hh, half=half, et=et, last=last):
                        r = e.matmul(psq[:, 512 * half:512 * (half + 1)],
                                     wT[:, 512 * et + 128 * hh:512 * et + 128 * (hh + 1)],
                                     xlnT[:, TH * et + HALO + 512 * half:TH * et + HALO + 512 * (half + 1)],
                                     start=(et == 0), stop=last)
                        if last and half == 1:
                            r.then_inc(ht, 1)
                    p.on("tensor", mmq)
            tq_ = tC.add(1)
            p.wait("vector", "tC", tq_)
            p.on("vector", lambda e, hh=hh, h=hv: e.tensor_single_scalar(
                out=qT[:, T * hh:T * (hh + 1)], in_=psq, scalar=SCALE,
                op=ALU.mult).then_inc(h, 1))
            t_qev = vC.add(1)
            p.on("tensor", lambda e, t=t_qev, h=hv: e.wait_ge(h, t))
        p.wait("scalar", "tC", tC.v)
        for et in range(NE):
            tw = _dma_tr(p, "cL", wT[:, 512 * et:512 * (et + 1)],
                         scrq[E + 512 * hg:E + 512 * (hg + 1), 128 * et:128 * (et + 1)])
        p.wait("tensor", "cL", tw)
        for hh in range(4):
            for third in range(3):
                for et in range(NE):
                    last = (et == NE - 1)
                    def mmk(e, hh=hh, third=third, et=et, last=last):
                        r = e.matmul(psk[:, 512 * third:512 * (third + 1)],
                                     wT[:, 512 * et + 128 * hh:512 * et + 128 * (hh + 1)],
                                     xlnT[:, TH * et + 512 * third:TH * et + 512 * (third + 1)],
                                     start=(et == 0), stop=last)
                        if last and third == 2:
                            r.then_inc(ht, 1)
                    p.on("tensor", mmk)
            tk_ = tC.add(1)
            p.wait("vector", "tC", tk_)
            p.on("vector", lambda e, hh=hh, h=hv: e.tensor_copy(
                out=kT[:, TH * hh:TH * (hh + 1)], in_=psk).then_inc(h, 1))
            t_kev = vC.add(1)
            p.on("tensor", lambda e, t=t_kev, h=hv: e.wait_ge(h, t))
        p.wait("scalar", "tC", tC.v)
        for et in range(NE):
            tw = _dma_tr(p, "cL", wT[:, 512 * et:512 * (et + 1)],
                         scrq[2 * E + 512 * hg:2 * E + 512 * (hg + 1), 128 * et:128 * (et + 1)])
        p.wait("tensor", "cL", tw)
        p.wait("tensor", "vC", t_ones)
        for tt in range(NTH):
            for et in range(NE):
                last = (et == NE - 1)
                def mmv(e, tt=tt, et=et, last=last):
                    r = e.matmul(psv, xlnT[:, TH * et + 128 * tt:TH * et + 128 * (tt + 1)],
                                 wT[:, 512 * et:512 * (et + 1)], start=(et == 0), stop=last)
                    if last:
                        r.then_inc(ht, 1)
                p.on("tensor", mmv)
            tv_ = tC.add(1)
            p.wait("vector", "tC", tv_)
            p.on("vector", lambda e, tt=tt, h=hv: e.tensor_copy(
                out=vsb[:].rearrange("p (t h c) -> p t h c", t=NTH, h=4)[:, tt, :, 0:128],
                in_=psv.rearrange("p (h c) -> p h c", h=4)).then_inc(h, 1))
            t_vev = vC.add(1)
            p.on("tensor", lambda e, t=t_vev, h=hv: e.wait_ge(h, t))

        psst_ar = [PSB[:, 0:640], PSA[:, 0:640]]
        pso_ar = [PSB[:, 768:900], PSA[:, 768:900]]
        st_h, ex_h, vm_h, evac_h = {}, {}, {}, {}
        iters = [(hh, r) for hh in range(4) for r in range(NT)]
        def emit_vmm(i):
            hh, r = iters[i]
            p.wait("tensor", "sC", ex_h[i])
            for m in range(5):
                lastm = (m == 4)
                def mmo(e, hh=hh, r=r, m=m, i=i, lastm=lastm):
                    rr = e.matmul(pso_ar[i % 2][:, 0:129], et2[i % 2][:, 128 * m:128 * (m + 1)],
                                  vsb[:, VW * (r + m) + 129 * hh:VW * (r + m) + 129 * (hh + 1)],
                                  start=(m == 0), stop=lastm)
                    if lastm:
                        rr.then_inc(ht, 1)
                p.on("tensor", mmo)
            vm_h[i] = tC.add(1)
        def emit_evac(i):
            hh, r = iters[i]
            p.wait("vector", "tC", vm_h[i])
            p.on("vector", lambda e, i=i, h=hv: e.reciprocal(out=rcol[:], in_=pso_ar[i % 2][:, 128:129]).then_inc(h, 1))
            p.wait("vector", "vC", vC.add(1))
            p.on("vector", lambda e, hg=hg, hh=hh, r=r, i=i, h=hv: e.tensor_single_scalar(
                out=osb[:, E * r + 128 * (4 * hg + hh):E * r + 128 * (4 * hg + hh + 1)],
                in_=pso_ar[i % 2][:, 0:128], scalar=rcol[:], op=ALU.mult).then_inc(h, 1))
            evac_h[i] = vC.add(1)
        for i, (hh, r) in enumerate(iters):
            if i >= 2:
                p.wait("tensor", "sC", ex_h[i - 2])
                p.on("tensor", lambda e, t=evac_h[i - 2], h=hv: e.wait_ge(h, t))
            slots = [0, 4] + [m for m in range(1, 4) if m < 4 - r]
            for m in range(5):
                p.on("tensor", lambda e, hh=hh, r=r, m=m, i=i: e.matmul(
                    psst_ar[i % 2][:, 128 * m:128 * (m + 1)],
                    kT[:, TH * hh + 128 * (r + m):TH * hh + 128 * (r + m + 1)],
                    qT[:, T * hh + 128 * r:T * hh + 128 * (r + 1)],
                    start=True, stop=(m not in slots)))
                if m in slots:
                    lastm = (m == max(slots))
                    def mmask(e, r=r, m=m, i=i, lastm=lastm):
                        rr = e.matmul(psst_ar[i % 2][:, 128 * m:128 * (m + 1)], ident_b[:],
                                      amask_sb[:, 128 * (5 * r + m):128 * (5 * r + m + 1)],
                                      start=False, stop=True)
                        if lastm:
                            rr.then_inc(ht, 1)
                    p.on("tensor", mmask)
            st_h[i] = tC.add(1)
            if i >= 1:
                emit_vmm(i - 1)
            p.wait("scalar", "tC", st_h[i])
            p.on("scalar", lambda e, i=i, h=hs: e.activation(out=et2[i % 2][:], in_=psst_ar[i % 2],
                                                             func=AF.Exp).then_inc(h, 1))
            ex_h[i] = sC.add(1)
            if i >= 1:
                emit_evac(i - 1)
        emit_vmm(len(iters) - 1)
        emit_evac(len(iters) - 1)
        p.barrier()

    s1a.close()
    oT = p.sb("oT", [128, NE * T], BF16, stack=s1)
    for r in range(NT):
        ev, ttr = emit_transpose_tiles(
            p, osb[:, E * r:E * (r + 1)],
            oT[:].rearrange("p (t c) -> p t c", c=T)[:, :, 128 * r:128 * (r + 1)],
            PSTB, ident_b, NE)
        p.wait("tensor", "vC", ev)
    p.barrier()

    woT = p.sb("woT", [128, NE * 512], BF16, stack=s1)
    psm = [PSA[:, 0:512], PSA[:, 512:1024]]
    for n in range(4):
        if n > 0:
            p.wait("scalar", "tC", p.ctr["tC"].v)
        for et in range(NE):
            tw = _dma_tr(p, "cL", woT[:, 512 * et:512 * (et + 1)],
                         scro[512 * n:512 * (n + 1), 128 * et:128 * (et + 1)])
        p.wait("tensor", "cL", tw)
        tevs = {}
        for r in range(NT):
            buf = xbuf[r % 2]
            tld = _dma(p, "sync", "yL", buf[:, 512 * n:512 * (n + 1)],
                       x0[128 * r:128 * (r + 1), 512 * n:512 * (n + 1)])
            pb = psm[r % 2]
            if r >= 2:
                p.on("tensor", lambda e, t=tevs[r - 2], h=hv: e.wait_ge(h, t))
            for et in range(NE):
                last = (et == NE - 1)
                def mmw(e, r=r, et=et, pb=pb, last=last):
                    rr = e.matmul(pb, oT[:, T * et + 128 * r:T * et + 128 * (r + 1)],
                                  woT[:, 512 * et:512 * (et + 1)], start=(et == 0), stop=last)
                    if last:
                        rr.then_inc(ht, 1)
                p.on("tensor", mmw)
            tm_ = tC.add(1)
            p.wait("vector", "tC", tm_)
            p.wait("vector", "yL", tld)
            p.on("vector", lambda e, pb=pb, buf=buf, n=n, h=hv: e.scalar_tensor_tensor(
                out=buf[:, 512 * n:512 * (n + 1)], in0=pb, scalar=1.0,
                in1=buf[:, 512 * n:512 * (n + 1)], op0=ALU.mult, op1=ALU.add).then_inc(h, 1))
            tev = vC.add(1)
            tevs[r] = tev
            p.wait("sync", "vC", tev)
            tst = _dma(p, "sync", "yS", xa[128 * r:128 * (r + 1), 512 * n:512 * (n + 1)],
                       buf[:, 512 * n:512 * (n + 1)])
        p.wait("tensor", "vC", tev)
    p.wait("sync", "yS", p.ctr["yS"].v)
    p.barrier()
    s1.close()

    s2 = contextlib.ExitStack()
    qf8 = [p.sb(f"qf8_{j}", [128, E], FP8, stack=s2) for j in range(2)]
    qfb = [p.sb(f"qfb_{j}", [128, E], BF16, stack=s2) for j in range(2)]
    qfTa = p.sb("qfTa", [128, NT * E], BF16, stack=s2)
    q2s = [p.sb(f"q2s{j}", [128, NE * 512], BF16, stack=s2) for j in range(2)]
    x2a = p.sb("x2a", [128, NT * E], F32, stack=s2)
    psm = [PSA[:, 0:512], PSA[:, 512:1024]]
    p.wait("gpsimd", "gS", t_q2done)
    ap_h.clear(); tr_h.clear(); ev_h.clear()
    vq_h = {}
    pstb_half = [PSTB[:, 0:1024], PSTB[:, 1024:2048]]
    for r in range(NT):
        par = r % 2
        xsl = x2a[:, E * r:E * (r + 1)]
        tld = _dma(p, "sync", "yL", xsl, xa[128 * r:128 * (r + 1), :])
        p.wait("vector", "yL", tld)
        p.wait("scalar", "yL", tld)
        if r >= 2:
            p.wait("vector", "sC", ap_h[r - 2])
            p.wait("scalar", "vC", vq_h[r - 2])
        ap = emit_ln2(p, xsl, xln2[par][:], sc2, par, scale_ap=scl_c[:])
        ap_h[r] = ap
        p.wait("vector", "sC", ap)
        if r >= 2:
            p.wait("vector", "tC", tr_h[r - 2])
        p.on("vector", lambda e, par=par, h=hv: e.tensor_copy(out=qf8[par][:], in_=xln2[par][:]).then_inc(h, 1))
        p.wait("vector", "vC", vC.add(1))
        p.on("vector", lambda e, par=par, h=hv: e.tensor_copy(out=qfb[par][:], in_=qf8[par][:]).then_inc(h, 1))
        tq_ = vC.add(1)
        vq_h[r] = tq_
        p.wait("tensor", "vC", tq_)
        for hb in range(2):
            ph = pstb_half[hb]
            if r >= 1:
                p.on("tensor", lambda e, t=ev_h[(r - 1, hb)], h=hv: e.wait_ge(h, t))
            for j in range(8):
                et = 8 * hb + j
                lastj = (j == 7)
                def tr1(e, et=et, j=j, par=par, ph=ph, lastj=lastj):
                    rr = e.transpose(ph[:, 128 * j:128 * (j + 1)],
                                     qfb[par][:, 128 * et:128 * (et + 1)], ident_b[:])
                    if lastj:
                        rr.then_inc(ht, 1)
                p.on("tensor", tr1)
            t = tC.add(1)
            p.wait("vector", "tC", t)
            dv = qfTa[:].rearrange("p (r t c) -> p (r t) c", r=NT, c=128)[:, NE * r + 8 * hb:NE * r + 8 * (hb + 1), :]
            p.on("vector", lambda e, dv=dv, ph=ph, h=hv: e.tensor_copy(
                out=dv, in_=ph.rearrange("p (t c) -> p t c", c=128)).then_inc(h, 1))
            ev_h[(r, hb)] = vC.add(1)
        tr_h[r] = t
    p.barrier()
    mm_hist = {}
    ev_hist = {}
    gi = 0
    for ns in range(4):
        bufw = q2s[ns % 2]
        if ns >= 2:
            p.wait("gpsimd", "tC", mm_hist[(ns - 2) * NT + NT - 1])
        tw = _dma(p, "gpsimd", "gL", bufw[:].rearrange("p (t n) -> p t n", n=512),
                  q2hr[:, :, 512 * ns:512 * (ns + 1)])
        p.wait("tensor", "gL", tw)
        for r in range(NT):
            pb = psm[gi % 2]
            if gi >= 2:
                p.on("tensor", lambda e, t=ev_hist[gi - 2], h=hv: e.wait_ge(h, t))
            for et in range(NE):
                last = (et == NE - 1)
                def mmf(e, r=r, et=et, pb=pb, bufw=bufw, last=last):
                    rr = e.matmul(pb, qfTa[:, E * r + 128 * et:E * r + 128 * (et + 1)],
                                  bufw[:, 512 * et:512 * (et + 1)], start=(et == 0), stop=last)
                    if last:
                        rr.then_inc(ht, 1)
                p.on("tensor", mmf)
            tm_ = tC.add(1)
            mm_hist[gi] = tm_
            p.wait("vector", "tC", tm_)
            p.on("vector", lambda e, pb=pb, r=r, ns=ns, h=hv: e.scalar_tensor_tensor(
                out=x2a[:, E * r + 512 * ns:E * r + 512 * (ns + 1)], in0=pb, scalar=dq_c[:],
                in1=x2a[:, E * r + 512 * ns:E * r + 512 * (ns + 1)],
                op0=ALU.mult, op1=ALU.add).then_inc(h, 1))
            ev_hist[gi] = vC.add(1)
            gi += 1
    p.wait("sync", "vC", p.ctr["vC"].v)
    for r in range(NT):
        _dma(p, "sync", "yS", x2[128 * r:128 * (r + 1), :], x2a[:, E * r:E * (r + 1)])
    p.wait("sync", "yS", p.ctr["yS"].v)
    p.barrier()
    s2.close()

    s3 = contextlib.ExitStack()
    xlnT2 = p.sb("xlnT2", [128, NE * T], BF16, stack=s3)
    wk1b = p.sb("wk1b", [128, NE * 128], BF16, stack=s3)
    wq1b = p.sb("wq1b", [128, NE * 128], BF16, stack=s3)
    wk2b = p.sb("wk2b", [128, E2], BF16, stack=s3)
    wvb = [p.sb(f"wvb{j}", [128, NE * 128], BF16, stack=s3) for j in range(2)]
    k1T = p.sb("k1T", [128, T], BF16, stack=s3)
    k1tmp = p.sb("k1tmp", [128, 128], BF16, stack=s3)
    q1sb = p.sb("q1sb", [128, D], F32, stack=s3)
    f_sb = p.sb("f_sb", [128, T], F32, stack=s3)
    a_sb = [p.sb(f"a_sb{j}", [128, T], F32, stack=s3) for j in range(2)]
    kg_sb = [p.sb(f"kg_sb{j}", [128, T], F32, stack=s3) for j in range(2)]
    v_f = p.sb("v_f", [128, T], F32, stack=s3)
    b_sb = p.sb("b_sb", [128, T], F32, stack=s3)
    ac_sb = [p.sb(f"ac_sb{j}", [128, T], BF16, stack=s3) for j in range(2)]
    hl_sb = [p.sb(f"hl_sb{j}", [128, T], BF16, stack=s3) for j in range(2)]
    pssm = PSB[:, 0:128]
    pstb1 = PSTB[:, 0:128]

    wk1r = wk1.rearrange("(t p) n -> p t n", p=128)
    wq1r = wq1.rearrange("(t p) n -> p t n", p=128)
    wvr = wv.rearrange("(t p) n -> p t n", p=128)
    _dma(p, "gpsimd", "gL", wk1b[:].rearrange("p (t n) -> p t n", n=128), wk1r)
    _dma(p, "gpsimd", "gL", wq1b[:].rearrange("p (t n) -> p t n", n=128), wq1r)
    twk = _dma(p, "gpsimd", "gL", wk2b[:], wk2)

    ap_h.clear(); tr_h.clear(); ev_h.clear()
    for r in range(NT):
        ln_tr_tile(r, x2[128 * r:128 * (r + 1), :],
                   lambda r, hb: xlnT2[:].rearrange("p (t c) -> p t c", c=T)[:, 8 * hb:8 * (hb + 1), 128 * r:128 * (r + 1)])
    p.barrier()

    p.wait("tensor", "gL", twk)
    for r in range(NT):
        for et in range(NE):
            last = (et == NE - 1)
            def mk1(e, r=r, et=et, last=last):
                rr = e.matmul(pssm, xlnT2[:, T * et + 128 * r:T * et + 128 * (r + 1)],
                              wk1b[:, 128 * et:128 * (et + 1)], start=(et == 0), stop=last)
                if last:
                    rr.then_inc(ht, 1)
            p.on("tensor", mk1)
        t1_ = tC.add(1)
        p.wait("vector", "tC", t1_)
        p.on("vector", lambda e, h=hv: e.tensor_copy(out=k1tmp[:], in_=pssm).then_inc(h, 1))
        tev = vC.add(1)
        p.wait("tensor", "vC", tev)
        p.on("tensor", lambda e, h=ht: e.transpose(pstb1, k1tmp[:], ident_b[:]).then_inc(h, 1))
        t2_ = tC.add(1)
        p.wait("vector", "tC", t2_)
        p.on("vector", lambda e, r=r, h=hv: e.tensor_copy(out=k1T[:, 128 * r:128 * (r + 1)],
                                                          in_=pstb1).then_inc(h, 1))
        tev2 = vC.add(1)
        p.on("tensor", lambda e, t=tev2, h=hv: e.wait_ge(h, t))
        for et in range(NE):
            last = (et == NE - 1)
            def mq1(e, r=r, et=et, last=last):
                rr = e.matmul(pssm, xlnT2[:, T * et + 128 * r:T * et + 128 * (r + 1)],
                              wq1b[:, 128 * et:128 * (et + 1)], start=(et == 0), stop=last)
                if last:
                    rr.then_inc(ht, 1)
            p.on("tensor", mq1)
        t3_ = tC.add(1)
        p.wait("vector", "tC", t3_)
        p.on("vector", lambda e, h=hv: e.tensor_copy(out=q1sb[:], in_=pssm).then_inc(h, 1))
        tev3 = vC.add(1)
        p.wait("sync", "vC", tev3)
        tst = _dma(p, "sync", "yS", q1o[128 * r:128 * (r + 1), :], q1sb[:])
        p.wait("sync", "yS", tst)
        p.on("tensor", lambda e, t=tev3, h=hv: e.wait_ge(h, t))
    p.barrier()

    psraw = PSB[:, 0:1024]
    psv1 = [PSA[:, 0:1024], PSA[:, 1024:2048]]
    raw_h, sigf_h, sig_h, vmm_h, vf_h, scan_h, st_h = {}, {}, {}, {}, {}, {}, {}
    for ch in range(NCH):
        bufw = wvb[ch % 2]
        if ch >= 2:
            p.wait("gpsimd", "tC", vmm_h[ch - 2])
        tw = _dma(p, "gpsimd", "gL", bufw[:].rearrange("p (t n) -> p t n", n=128),
                  wvr[:, :, 128 * ch:128 * (ch + 1)])
        if ch >= 1:
            p.wait("tensor", "sC", sigf_h[ch - 1])
        for half in range(2):
            lasth = (half == 1)
            def mraw(e, ch=ch, half=half, lasth=lasth):
                rr = e.matmul(psraw[:, 512 * half:512 * (half + 1)], wk2b[:, 128 * ch:128 * (ch + 1)],
                              k1T[:, 512 * half:512 * (half + 1)], start=True, stop=True)
                if lasth:
                    rr.then_inc(ht, 1)
            p.on("tensor", mraw)
        raw_h[ch] = tC.add(1)
        p.wait("tensor", "gL", tw)
        if ch >= 2:
            p.wait("tensor", "vC", vf_h[ch - 2])
        for half in range(2):
            for et in range(NE):
                last = (et == NE - 1)
                def mv2(e, ch=ch, et=et, half=half, bufw=bufw, last=last, lasth=(half == 1)):
                    rr = e.matmul(psv1[ch % 2][:, 512 * half:512 * (half + 1)],
                                  bufw[:, 128 * et:128 * (et + 1)],
                                  xlnT2[:, T * et + 512 * half:T * et + 512 * (half + 1)],
                                  start=(et == 0), stop=last)
                    if last and lasth:
                        rr.then_inc(ht, 1)
                p.on("tensor", mv2)
        vmm_h[ch] = tC.add(1)
        p.wait("scalar", "tC", raw_h[ch])
        if ch >= 2:
            p.wait("scalar", "vC", scan_h[ch - 2])
        p.on("scalar", lambda e, ch=ch, h=hs: e.activation(out=f_sb[:], in_=psraw,
                                                           func=AF.Sigmoid).then_inc(h, 1))
        sigf_h[ch] = sC.add(1)
        p.wait("scalar", "sC", sigf_h[ch])
        p.on("scalar", lambda e, ch=ch: e.activation(out=a_sb[ch % 2][:], in_=f_sb[:], func=AF.Sigmoid))
        p.on("scalar", lambda e, ch=ch, h=hs: e.activation(out=kg_sb[ch % 2][:], in_=f_sb[:],
                                                           func=AF.Sigmoid, scale=-1.0).then_inc(h, 1))
        sig_h[ch] = sC.add(1)
        p.wait("vector", "tC", vmm_h[ch])
        p.on("vector", lambda e, ch=ch, h=hv: e.tensor_copy(out=v_f[:], in_=psv1[ch % 2]).then_inc(h, 1))
        vf_h[ch] = vC.add(1)
        p.wait("vector", "vC", vf_h[ch])
        p.wait("vector", "sC", sig_h[ch])
        p.on("vector", lambda e, ch=ch, h=hv: e.tensor_mul(out=b_sb[:], in0=kg_sb[ch % 2][:],
                                                           in1=v_f[:]).then_inc(h, 1))
        tb_ = vC.add(1)
        p.wait("vector", "vC", tb_)
        if ch >= 2:
            p.wait("vector", "yS", st_h[ch - 2])
        p.on("vector", lambda e, ch=ch: e.tensor_tensor_scan(
            out=ac_sb[ch % 2][:], data0=a_sb[ch % 2][:], data1=a_sb[ch % 2][:],
            initial=1.0, op0=ALU.mult, op1=ALU.bypass))
        p.on("vector", lambda e, ch=ch, h=hv: e.tensor_tensor_scan(
            out=hl_sb[ch % 2][:], data0=a_sb[ch % 2][:], data1=b_sb[:],
            initial=0.0, op0=ALU.mult, op1=ALU.add).then_inc(h, 1))
        scan_h[ch] = vC.add(1)
        p.wait("sync", "vC", scan_h[ch])
        _dma(p, "sync", "yS", acum[128 * ch:128 * (ch + 1), :], ac_sb[ch % 2][:])
        st_h[ch] = _dma(p, "sync", "yS", hloc[128 * ch:128 * (ch + 1), :], hl_sb[ch % 2][:])

    p.wait("sync", "yS", p.ctr["yS"].v)
    p.wait("gpsimd", "gS", p.ctr["gS"].v)
    s3.close()

    p.emit()
    stack.close()
    return nc



def build_B():
    nc = bass.Bass("TRN2", target_bir_lowering=False)
    stack = contextlib.ExitStack()
    p = P(nc, stack)

    x2 = nc.dram_tensor("x2", [T, E], F32, kind="ExternalInput").ap()
    acum = nc.dram_tensor("acum", [E2, T], BF16, kind="ExternalInput").ap()
    hloc = nc.dram_tensor("hloc", [E2, T], BF16, kind="ExternalInput").ap()
    q1t = nc.dram_tensor("q1t", [D, T], F32, kind="ExternalInput").ap()
    predA = nc.dram_tensor("predA", [128, 96], F32, kind="ExternalInput").ap()
    predH = nc.dram_tensor("predH", [128, 96], F32, kind="ExternalInput").ap()
    wq2 = nc.dram_tensor("wq2", [D, E2], F32, kind="ExternalInput").ap()
    wo = nc.dram_tensor("wo", [E2, E], F32, kind="ExternalInput").ap()
    nwr = nc.dram_tensor("nwr", [128, NCH], F32, kind="ExternalInput").ap()
    wrr = nc.dram_tensor("wrr", [128, NE], F32, kind="ExternalInput").ap()
    brr = nc.dram_tensor("brr", [1, 1], F32, kind="ExternalInput").ap()
    identf = nc.dram_tensor("identf", [128, 128], F32, kind="ExternalInput").ap()

    x3 = nc.dram_tensor("x3", [T, E], F32, kind="ExternalOutput").ap()
    xlnT3o = nc.dram_tensor("xlnT3o", [128, NE * T], BF16, kind="ExternalOutput").ap()
    scso = nc.dram_tensor("scso", [1, T], F32, kind="ExternalOutput").ap()

    ident_f = p.sb("ident_f", [128, 128], F32)
    eps5 = p.sb("eps5", [128, 1], F32)
    xbuf = [p.sb(f"xbuf{i}", [128, E], F32) for i in range(2)]
    xln_f = p.sb("xln_f", [128, E], F32)
    xln_f2 = p.sb("xln_f2", [128, E], F32)
    sc2 = mk_sc2(p, "l0")
    sc = {
        "bn24": p.sb("bn24", [128, 24], F32),
        "mv": p.sb("mv", [128, 2], F32),
        "sd": p.sb("sd", [128, 1], F32),
        "rstd": p.sb("rstd", [128, 1], F32),
        "negmr": p.sb("negmr", [128, 1], F32),
        "eps": eps5,
    }
    PSA = p.ps("PSA", [128, 2048], F32)
    PSB = p.ps("PSB", [128, 1024], F32)

    hv, hs, ht = p.sems["vC"], p.sems["sC"], p.sems["tC"]
    vC, sC, tC = p.ctr["vC"], p.ctr["sC"], p.ctr["tC"]

    pA = p.sb("pA", [128, 96], F32)
    pH = p.sb("pH", [128, 96], F32)
    hin = p.sb("hin", [128, NCH], F32)
    q1f = p.sb("q1f", [128, T], F32)
    q1b = p.sb("q1b", [128, T], BF16)
    wq2b = p.sb("wq2b", [128, E2], BF16)
    nwc = p.sb("nwc", [128, NCH], F32)
    wrb = p.sb("wrb", [128, NE], BF16)
    brc = p.sb("brc", [1, 1], F32)
    onesb = p.sb("onesb", [128, 1], BF16)
    onesrow = p.sb("onesrow", [1, 128], BF16)
    acb = [p.sb(f"acb{i}", [128, T], BF16) for i in range(2)]
    hlb = [p.sb(f"hlb{i}", [128, T], BF16) for i in range(2)]
    h_sb = p.sb("h_sb", [128, T], BF16)
    qg_sb = p.sb("qg_sb", [128, T], BF16)
    o2_sb = p.sb("o2_sb", [128, T], BF16)
    out_sb = p.sb("out_sb", [128, NCH * T], BF16)
    ssqf = p.sb("ssqf", [1, T], F32)
    ssqb = p.sb("ssqb", [1, T], BF16)
    var_c = p.sb("var_c", [128, NT], F32)
    sd_c = p.sb("sd_c", [128, NT], F32)
    rstd_c = p.sb("rstd_c", [128, NT], F32)

    _dma(p, "sync", "yL", ident_f[:], identf)
    _dma(p, "sync", "yL", pA[:], predA)
    _dma(p, "sync", "yL", pH[:], predH)
    _dma(p, "sync", "yL", q1f[:], q1t)
    _dma(p, "sync", "yL", nwc[:], nwr)
    t_br = _dma(p, "sync", "yL", brc[:], brr)
    _dma(p, "gpsimd", "gL", wq2b[:], wq2)
    t_wr = _dma(p, "gpsimd", "gL", wrb[:], wrr)
    p.on("vector", lambda e: e.memset(eps5[:], 1e-5))
    p.on("vector", lambda e: e.memset(onesb[:], 1.0))
    p.on("vector", lambda e: e.memset(onesrow[:], 1.0))
    p.wait("vector", "yL", t_br)
    p.wait("vector", "gL", t_wr)
    p.wait("tensor", "yL", t_br)
    p.wait("tensor", "gL", t_wr)
    p.on("vector", lambda e, h=hv: e.tensor_copy(out=q1b[:], in_=q1f[:]).then_inc(h, 1))
    p.wait("vector", "vC", vC.add(1))
    p.on("vector", lambda e, h=hv: e.memset(hin[:], 0.0).then_inc(h, 1))
    p.wait("vector", "vC", vC.add(1))
    for j in range(3):
        p.on("vector", lambda e, j=j, h=hv: e.tensor_mul(out=hin[:], in0=pA[:, 32 * j:32 * (j + 1)],
                                                         in1=hin[:]).then_inc(h, 1))
        p.wait("vector", "vC", vC.add(1))
        p.on("vector", lambda e, j=j, h=hv: e.tensor_add(out=hin[:], in0=hin[:],
                                                         in1=pH[:, 32 * j:32 * (j + 1)]).then_inc(h, 1))
        p.wait("vector", "vC", vC.add(1))
    p.barrier()

    psg = PSA[:, 0:1024]
    psr1 = PSB[0:1, 0:512]
    psr2 = PSB[0:1, 512:1024]
    t_ldprev = 0
    for ch in range(NCH):
        ab = acb[ch % 2]
        hb = hlb[ch % 2]
        if ch >= 2:
            p.wait("sync", "vC", t_hdone)
        _dma(p, "sync", "yL", ab[:], acum[128 * ch:128 * (ch + 1), :])
        tld = _dma(p, "sync", "yL", hb[:], hloc[128 * ch:128 * (ch + 1), :])
        if ch > 0:
            p.wait("tensor", "sC", t_silu)
        for half in range(2):
            lasth = (half == 1)
            def mmg(e, ch=ch, half=half, lasth=lasth):
                rr = e.matmul(psg[:, 512 * half:512 * (half + 1)], wq2b[:, 128 * ch:128 * (ch + 1)],
                              q1b[:, 512 * half:512 * (half + 1)], start=True, stop=True)
                if lasth:
                    rr.then_inc(ht, 1)
            p.on("tensor", mmg)
        tqg = tC.add(1)
        if ch > 0:
            p.wait("scalar", "vC", t_out)
        p.wait("scalar", "tC", tqg)
        p.on("scalar", lambda e, h=hs: e.activation(out=qg_sb[:], in_=psg, func=AF.Silu).then_inc(h, 1))
        t_silu = sC.add(1)
        p.wait("vector", "yL", tld)
        p.on("vector", lambda e, ch=ch, ab=ab, hb=hb, h=hv: e.scalar_tensor_tensor(
            out=h_sb[:], in0=ab[:], scalar=hin[:, ch:ch + 1], in1=hb[:],
            op0=ALU.mult, op1=ALU.add).then_inc(h, 1))
        t_hdone = vC.add(1)
        p.wait("vector", "vC", t_hdone)
        p.wait("vector", "sC", t_silu)
        p.on("vector", lambda e, ch=ch, h=hv: e.tensor_mul(
            out=out_sb[:, T * ch:T * (ch + 1)], in0=qg_sb[:], in1=h_sb[:]).then_inc(h, 1))
        t_out = vC.add(1)
        p.wait("scalar", "vC", t_out)
        p.on("scalar", lambda e, ch=ch, h=hs: e.activation(
            out=o2_sb[:], in_=out_sb[:, T * ch:T * (ch + 1)], func=AF.Square).then_inc(h, 1))
        t_o2 = sC.add(1)
        p.wait("tensor", "sC", t_o2)
        for half in range(2):
            lasth = (half == 1)
            def mms(e, ch=ch, half=half, lasth=lasth):
                rr = e.matmul((psr1 if half == 0 else psr2), onesb[:],
                              o2_sb[:, 512 * half:512 * (half + 1)],
                              start=(ch == 0), stop=(ch == NCH - 1))
                if lasth:
                    rr.then_inc(ht, 1)
            p.on("tensor", mms)
        t_ssq = tC.add(1)
        p.on("scalar", lambda e, t=t_ssq, h=ht: e.wait_ge(h, t))
    p.barrier()

    p.on("vector", lambda e, h=hv: e.tensor_copy(out=ssqf[0:1, 0:512], in_=psr1).then_inc(h, 1))
    p.on("vector", lambda e, h=hv: e.tensor_copy(out=ssqf[0:1, 512:1024], in_=psr2).then_inc(h, 1))
    p.wait("vector", "vC", vC.add(2))
    p.on("vector", lambda e, h=hv: e.tensor_copy(out=ssqb[:], in_=ssqf[:]).then_inc(h, 1))
    tssb = vC.add(1)
    p.wait("tensor", "vC", tssb)
    pscol = PSB[:, 0:128]
    for r in range(NT):
        p.on("tensor", lambda e, r=r, h=ht: e.matmul(pscol[:, 0:1], ssqb[0:1, 128 * r:128 * (r + 1)],
                                                     onesb[0:1, 0:1], start=True, stop=True).then_inc(h, 1))
        tcol = tC.add(1)
        p.wait("vector", "tC", tcol)
        p.on("vector", lambda e, r=r, h=hv: e.tensor_scalar(
            out=var_c[:, r:r + 1], in0=pscol[:, 0:1], scalar1=float(1.0 / E2), scalar2=1e-6,
            op0=ALU.mult, op1=ALU.add).then_inc(h, 1))
        tvar = vC.add(1)
        p.on("tensor", lambda e, t=tvar, h=hv: e.wait_ge(h, t))
    p.wait("scalar", "vC", vC.v)
    p.on("scalar", lambda e, h=hs: e.activation(out=sd_c[:], in_=var_c[:], func=AF.Sqrt).then_inc(h, 1))
    p.wait("vector", "sC", sC.add(1))
    p.on("vector", lambda e, h=hv: e.reciprocal(out=rstd_c[:], in_=sd_c[:]).then_inc(h, 1))
    trst = vC.add(1)
    p.barrier()

    wos = p.sb("wos", [128, NCH * 512], BF16)
    wor = wo.rearrange("(t p) n -> p t n", p=128)
    psm = [PSA[:, 0:512], PSA[:, 512:1024]]
    for n in range(4):
        if n > 0:
            p.wait("gpsimd", "tC", tC.v)
        tw = _dma(p, "gpsimd", "gL", wos[:].rearrange("p (t n) -> p t n", n=512),
                  wor[:, :, 512 * n:512 * (n + 1)])
        p.wait("vector", "gL", tw)
        for c in range(NCH):
            lastc = (c == NCH - 1)
            def scw(e, c=c, lastc=lastc):
                rr = e.tensor_single_scalar(out=wos[:, 512 * c:512 * (c + 1)],
                                            in_=wos[:, 512 * c:512 * (c + 1)],
                                            scalar=nwc[:, c:c + 1], op=ALU.mult)
                if lastc:
                    rr.then_inc(hv, 1)
            p.on("vector", scw)
        tnw = vC.add(1)
        p.wait("tensor", "vC", tnw)
        tevs = {}
        for r in range(NT):
            buf = xbuf[r % 2]
            tld = _dma(p, "sync", "yL", buf[:, 512 * n:512 * (n + 1)],
                       x2[128 * r:128 * (r + 1), 512 * n:512 * (n + 1)])
            pb = psm[r % 2]
            if r >= 2:
                p.on("tensor", lambda e, t=tevs[r - 2], h=hv: e.wait_ge(h, t))
            for c in range(NCH):
                lastc = (c == NCH - 1)
                def mmo(e, r=r, c=c, pb=pb, lastc=lastc):
                    rr = e.matmul(pb, out_sb[:, T * c + 128 * r:T * c + 128 * (r + 1)],
                                  wos[:, 512 * c:512 * (c + 1)], start=(c == 0), stop=lastc)
                    if lastc:
                        rr.then_inc(ht, 1)
                p.on("tensor", mmo)
            tm_ = tC.add(1)
            p.wait("vector", "tC", tm_)
            p.wait("vector", "yL", tld)
            p.on("vector", lambda e, pb=pb, buf=buf, n=n, r=r, h=hv: e.scalar_tensor_tensor(
                out=buf[:, 512 * n:512 * (n + 1)], in0=pb, scalar=rstd_c[:, r:r + 1],
                in1=buf[:, 512 * n:512 * (n + 1)], op0=ALU.mult, op1=ALU.add).then_inc(h, 1))
            tev = vC.add(1)
            tevs[r] = tev
            p.wait("sync", "vC", tev)
            tst = _dma(p, "sync", "yS", x3[128 * r:128 * (r + 1), 512 * n:512 * (n + 1)],
                       buf[:, 512 * n:512 * (n + 1)])
        p.wait("tensor", "vC", tev)
    p.wait("sync", "yS", p.ctr["yS"].v)
    p.barrier()

    xlnT3 = p.sb("xlnT3", [128, NE * T], BF16)
    xln2 = [xln_f, xln_f2]
    p.on("vector", lambda e: e.memset(sc2["eps"][:], 1e-5))
    pst_half = [PSA[:, 0:1024], PSA[:, 1024:2048]]
    ap_h, tr_h, ev_h = {}, {}, {}
    for i in range(NT):
        par = i % 2
        buf = xbuf[par]
        if i >= 2:
            p.wait("sync", "sC", ap_h[i - 2])
        tld = _dma(p, "sync", "yL", buf[:], x3[128 * i:128 * (i + 1), :])
        p.wait("vector", "yL", tld)
        p.wait("scalar", "yL", tld)
        if i >= 2:
            p.wait("scalar", "tC", tr_h[i - 2])
        ap = emit_ln2(p, buf, xln2[par][:], sc2, par)
        ap_h[i] = ap
        p.wait("tensor", "sC", ap)
        for hb in range(2):
            ph = pst_half[hb]
            if i >= 1:
                p.on("tensor", lambda e, t=ev_h[(i - 1, hb)], h=hv: e.wait_ge(h, t))
            for j in range(8):
                et = 8 * hb + j
                lastj = (j == 7)
                def tr1(e, et=et, j=j, par=par, ph=ph, lastj=lastj):
                    rr = e.transpose(ph[:, 128 * j:128 * (j + 1)],
                                     xln2[par][:, 128 * et:128 * (et + 1)], ident_f[:])
                    if lastj:
                        rr.then_inc(ht, 1)
                p.on("tensor", tr1)
            t = tC.add(1)
            p.wait("vector", "tC", t)
            dv = xlnT3[:].rearrange("p (t c) -> p t c", c=T)[:, 8 * hb:8 * (hb + 1), 128 * i:128 * (i + 1)]
            p.on("vector", lambda e, dv=dv, ph=ph, h=hv: e.tensor_copy(
                out=dv, in_=ph.rearrange("p (t c) -> p t c", c=128)).then_inc(h, 1))
            ev_h[(i, hb)] = vC.add(1)
        tr_h[i] = t
    ev_last = ev_h[(NT - 1, 1)]
    p.wait("sync", "vC", ev_last)
    p.on("tensor", lambda e, t=ev_last, h=hv: e.wait_ge(h, t))
    tso = _dma(p, "sync", "yS", xlnT3o, xlnT3[:])
    for half in range(2):
        for et in range(NE):
            laste = (et == NE - 1)
            def mmsc(e, half=half, et=et, laste=laste):
                rr = e.matmul((psr1 if half == 0 else psr2), wrb[:, et:et + 1],
                              xlnT3[:, T * et + 512 * half:T * et + 512 * (half + 1)],
                              start=(et == 0), stop=laste)
                if laste and half == 1:
                    rr.then_inc(ht, 1)
            p.on("tensor", mmsc)
    tsc_ = tC.add(1)
    p.wait("vector", "tC", tsc_)
    p.on("vector", lambda e, h=hv: e.tensor_single_scalar(out=ssqf[0:1, 0:512], in_=psr1,
                                                          scalar=brc[0:1, 0:1], op=ALU.add).then_inc(h, 1))
    p.on("vector", lambda e, h=hv: e.tensor_single_scalar(out=ssqf[0:1, 512:1024], in_=psr2,
                                                          scalar=brc[0:1, 0:1], op=ALU.add).then_inc(h, 1))
    tsf = vC.add(2)
    p.wait("sync", "vC", tsf)
    _dma(p, "sync", "yS", scso, ssqf[:])
    p.wait("sync", "yS", p.ctr["yS"].v)
    p.emit()
    stack.close()
    return nc


def build_C():
    nc = bass.Bass("TRN2", target_bir_lowering=False)
    stack = contextlib.ExitStack()
    p = P(nc, stack)

    x3 = nc.dram_tensor("x3", [T, E], F32, kind="ExternalInput").ap()
    xlnT3 = nc.dram_tensor("xlnT3", [128, NE * T], BF16, kind="ExternalInput").ap()
    scall = nc.dram_tensor("scall", [1, 4096], F32, kind="ExternalInput").ap()
    msc = nc.dram_tensor("msc", [128, NT], F32, kind="ExternalInput").ap()
    w1 = nc.dram_tensor("w1", [E, DF], F32, kind="ExternalInput").ap()
    b1r = nc.dram_tensor("b1r", [128, NF], F32, kind="ExternalInput").ap()
    w2 = nc.dram_tensor("w2", [DF, E], F32, kind="ExternalInput").ap()
    b2r = nc.dram_tensor("b2r", [1, E], F32, kind="ExternalInput").ap()
    y = nc.dram_tensor("y", [T, E], F32, kind="ExternalOutput").ap()

    PSA = p.ps("PSA", [128, 2048], F32)
    PSB = p.ps("PSB", [128, 1024], F32)
    hv, hs, ht = p.sems["vC"], p.sems["sC"], p.sems["tC"]
    vC, sC, tC = p.ctr["vC"], p.ctr["sC"], p.ctr["tC"]

    xlt = p.sb("xlt", [128, NE * T], BF16)
    w1b = [p.sb(f"w1b{i}", [128, NE * 128], BF16) for i in range(2)]
    xbuf = [p.sb(f"xbuf{i}", [128, E], F32) for i in range(2)]
    tmp = p.sb("tmp", [128, 512], F32)
    b1c = p.sb("b1c", [128, NF], F32)
    b2b = p.sb("b2b", [1, E], BF16)
    b2f = p.sb("b2f", [1, E], F32)
    mscb = p.sb("mscb", [128, NT], F32)
    cnt = p.sb("cnt", [128, NT], F32)
    msel = p.sb("msel", [128, NT], F32)
    onesf = p.sb("onesf", [1, 128], F32)
    onesrow = p.sb("onesrow", [1, 128], BF16)
    sC1 = contextlib.ExitStack()
    scrow = p.sb("scrow", [1, 4096], F32, stack=sC1)
    J = p.sb("J", [128, 4096], F32, stack=sC1)
    junk = p.sb("junk", [128, 4096], F32, stack=sC1)

    tld0 = _dma(p, "sync", "yL", xlt[:], xlnT3)
    _dma(p, "sync", "yL", scrow[:], scall)
    _dma(p, "sync", "yL", mscb[:], msc)
    _dma(p, "sync", "yL", b1c[:], b1r)
    t_c0 = _dma(p, "sync", "yL", b2f[:], b2r)
    p.on("vector", lambda e: e.memset(onesf[:], 1.0))
    p.on("vector", lambda e: e.memset(onesrow[:], 1.0))
    p.wait("vector", "yL", t_c0)
    p.on("vector", lambda e, h=hv: e.tensor_copy(out=b2b[:], in_=b2f[:]).then_inc(h, 1))
    p.wait("vector", "vC", vC.add(1))

    psj = PSB[:, 0:512]
    for s8 in range(8):
        p.on("tensor", lambda e, s8=s8, h=ht: e.matmul(psj, onesf[:], scrow[0:1, 512 * s8:512 * (s8 + 1)],
                                                       start=True, stop=True).then_inc(h, 1))
        tj = tC.add(1)
        p.wait("vector", "tC", tj)
        p.on("vector", lambda e, s8=s8, h=hv: e.tensor_copy(out=J[:, 512 * s8:512 * (s8 + 1)],
                                                            in_=psj).then_inc(h, 1))
        tjc = vC.add(1)
        p.on("tensor", lambda e, t=tjc, h=hv: e.wait_ge(h, t))
    p.wait("vector", "vC", vC.v)
    for r in range(NT):
        p.on("vector", lambda e, r=r, h=hv: e.tensor_scalar(
            out=junk[:], in0=J[:], scalar1=mscb[:, r:r + 1], scalar2=0.0,
            op0=ALU.is_gt, op1=ALU.add, accum_out=cnt[:, r:r + 1]).then_inc(h, 1))
    p.wait("vector", "vC", vC.add(NT))
    p.on("vector", lambda e, h=hv: e.tensor_single_scalar(out=msel[:], in_=cnt[:], scalar=2048.0,
                                                          op=ALU.is_lt).then_inc(h, 1))
    tmsel = vC.add(1)
    p.barrier()
    sC1.close()

    h1g = p.sb("h1g", [128, 32 * T], BF16)
    oacc = p.sb("oacc", [128, NT * E], BF16)
    w2b = [p.sb(f"w2b{j}", [128, 32 * 256], BF16) for j in range(2)]
    w1r = w1.rearrange("(t p) f -> p t f", p=128)
    w2r = w2.rearrange("(t p) n -> p t n", p=128)
    psh = [PSA[:, 0:1024], PSA[:, 1024:2048]]
    psw = [PSB[:, 0:256], PSB[:, 512:768]]
    th_hist = {}
    gelu_hist = {}
    for fh in range(2):
        for ft in range(32):
            fg = 32 * fh + ft
            bufw = w1b[ft % 2]
            if fg >= 2:
                p.wait("gpsimd", "tC", th_hist[fg - 2])
            tw = _dma(p, "gpsimd", "gL", bufw[:].rearrange("p (t n) -> p t n", n=128),
                      w1r[:, :, 128 * fg:128 * (fg + 1)])
            p.wait("tensor", "gL", tw)
            if fg == 0:
                p.wait("tensor", "yL", tld0)
            if fg >= 2:
                p.wait("tensor", "sC", gelu_hist[fg - 2])
            ph = psh[fg % 2]
            for half in range(2):
                for et in range(NE):
                    laste = (et == NE - 1)
                    def mmh(e, half=half, et=et, bufw=bufw, ph=ph, laste=laste, lasth=(half == 1)):
                        rr = e.matmul(ph[:, 512 * half:512 * (half + 1)],
                                      bufw[:, 128 * et:128 * (et + 1)],
                                      xlt[:, T * et + 512 * half:T * et + 512 * (half + 1)],
                                      start=(et == 0), stop=laste)
                        if laste and lasth:
                            rr.then_inc(ht, 1)
                    p.on("tensor", mmh)
            th_ = tC.add(1)
            th_hist[fg] = th_
            p.wait("scalar", "tC", th_)
            p.on("scalar", lambda e, ft=ft, fg=fg, ph=ph, h=hs: e.activation(
                out=h1g[:, T * ft:T * (ft + 1)], in_=ph, func=AF.Gelu_apprx_tanh,
                bias=b1c[:, fg:fg + 1]).then_inc(h, 1))
            gelu_hist[fg] = sC.add(1)
        t_gelu_all = gelu_hist[32 * fh + 31]
        w2mm_hist = {}
        ev_hist = {}
        gi = 0
        for n in range(8):
            bufw2 = w2b[n % 2]
            if n >= 2:
                p.wait("gpsimd", "tC", w2mm_hist[n - 2])
            tw2 = _dma(p, "gpsimd", "gL", bufw2[:].rearrange("p (t n) -> p t n", n=256),
                       w2r[:, 32 * fh:32 * (fh + 1), 256 * n:256 * (n + 1)])
            p.wait("tensor", "gL", tw2)
            if n == 0:
                p.wait("tensor", "sC", t_gelu_all)
            for r in range(NT):
                pb = psw[gi % 2]
                if fh == 1:
                    buf = xbuf[r % 2]
                    tld = _dma(p, "sync", "yL", buf[:, 256 * n:256 * (n + 1)],
                               x3[128 * r:128 * (r + 1), 256 * n:256 * (n + 1)])
                if gi >= 2:
                    p.on("tensor", lambda e, t=ev_hist[gi - 2], h=hv: e.wait_ge(h, t))
                for c in range(32):
                    lastc = (c == 31)
                    stop = lastc and (fh == 0)
                    def mmw2(e, r=r, c=c, pb=pb, bufw2=bufw2, lastc=lastc, stop=stop, fh=fh):
                        rr = e.matmul(pb, h1g[:, T * c + 128 * r:T * c + 128 * (r + 1)],
                                      bufw2[:, 256 * c:256 * (c + 1)], start=(c == 0), stop=stop)
                        if lastc and fh == 0:
                            rr.then_inc(ht, 1)
                    p.on("tensor", mmw2)
                if fh == 1:
                    def mmb2(e, pb=pb, n=n, h=ht):
                        e.matmul(pb, onesrow[:], b2b[0:1, 256 * n:256 * (n + 1)],
                                 start=False, stop=True).then_inc(h, 1)
                    p.on("tensor", mmb2)
                tm_ = tC.add(1)
                w2mm_hist[n] = tm_
                p.wait("vector", "tC", tm_)
                if fh == 0:
                    p.on("vector", lambda e, r=r, n=n, pb=pb, h=hv: e.tensor_copy(
                        out=oacc[:, E * r + 256 * n:E * r + 256 * (n + 1)], in_=pb).then_inc(h, 1))
                    tev = vC.add(1)
                else:
                    p.wait("vector", "yL", tld)
                    p.on("vector", lambda e, r=r, n=n, pb=pb, h=hv: e.tensor_tensor(
                        out=tmp[:, 0:256], in0=pb, in1=oacc[:, E * r + 256 * n:E * r + 256 * (n + 1)],
                        op=ALU.add).then_inc(h, 1))
                    p.wait("vector", "vC", vC.add(1))
                    p.on("vector", lambda e, r=r, n=n, buf=buf, h=hv: e.scalar_tensor_tensor(
                        out=buf[:, 256 * n:256 * (n + 1)], in0=tmp[:, 0:256], scalar=msel[:, r:r + 1],
                        in1=buf[:, 256 * n:256 * (n + 1)], op0=ALU.mult, op1=ALU.add).then_inc(h, 1))
                    tev = vC.add(1)
                    p.wait("sync", "vC", tev)
                    tst = _dma(p, "sync", "yS", y[128 * r:128 * (r + 1), 256 * n:256 * (n + 1)],
                               buf[:, 256 * n:256 * (n + 1)])
                ev_hist[gi] = tev
                gi += 1
    p.wait("sync", "yS", p.ctr["yS"].v)
    p.emit()
    stack.close()
    return nc


_CACHE = {}


def _consts():
    identf = np.eye(128, dtype=np.float32)
    identb = np.eye(128, dtype=BF)
    return identf, identb, None, None


def _amask(is_start):
    am = np.zeros((40, 128, 128), dtype=np.float32)
    jj, ii = np.mgrid[0:128, 0:128]
    for r in range(8):
        am[r * 5 + 0][jj < ii] = NEG
        am[r * 5 + 4][jj > ii] = NEG
        if is_start:
            for m in range(5):
                if m < 4 - r:
                    am[r * 5 + m][:, :] = NEG
    return np.ascontiguousarray(am.astype(BF))


def run_A(X, inputs):
    identf, identb, _, _ = _consts()
    am_st, am_n = _amask(True), _amask(False)
    if "A" not in _CACHE:
        _CACHE["A"] = build_A()
    sclb = np.full((128, 1), np.float32(inputs["L0_ffn_scale"]), np.float32)
    scl1b = np.full((128, 1), np.float32(inputs["L0_ffn_scale1"]), np.float32)
    in_maps = []
    for c in range(8):
        b, k = c // 4, c % 4
        x0 = np.ascontiguousarray(X[b, T * k:T * (k + 1)])
        xhv = X[b, T * k - HALO:T * k] if k > 0 else np.zeros((HALO, E), np.float32)
        in_maps.append({
            "x0": x0, "xh": np.ascontiguousarray(xhv),
            "wqkv": np.ascontiguousarray(np.asarray(inputs["L0_attn_w_qkv"], np.float32)),
            "wout": np.ascontiguousarray(np.asarray(inputs["L0_attn_w_out"], np.float32)),
            "in2": np.ascontiguousarray(np.asarray(inputs["L0_ffn_input2"], np.float32)),
            "wk1": np.ascontiguousarray(np.asarray(inputs["L1_attn_Wk1"], np.float32)),
            "wq1": np.ascontiguousarray(np.asarray(inputs["L1_attn_Wq1"], np.float32)),
            "wk2": np.ascontiguousarray(np.asarray(inputs["L1_attn_Wk2"], np.float32)),
            "wv": np.ascontiguousarray(np.asarray(inputs["L1_attn_Wv"], np.float32)),
            "identf": identf, "identb": identb, "amask": am_st if k == 0 else am_n,
            "sclb": sclb, "scl1b": scl1b,
        })
    res = run_bass_kernel_spmd(_CACHE["A"], in_maps, core_ids=list(range(8)))
    return res.results


def run_B(outA, inputs):
    if "B" not in _CACHE:
        _CACHE["B"] = build_B()
    identf, _, _, _ = _consts()
    wq2 = np.ascontiguousarray(np.asarray(inputs["L1_attn_Wq2"], np.float32))
    wo = np.ascontiguousarray(np.asarray(inputs["L1_attn_Wo"], np.float32))
    nwr = np.ascontiguousarray(np.asarray(inputs["L1_attn_norm_w"], np.float32).reshape(NCH, 128).T)
    wrr = np.ascontiguousarray(np.asarray(inputs["L1_ffn_w_r"], np.float32).reshape(NE, 128).T)
    brr = np.full((1, 1), np.float32(inputs["L1_ffn_b_r"]), np.float32)
    in_maps = []
    for c in range(8):
        b, k = c // 4, c % 4
        predA = np.zeros((128, 96), np.float32)
        predH = np.zeros((128, 96), np.float32)
        predA[:, 0:32 * (3 - k)] = 1.0
        for idx, j in enumerate(range(k)):
            col = 32 * (3 - k + idx)
            aj = np.asarray(outA[4 * b + j]["acum"])[:, T - 1].astype(np.float32)
            hj = np.asarray(outA[4 * b + j]["hloc"])[:, T - 1].astype(np.float32)
            predA[:, col:col + 32] = aj.reshape(NCH, 128).T
            predH[:, col:col + 32] = hj.reshape(NCH, 128).T
        in_maps.append({
            "x2": np.ascontiguousarray(outA[c]["x2"]),
            "acum": np.ascontiguousarray(outA[c]["acum"]),
            "hloc": np.ascontiguousarray(outA[c]["hloc"]),
            "q1t": np.ascontiguousarray(outA[c]["q1o"].T),
            "predA": predA, "predH": predH,
            "wq2": wq2, "wo": wo, "nwr": nwr, "wrr": wrr, "brr": brr,
            "identf": identf,
        })
    res = run_bass_kernel_spmd(_CACHE["B"], in_maps, core_ids=list(range(8)))
    return res.results


def run_C(outB, inputs):
    if "C" not in _CACHE:
        _CACHE["C"] = build_C()
    w1 = np.ascontiguousarray(np.asarray(inputs["L1_ffn_w1"], np.float32))
    w2 = np.ascontiguousarray(np.asarray(inputs["L1_ffn_w2"], np.float32))
    b1r = np.ascontiguousarray(np.asarray(inputs["L1_ffn_b1"], np.float32).reshape(NF, 128).T)
    b2r = np.ascontiguousarray(np.asarray(inputs["L1_ffn_b2"], np.float32).reshape(1, E))
    in_maps = []
    for c in range(8):
        b, k = c // 4, c % 4
        scall = np.concatenate([outB[4 * b + j]["scso"][0] for j in range(4)]).reshape(1, 4096)
        msc = np.ascontiguousarray(outB[c]["scso"][0].reshape(NT, 128).T)
        in_maps.append({
            "x3": np.ascontiguousarray(outB[c]["x3"]),
            "xlnT3": np.ascontiguousarray(outB[c]["xlnT3o"]),
            "scall": np.ascontiguousarray(scall), "msc": msc,
            "w1": w1, "b1r": b1r, "w2": w2, "b2r": b2r,
        })
    res = run_bass_kernel_spmd(_CACHE["C"], in_maps, core_ids=list(range(8)))
    return res.results


def kernel(L0_attn_x, L0_attn_w_qkv, L0_attn_w_out, L0_ffn_input2, L0_ffn_scale,
           L0_ffn_scale1, L1_attn_Wq1, L1_attn_Wq2, L1_attn_Wk1, L1_attn_Wk2,
           L1_attn_Wv, L1_attn_Wo, L1_attn_norm_w, L1_ffn_w_r, L1_ffn_b_r,
           L1_ffn_w1, L1_ffn_b1, L1_ffn_w2, L1_ffn_b2):
    inputs = dict(L0_attn_w_qkv=L0_attn_w_qkv, L0_attn_w_out=L0_attn_w_out,
                  L0_ffn_input2=L0_ffn_input2, L0_ffn_scale=L0_ffn_scale,
                  L0_ffn_scale1=L0_ffn_scale1, L1_attn_Wq1=L1_attn_Wq1,
                  L1_attn_Wq2=L1_attn_Wq2, L1_attn_Wk1=L1_attn_Wk1,
                  L1_attn_Wk2=L1_attn_Wk2, L1_attn_Wv=L1_attn_Wv,
                  L1_attn_Wo=L1_attn_Wo, L1_attn_norm_w=L1_attn_norm_w,
                  L1_ffn_w_r=L1_ffn_w_r, L1_ffn_b_r=L1_ffn_b_r,
                  L1_ffn_w1=L1_ffn_w1, L1_ffn_b1=L1_ffn_b1,
                  L1_ffn_w2=L1_ffn_w2, L1_ffn_b2=L1_ffn_b2)
    X = np.ascontiguousarray(np.asarray(L0_attn_x, dtype=np.float32))
    B, S, _ = X.shape
    outA = run_A(X, inputs)
    outB = run_B(outA, inputs)
    outC = run_C(outB, inputs)
    out = np.empty((B, S, E), np.float32)
    for c in range(8):
        b, k = c // 4, c % 4
        out[b, T * k:T * (k + 1)] = outC[c]["y"]
    return out


# version 07
# speedup vs baseline: 1.0522x; 1.0419x over previous
import contextlib
import numpy as np
import ml_dtypes
import concourse.bass as bass
import concourse.mybir as mybir
from concourse.bass_utils import run_bass_kernel_spmd

F32 = mybir.dt.float32
BF16 = mybir.dt.bfloat16
FP8 = mybir.dt.float8e4
AF = mybir.ActivationFunctionType
ALU = mybir.AluOpType
BF = ml_dtypes.bfloat16

T = 1024
HALO = 512
E = 2048
E2 = 4096
DF = 8192
NH = 16
D = 128
NE = E // 128
NT = T // 128
NTH = (T + HALO) // 128
NCH = E2 // 128
NF = DF // 128
TH = NTH * 128
SCALE = float(1.0 / np.sqrt(128.0))
NEG = -1.0e30

ENGS = ("sync", "gpsimd", "vector", "scalar", "tensor")


class Ctr:
    __slots__ = ("v",)
    def __init__(self):
        self.v = 0
    def add(self, n=1):
        self.v += n
        return self.v


class P:
    def __init__(self, nc, stack):
        self.nc = nc
        self.stack = stack
        self.q = {k: [] for k in ENGS}
        self.sems = {}
        self.ctr = {}
        self.nbar = 0
        for name in ("bar", "yL", "gL", "cL", "vC", "sC", "tC", "yS", "gS", "gQ", "cM"):
            self.sems[name] = stack.enter_context(nc.semaphore(name))
            self.ctr[name] = Ctr()

    def on(self, eng, fn):
        self.q[eng].append(fn)

    def wait(self, eng, sem, val):
        h = self.sems[sem]
        self.on(eng, lambda e, h=h, v=val: e.wait_ge(h, v))

    def barrier(self):
        self.nbar += 1
        tgt = 5 * self.nbar
        h = self.sems["bar"]
        for eng in ENGS:
            self.on(eng, lambda e, h=h: e.sem_inc(h, 1))
            self.on(eng, lambda e, h=h, t=tgt: e.wait_ge(h, t))

    def sb(self, name, shape, dtype, stack=None):
        return (stack or self.stack).enter_context(self.nc.sbuf_tensor(name, shape, dtype))

    def ps(self, name, shape, dtype=F32, stack=None):
        return (stack or self.stack).enter_context(self.nc.psum_tensor(name, shape, dtype))

    def emit(self):
        nc = self.nc
        with nc.Block() as block:
            @block.sync
            def _(e):
                for f in self.q["sync"]:
                    f(e)
            @block.gpsimd
            def _(e):
                for f in self.q["gpsimd"]:
                    f(e)
            @block.vector
            def _(e):
                for f in self.q["vector"]:
                    f(e)
            @block.scalar
            def _(e):
                for f in self.q["scalar"]:
                    f(e)
            @block.tensor
            def _(e):
                for f in self.q["tensor"]:
                    f(e)


def _dma(p, eng, sem, out, in_, **kw):
    h = p.sems[sem]
    p.on(eng, lambda e, o=out, i=in_, h=h, kw=kw: e.dma_start(out=o, in_=i, **kw).then_inc(h, 16))
    return p.ctr[sem].add(16)


def _dma_tr(p, sem, out, in_):
    h = p.sems[sem]
    p.on("scalar", lambda e, o=out, i=in_, h=h: e.dma_start(out=o, in_=i, transpose=True).then_inc(h, 16))
    return p.ctr[sem].add(16)



def emit_ln2(p, xbuf, dst, sc2, par, scale_ap=None):
    hv, hs = p.sems["vC"], p.sems["sC"]
    vC, sC = p.ctr["vC"], p.ctr["sC"]
    bn24, mv, negm, u, rstd = (sc2[k][par] for k in ("bn24", "mv", "negm", "u", "rstd"))
    for c in range(4):
        if c < 3:
            p.on("vector", lambda e, c=c, bn24=bn24, xbuf=xbuf: e.bn_stats(
                out=bn24[:, 6 * c:6 * (c + 1)], in_=xbuf[:, 512 * c:512 * (c + 1)]))
        else:
            p.on("vector", lambda e, c=c, bn24=bn24, xbuf=xbuf, h=hv: e.bn_stats(
                out=bn24[:, 6 * c:6 * (c + 1)], in_=xbuf[:, 512 * c:512 * (c + 1)]).then_inc(h, 1))
    p.wait("vector", "vC", vC.add(1))
    p.on("vector", lambda e, bn24=bn24, mv=mv, h=hv: e.bn_aggr(out=mv[:], in_=bn24[:]).then_inc(h, 1))
    p.wait("vector", "vC", vC.add(1))
    p.on("vector", lambda e, mv=mv, negm=negm, h=hv: e.tensor_single_scalar(
        out=negm[:], in_=mv[:, 0:1], scalar=-1.0, op=ALU.mult).then_inc(h, 1))
    tv = vC.add(1)
    p.wait("scalar", "vC", tv)
    p.on("scalar", lambda e, mv=mv, u=u, h=hs: e.activation(
        out=u[:], in_=mv[:, 1:2], func=AF.Ln, bias=sc2["eps"][:]).then_inc(h, 1))
    p.wait("scalar", "sC", p.ctr["sC"].add(1))
    p.on("scalar", lambda e, u=u, rstd=rstd, h=hs: e.activation(
        out=rstd[:], in_=u[:], func=AF.Exp, scale=-0.5).then_inc(h, 1))
    p.wait("scalar", "sC", p.ctr["sC"].add(1))
    if scale_ap is not None:
        p.on("scalar", lambda e, rstd=rstd, m=scale_ap, h=hs: e.activation(
            out=rstd[:], in_=rstd[:], func=AF.Identity, scale=m).then_inc(h, 1))
        p.wait("scalar", "sC", p.ctr["sC"].add(1))
    p.on("scalar", lambda e, xbuf=xbuf, d=dst, negm=negm, h=hs: e.activation(
        out=d, in_=xbuf[:], func=AF.Identity, bias=negm[:]).then_inc(h, 1))
    p.wait("scalar", "sC", p.ctr["sC"].add(1))
    p.on("scalar", lambda e, d=dst, rstd=rstd, h=hs: e.activation(
        out=d, in_=d, func=AF.Identity, scale=rstd[:]).then_inc(h, 1))
    return p.ctr["sC"].add(1)


def mk_sc2(p, name, stack=None):
    return {
        "bn24": [p.sb(f"{name}bn{j}", [128, 24], F32, stack=stack) for j in range(2)],
        "mv": [p.sb(f"{name}mv{j}", [128, 2], F32, stack=stack) for j in range(2)],
        "negm": [p.sb(f"{name}nm{j}", [128, 1], F32, stack=stack) for j in range(2)],
        "u": [p.sb(f"{name}u{j}", [128, 1], F32, stack=stack) for j in range(2)],
        "rstd": [p.sb(f"{name}rs{j}", [128, 1], F32, stack=stack) for j in range(2)],
        "eps": p.sb(f"{name}eps", [128, 1], F32, stack=stack),
    }

def emit_ln(p, xbuf, dst, sc, scale_ap=None):
    hv, hs = p.sems["vC"], p.sems["sC"]
    vC, sC = p.ctr["vC"], p.ctr["sC"]
    for c in range(4):
        if c < 3:
            p.on("vector", lambda e, c=c: e.bn_stats(out=sc["bn24"][:, 6 * c:6 * (c + 1)],
                                                     in_=xbuf[:, 512 * c:512 * (c + 1)]))
        else:
            p.on("vector", lambda e, c=c, h=hv: e.bn_stats(out=sc["bn24"][:, 6 * c:6 * (c + 1)],
                                                           in_=xbuf[:, 512 * c:512 * (c + 1)]).then_inc(h, 1))
    t0 = vC.add(1)
    p.wait("vector", "vC", t0)
    p.on("vector", lambda e, h=hv: e.bn_aggr(out=sc["mv"][:], in_=sc["bn24"][:]).then_inc(h, 1))
    t1 = vC.add(1)
    p.wait("scalar", "vC", t1)
    p.on("scalar", lambda e, h=hs: e.activation(out=sc["sd"][:], in_=sc["mv"][:, 1:2],
                                                func=AF.Sqrt, bias=sc["eps"][:]).then_inc(h, 1))
    t2 = sC.add(1)
    p.wait("vector", "sC", t2)
    p.on("vector", lambda e, h=hv: e.reciprocal(out=sc["rstd"][:], in_=sc["sd"][:]).then_inc(h, 1))
    tr0 = vC.add(1)
    p.wait("vector", "vC", tr0)
    if scale_ap is not None:
        p.on("vector", lambda e, m=scale_ap, h=hv: e.tensor_single_scalar(
            out=sc["rstd"][:], in_=sc["rstd"][:], scalar=m, op=ALU.mult).then_inc(h, 1))
        tr1 = vC.add(1)
        p.wait("vector", "vC", tr1)
    p.on("vector", lambda e, h=hv: e.tensor_scalar(out=sc["negmr"][:], in0=sc["mv"][:, 0:1],
                                                   scalar1=sc["rstd"][:], scalar2=-1.0,
                                                   op0=ALU.mult, op1=ALU.mult).then_inc(h, 1))
    t3 = vC.add(1)
    p.wait("scalar", "vC", t3)
    p.on("scalar", lambda e, d=dst, h=hs: e.activation(out=d, in_=xbuf[:], func=AF.Identity,
                                                       scale=sc["rstd"][:], bias=sc["negmr"][:]).then_inc(h, 1))
    return sC.add(1), t3


def emit_transpose_tiles(p, src, dst_view, pst, ident, ncols, wait_s=None):
    tC, vC = p.ctr["tC"], p.ctr["vC"]
    ht, hv = p.sems["tC"], p.sems["vC"]
    if wait_s is not None:
        p.wait("tensor", wait_s[0], wait_s[1])
    for j in range(ncols):
        if j < ncols - 1:
            p.on("tensor", lambda e, j=j: e.transpose(pst[:, 128 * j:128 * (j + 1)],
                                                      src[:, 128 * j:128 * (j + 1)], ident[:]))
        else:
            p.on("tensor", lambda e, j=j, h=ht: e.transpose(pst[:, 128 * j:128 * (j + 1)],
                                                            src[:, 128 * j:128 * (j + 1)], ident[:]).then_inc(h, 1))
    t = tC.add(1)
    p.wait("vector", "tC", t)
    p.on("vector", lambda e, h=hv: e.tensor_copy(
        out=dst_view, in_=pst[:, 0:128 * ncols].rearrange("p (t c) -> p t c", c=128)).then_inc(h, 1))
    return vC.add(1), t


def build_A():
    nc = bass.Bass("TRN2", target_bir_lowering=False)
    stack = contextlib.ExitStack()
    p = P(nc, stack)

    x0 = nc.dram_tensor("x0", [T, E], F32, kind="ExternalInput").ap()
    xh = nc.dram_tensor("xh", [HALO, E], F32, kind="ExternalInput").ap()
    wqkv = nc.dram_tensor("wqkv", [3 * E, E], F32, kind="ExternalInput").ap()
    wout = nc.dram_tensor("wout", [E, E], F32, kind="ExternalInput").ap()
    in2 = nc.dram_tensor("in2", [E, E], F32, kind="ExternalInput").ap()
    wk1 = nc.dram_tensor("wk1", [E, D], F32, kind="ExternalInput").ap()
    wq1 = nc.dram_tensor("wq1", [E, D], F32, kind="ExternalInput").ap()
    wk2 = nc.dram_tensor("wk2", [D, E2], F32, kind="ExternalInput").ap()
    wv = nc.dram_tensor("wv", [E, E2], F32, kind="ExternalInput").ap()
    identf = nc.dram_tensor("identf", [128, 128], F32, kind="ExternalInput").ap()
    identb = nc.dram_tensor("identb", [128, 128], BF16, kind="ExternalInput").ap()
    amask = nc.dram_tensor("amask", [40, 128, 128], BF16, kind="ExternalInput").ap()
    sclb = nc.dram_tensor("sclb", [128, 1], F32, kind="ExternalInput").ap()
    scl1b = nc.dram_tensor("scl1b", [128, 1], F32, kind="ExternalInput").ap()

    x2 = nc.dram_tensor("x2", [T, E], F32, kind="ExternalOutput").ap()
    acum = nc.dram_tensor("acum", [E2, T], BF16, kind="ExternalOutput").ap()
    hloc = nc.dram_tensor("hloc", [E2, T], BF16, kind="ExternalOutput").ap()
    q1o = nc.dram_tensor("q1o", [T, D], F32, kind="ExternalOutput").ap()

    scrq = nc.dram_tensor("scrq", [3 * E, E], BF16, kind="Internal").ap()
    scro = nc.dram_tensor("scro", [E, E], BF16, kind="Internal").ap()
    q2h = nc.dram_tensor("q2h", [E, E], BF16, kind="Internal").ap()
    xa = nc.dram_tensor("xa", [T, E], F32, kind="Internal").ap()

    ident_f = p.sb("ident_f", [128, 128], F32)
    ident_b = p.sb("ident_b", [128, 128], BF16)
    eps5 = p.sb("eps5", [128, 1], F32)
    scl_c = p.sb("scl_c", [128, 1], F32)
    scl1_c = p.sb("scl1_c", [128, 1], F32)
    dq_c = p.sb("dq_c", [128, 1], F32)
    xbuf = [p.sb(f"xbuf{i}", [128, E], F32) for i in range(2)]
    xln_f = p.sb("xln_f", [128, E], F32)
    xln_f2 = p.sb("xln_f2", [128, E], F32)
    sc2 = mk_sc2(p, "l0")
    sc = {
        "bn24": p.sb("bn24", [128, 24], F32),
        "mv": p.sb("mv", [128, 2], F32),
        "sd": p.sb("sd", [128, 1], F32),
        "rstd": p.sb("rstd", [128, 1], F32),
        "negmr": p.sb("negmr", [128, 1], F32),
        "eps": eps5,
    }
    PSA = p.ps("PSA", [128, 2048], F32)
    PSB = p.ps("PSB", [128, 1024], F32)
    PSTB = p.ps("PSTB", [128, 2048], BF16)

    vC, sC, tC = p.ctr["vC"], p.ctr["sC"], p.ctr["tC"]
    hv, hs, ht = p.sems["vC"], p.sems["sC"], p.sems["tC"]

    _dma(p, "sync", "yL", ident_f[:], identf)
    _dma(p, "sync", "yL", ident_b[:], identb)
    _dma(p, "sync", "yL", scl_c[:], sclb)
    t_consts = _dma(p, "sync", "yL", scl1_c[:], scl1b)
    p.wait("tensor", "yL", t_consts)
    p.wait("scalar", "yL", t_consts)
    p.on("vector", lambda e: e.memset(eps5[:], 1e-5))
    p.wait("vector", "yL", t_consts)
    p.on("vector", lambda e, h=hv: e.tensor_mul(out=dq_c[:], in0=scl_c[:], in1=scl1_c[:]).then_inc(h, 1))
    p.wait("vector", "vC", p.ctr["vC"].add(1))
    p.on("vector", lambda e, h=hv: e.reciprocal(out=dq_c[:], in_=dq_c[:]).then_inc(h, 1))
    p.wait("vector", "vC", p.ctr["vC"].add(1))

    t_chunk = {}
    in2r = in2.rearrange("(t p) n -> p t n", p=128)
    q2hr = q2h.rearrange("(t p) n -> p t n", p=128)

    s1 = contextlib.ExitStack()
    xlnT = p.sb("xlnT", [128, NE * TH], BF16, stack=s1)

    xln2 = [xln_f, xln_f2]
    p.on("vector", lambda e: e.memset(sc2["eps"][:], 1e-5))
    pst_half = [PSA[:, 0:1024], PSA[:, 1024:2048]]
    ap_h, tr_h, ev_h = {}, {}, {}
    def ln_tr_tile(i, srcap, dstv_fn, n_et=NE):
        par = i % 2
        buf = xbuf[par]
        if i >= 2:
            p.wait("sync", "sC", ap_h[i - 2])
        tld = _dma(p, "sync", "yL", buf[:], srcap)
        p.wait("vector", "yL", tld)
        p.wait("scalar", "yL", tld)
        if i >= 2:
            p.wait("scalar", "tC", tr_h[i - 2])
        ap = emit_ln2(p, buf, xln2[par][:], sc2, par)
        ap_h[i] = ap
        p.wait("tensor", "sC", ap)
        for hb in range(2):
            ph = pst_half[hb]
            if i >= 1:
                p.on("tensor", lambda e, t=ev_h[(i - 1, hb)], h=hv: e.wait_ge(h, t))
            for j in range(8):
                et = 8 * hb + j
                lastj = (j == 7)
                def tr1(e, et=et, j=j, par=par, ph=ph, lastj=lastj):
                    rr = e.transpose(ph[:, 128 * j:128 * (j + 1)],
                                     xln2[par][:, 128 * et:128 * (et + 1)], ident_f[:])
                    if lastj:
                        rr.then_inc(ht, 1)
                p.on("tensor", tr1)
            t = tC.add(1)
            p.wait("vector", "tC", t)
            p.on("vector", lambda e, i=i, hb=hb, ph=ph, dv=dstv_fn(i, hb), h=hv: e.tensor_copy(
                out=dv, in_=ph.rearrange("p (t c) -> p t c", c=128)).then_inc(h, 1))
            ev_h[(i, hb)] = vC.add(1)
        tr_h[i] = t
    for i in range(NTH):
        srcap = xh[128 * i:128 * (i + 1), :] if i < 4 else x0[128 * (i - 4):128 * (i - 3), :]
        ln_tr_tile(i, srcap,
                   lambda i, hb: xlnT[:].rearrange("p (t c) -> p t c", c=TH)[:, 8 * hb:8 * (hb + 1), 128 * i:128 * (i + 1)])
        if i == 1:
            t_go = p.ctr["yL"].v
    p.wait("gpsimd", "yL", t_go)
    for hg in range(4):
        for pi, part in enumerate(("q", "k", "v")):
            r0 = pi * E + 512 * hg
            t_chunk[(part, hg)] = _dma(p, "gpsimd", "gL", scrq[r0:r0 + 512, :], wqkv[r0:r0 + 512, :])
            p.wait("gpsimd", "gL", t_chunk[(part, hg)])
    for j in range(2):
        t_wout = _dma(p, "gpsimd", "gL", scro[1024 * j:1024 * (j + 1), :], wout[1024 * j:1024 * (j + 1), :])
        p.wait("gpsimd", "gL", t_wout)
    p.barrier()

    osb = p.sb("osb", [128, NT * E], BF16, stack=s1)
    s1a = contextlib.ExitStack()
    wTd = [p.sb(f"wT{j}", [128, NE * 512], BF16, stack=s1a) for j in range(2)]
    qT = p.sb("qT", [128, 4 * T], BF16, stack=s1a)
    kT = p.sb("kT", [128, 4 * TH], BF16, stack=s1a)
    VW = 4 * 129
    vsb = p.sb("vsb", [128, NTH * VW], BF16, stack=s1a)
    et2 = [p.sb(f"et2_{j}", [128, 640], BF16, stack=s1a) for j in range(2)]
    amask_sb = p.sb("amask_sb", [128, 40 * 128], BF16, stack=s1a)
    rcol = p.sb("rcol", [128, 1], F32, stack=s1a)
    psq = PSA[:, 0:1024]
    psk = PSA[:, 0:1536]
    psv = PSA[:, 0:512]

    q2f = p.sb("q2f", [128, NE * 128], F32, stack=s1a)
    q2q = p.sb("q2q", [128, NE * 128], FP8, stack=s1a)
    q2b = p.sb("q2b", [128, NE * 128], BF16, stack=s1a)
    def emit_q2_chunk(ns):
        tld = _dma(p, "gpsimd", "gQ", q2f[:].rearrange("p (t n) -> p t n", n=128),
                   in2r[:, :, 128 * ns:128 * (ns + 1)])
        p.wait("vector", "gQ", tld)
        p.on("vector", lambda e, h=hv: e.tensor_single_scalar(out=q2f[:], in_=q2f[:], scalar=scl1_c[:],
                                                               op=ALU.mult).then_inc(h, 1))
        p.wait("vector", "vC", vC.add(1))
        p.on("vector", lambda e, h=hv: e.tensor_copy(out=q2q[:], in_=q2f[:]).then_inc(h, 1))
        p.wait("vector", "vC", vC.add(1))
        p.on("vector", lambda e, h=hv: e.tensor_copy(out=q2b[:], in_=q2q[:]).then_inc(h, 1))
        tq = vC.add(1)
        p.wait("gpsimd", "vC", tq)
        tst = _dma(p, "gpsimd", "gS", q2hr[:, :, 128 * ns:128 * (ns + 1)],
                   q2b[:].rearrange("p (t n) -> p t n", n=128))
        p.wait("gpsimd", "gS", tst)
        return p.ctr["gS"].v

    tam = _dma(p, "sync", "yL", amask_sb[:], amask.rearrange("s j i -> j s i"))
    p.wait("tensor", "yL", tam)
    SPH = [(part, hg) for hg in range(4) for part in ("q", "k", "v")]
    ld_done, mm_done = {}, {}
    def emit_wt_load(j):
        part, hg = SPH[j]
        buf = wTd[j % 2]
        sem = "cL" if j % 2 == 0 else "cM"
        if j >= 2:
            p.wait("scalar", "tC", mm_done[j - 2])
        p.wait("scalar", "gL", t_chunk[(part, hg)])
        r0 = {"q": 0, "k": E, "v": 2 * E}[part] + 512 * hg
        for et in range(NE):
            tw = _dma_tr(p, sem, buf[:, 512 * et:512 * (et + 1)],
                         scrq[r0:r0 + 512, 128 * et:128 * (et + 1)])
        ld_done[j] = (sem, tw)
    emit_wt_load(0)
    emit_wt_load(1)
    for hg in range(4):
        jq, jk, jv = 3 * hg, 3 * hg + 1, 3 * hg + 2
        wT = wTd[jq % 2]
        p.wait("tensor", *ld_done[jq])
        for tt in range(NTH):
            for hh in range(4):
                p.on("vector", lambda e, tt=tt, hh=hh: e.memset(
                    vsb[:, VW * tt + 129 * hh + 128:VW * tt + 129 * hh + 129], 1.0))
        p.on("vector", lambda e, h=hv: e.engine_nop().then_inc(h, 1))
        t_ones = vC.add(1)

        for hh in range(4):
            for half in range(2):
                for et in range(NE):
                    last = (et == NE - 1)
                    def mmq(e, hh=hh, half=half, et=et, last=last, wT=wT):
                        r = e.matmul(psq[:, 512 * half:512 * (half + 1)],
                                     wT[:, 512 * et + 128 * hh:512 * et + 128 * (hh + 1)],
                                     xlnT[:, TH * et + HALO + 512 * half:TH * et + HALO + 512 * (half + 1)],
                                     start=(et == 0), stop=last)
                        if last and half == 1:
                            r.then_inc(ht, 1)
                    p.on("tensor", mmq)
            tq_ = tC.add(1)
            p.wait("vector", "tC", tq_)
            p.on("vector", lambda e, hh=hh, h=hv: e.tensor_single_scalar(
                out=qT[:, T * hh:T * (hh + 1)], in_=psq, scalar=SCALE,
                op=ALU.mult).then_inc(h, 1))
            t_qev = vC.add(1)
            p.on("tensor", lambda e, t=t_qev, h=hv: e.wait_ge(h, t))
        mm_done[jq] = tC.v
        emit_wt_load(jv)
        wT = wTd[jk % 2]
        p.wait("tensor", *ld_done[jk])
        for hh in range(4):
            for third in range(3):
                for et in range(NE):
                    last = (et == NE - 1)
                    def mmk(e, hh=hh, third=third, et=et, last=last, wT=wT):
                        r = e.matmul(psk[:, 512 * third:512 * (third + 1)],
                                     wT[:, 512 * et + 128 * hh:512 * et + 128 * (hh + 1)],
                                     xlnT[:, TH * et + 512 * third:TH * et + 512 * (third + 1)],
                                     start=(et == 0), stop=last)
                        if last and third == 2:
                            r.then_inc(ht, 1)
                    p.on("tensor", mmk)
            tk_ = tC.add(1)
            p.wait("vector", "tC", tk_)
            p.on("vector", lambda e, hh=hh, h=hv: e.tensor_copy(
                out=kT[:, TH * hh:TH * (hh + 1)], in_=psk).then_inc(h, 1))
            t_kev = vC.add(1)
            p.on("tensor", lambda e, t=t_kev, h=hv: e.wait_ge(h, t))
        mm_done[jk] = tC.v
        if jk + 2 < 12:
            emit_wt_load(jk + 2)
        wT = wTd[jv % 2]
        p.wait("tensor", *ld_done[jv])
        p.wait("tensor", "vC", t_ones)
        for tt in range(NTH):
            for et in range(NE):
                last = (et == NE - 1)
                def mmv(e, tt=tt, et=et, last=last, wT=wT):
                    r = e.matmul(psv, xlnT[:, TH * et + 128 * tt:TH * et + 128 * (tt + 1)],
                                 wT[:, 512 * et:512 * (et + 1)], start=(et == 0), stop=last)
                    if last:
                        r.then_inc(ht, 1)
                p.on("tensor", mmv)
            tv_ = tC.add(1)
            p.wait("vector", "tC", tv_)
            p.on("vector", lambda e, tt=tt, h=hv: e.tensor_copy(
                out=vsb[:].rearrange("p (t h c) -> p t h c", t=NTH, h=4)[:, tt, :, 0:128],
                in_=psv.rearrange("p (h c) -> p h c", h=4)).then_inc(h, 1))
            t_vev = vC.add(1)
            p.on("tensor", lambda e, t=t_vev, h=hv: e.wait_ge(h, t))
        mm_done[jv] = tC.v
        if jv + 2 < 12:
            emit_wt_load(jv + 2)

        psst_ar = [PSB[:, 0:640], PSA[:, 0:640]]
        pso_ar = [PSB[:, 768:900], PSA[:, 768:900]]
        st_h, ex_h, vm_h, evac_h = {}, {}, {}, {}
        iters = [(hh, r) for hh in range(4) for r in range(NT)]
        def emit_vmm(i):
            hh, r = iters[i]
            p.wait("tensor", "sC", ex_h[i])
            for m in range(5):
                lastm = (m == 4)
                def mmo(e, hh=hh, r=r, m=m, i=i, lastm=lastm):
                    rr = e.matmul(pso_ar[i % 2][:, 0:129], et2[i % 2][:, 128 * m:128 * (m + 1)],
                                  vsb[:, VW * (r + m) + 129 * hh:VW * (r + m) + 129 * (hh + 1)],
                                  start=(m == 0), stop=lastm)
                    if lastm:
                        rr.then_inc(ht, 1)
                p.on("tensor", mmo)
            vm_h[i] = tC.add(1)
        def emit_evac(i):
            hh, r = iters[i]
            p.wait("vector", "tC", vm_h[i])
            p.on("vector", lambda e, i=i, h=hv: e.reciprocal(out=rcol[:], in_=pso_ar[i % 2][:, 128:129]).then_inc(h, 1))
            p.wait("vector", "vC", vC.add(1))
            p.on("vector", lambda e, hg=hg, hh=hh, r=r, i=i, h=hv: e.tensor_single_scalar(
                out=osb[:, E * r + 128 * (4 * hg + hh):E * r + 128 * (4 * hg + hh + 1)],
                in_=pso_ar[i % 2][:, 0:128], scalar=rcol[:], op=ALU.mult).then_inc(h, 1))
            evac_h[i] = vC.add(1)
        for i, (hh, r) in enumerate(iters):
            if i >= 2:
                p.wait("tensor", "sC", ex_h[i - 2])
                p.on("tensor", lambda e, t=evac_h[i - 2], h=hv: e.wait_ge(h, t))
            slots = [0, 4] + [m for m in range(1, 4) if m < 4 - r]
            for m in range(5):
                p.on("tensor", lambda e, hh=hh, r=r, m=m, i=i: e.matmul(
                    psst_ar[i % 2][:, 128 * m:128 * (m + 1)],
                    kT[:, TH * hh + 128 * (r + m):TH * hh + 128 * (r + m + 1)],
                    qT[:, T * hh + 128 * r:T * hh + 128 * (r + 1)],
                    start=True, stop=(m not in slots)))
                if m in slots:
                    lastm = (m == max(slots))
                    def mmask(e, r=r, m=m, i=i, lastm=lastm):
                        rr = e.matmul(psst_ar[i % 2][:, 128 * m:128 * (m + 1)], ident_b[:],
                                      amask_sb[:, 128 * (5 * r + m):128 * (5 * r + m + 1)],
                                      start=False, stop=True)
                        if lastm:
                            rr.then_inc(ht, 1)
                    p.on("tensor", mmask)
            st_h[i] = tC.add(1)
            if i >= 1:
                emit_vmm(i - 1)
            p.wait("scalar", "tC", st_h[i])
            p.on("scalar", lambda e, i=i, h=hs: e.activation(out=et2[i % 2][:], in_=psst_ar[i % 2],
                                                             func=AF.Exp).then_inc(h, 1))
            ex_h[i] = sC.add(1)
            if i >= 1:
                emit_evac(i - 1)
        emit_vmm(len(iters) - 1)
        emit_evac(len(iters) - 1)
        for qq in range(4):
            t_q2done = emit_q2_chunk(4 * hg + qq)
        p.barrier()

    s1a.close()
    oT = p.sb("oT", [128, NE * T], BF16, stack=s1)
    for r in range(NT):
        ev, ttr = emit_transpose_tiles(
            p, osb[:, E * r:E * (r + 1)],
            oT[:].rearrange("p (t c) -> p t c", c=T)[:, :, 128 * r:128 * (r + 1)],
            PSTB, ident_b, NE)
        p.wait("tensor", "vC", ev)
    p.barrier()

    woT = p.sb("woT", [128, NE * 512], BF16, stack=s1)
    psm = [PSA[:, 0:512], PSA[:, 512:1024]]
    p.wait("scalar", "gL", t_wout)
    for n in range(4):
        if n > 0:
            p.wait("scalar", "tC", p.ctr["tC"].v)
        for et in range(NE):
            tw = _dma_tr(p, "cL", woT[:, 512 * et:512 * (et + 1)],
                         scro[512 * n:512 * (n + 1), 128 * et:128 * (et + 1)])
        p.wait("tensor", "cL", tw)
        tevs = {}
        for r in range(NT):
            buf = xbuf[r % 2]
            tld = _dma(p, "sync", "yL", buf[:, 512 * n:512 * (n + 1)],
                       x0[128 * r:128 * (r + 1), 512 * n:512 * (n + 1)])
            pb = psm[r % 2]
            if r >= 2:
                p.on("tensor", lambda e, t=tevs[r - 2], h=hv: e.wait_ge(h, t))
            for et in range(NE):
                last = (et == NE - 1)
                def mmw(e, r=r, et=et, pb=pb, last=last):
                    rr = e.matmul(pb, oT[:, T * et + 128 * r:T * et + 128 * (r + 1)],
                                  woT[:, 512 * et:512 * (et + 1)], start=(et == 0), stop=last)
                    if last:
                        rr.then_inc(ht, 1)
                p.on("tensor", mmw)
            tm_ = tC.add(1)
            p.wait("vector", "tC", tm_)
            p.wait("vector", "yL", tld)
            p.on("vector", lambda e, pb=pb, buf=buf, n=n, h=hv: e.scalar_tensor_tensor(
                out=buf[:, 512 * n:512 * (n + 1)], in0=pb, scalar=1.0,
                in1=buf[:, 512 * n:512 * (n + 1)], op0=ALU.mult, op1=ALU.add).then_inc(h, 1))
            tev = vC.add(1)
            tevs[r] = tev
            p.wait("sync", "vC", tev)
            tst = _dma(p, "sync", "yS", xa[128 * r:128 * (r + 1), 512 * n:512 * (n + 1)],
                       buf[:, 512 * n:512 * (n + 1)])
        p.wait("tensor", "vC", tev)
    p.wait("sync", "yS", p.ctr["yS"].v)
    p.barrier()
    s1.close()

    s2 = contextlib.ExitStack()
    qf8 = [p.sb(f"qf8_{j}", [128, E], FP8, stack=s2) for j in range(2)]
    qfb = [p.sb(f"qfb_{j}", [128, E], BF16, stack=s2) for j in range(2)]
    qfTa = p.sb("qfTa", [128, NT * E], BF16, stack=s2)
    q2s = [p.sb(f"q2s{j}", [128, NE * 512], BF16, stack=s2) for j in range(2)]
    x2a = p.sb("x2a", [128, NT * E], F32, stack=s2)
    psm = [PSA[:, 0:512], PSA[:, 512:1024]]
    p.wait("gpsimd", "gS", t_q2done)
    ap_h.clear(); tr_h.clear(); ev_h.clear()
    vq_h = {}
    pstb_half = [PSTB[:, 0:1024], PSTB[:, 1024:2048]]
    for r in range(NT):
        par = r % 2
        xsl = x2a[:, E * r:E * (r + 1)]
        tld = _dma(p, "sync", "yL", xsl, xa[128 * r:128 * (r + 1), :])
        p.wait("vector", "yL", tld)
        p.wait("scalar", "yL", tld)
        if r >= 2:
            p.wait("vector", "sC", ap_h[r - 2])
            p.wait("scalar", "vC", vq_h[r - 2])
        ap = emit_ln2(p, xsl, xln2[par][:], sc2, par, scale_ap=scl_c[:])
        ap_h[r] = ap
        p.wait("vector", "sC", ap)
        if r >= 2:
            p.wait("vector", "tC", tr_h[r - 2])
        p.on("vector", lambda e, par=par, h=hv: e.tensor_copy(out=qf8[par][:], in_=xln2[par][:]).then_inc(h, 1))
        p.wait("vector", "vC", vC.add(1))
        p.on("vector", lambda e, par=par, h=hv: e.tensor_copy(out=qfb[par][:], in_=qf8[par][:]).then_inc(h, 1))
        tq_ = vC.add(1)
        vq_h[r] = tq_
        p.wait("tensor", "vC", tq_)
        for hb in range(2):
            ph = pstb_half[hb]
            if r >= 1:
                p.on("tensor", lambda e, t=ev_h[(r - 1, hb)], h=hv: e.wait_ge(h, t))
            for j in range(8):
                et = 8 * hb + j
                lastj = (j == 7)
                def tr1(e, et=et, j=j, par=par, ph=ph, lastj=lastj):
                    rr = e.transpose(ph[:, 128 * j:128 * (j + 1)],
                                     qfb[par][:, 128 * et:128 * (et + 1)], ident_b[:])
                    if lastj:
                        rr.then_inc(ht, 1)
                p.on("tensor", tr1)
            t = tC.add(1)
            p.wait("vector", "tC", t)
            dv = qfTa[:].rearrange("p (r t c) -> p (r t) c", r=NT, c=128)[:, NE * r + 8 * hb:NE * r + 8 * (hb + 1), :]
            p.on("vector", lambda e, dv=dv, ph=ph, h=hv: e.tensor_copy(
                out=dv, in_=ph.rearrange("p (t c) -> p t c", c=128)).then_inc(h, 1))
            ev_h[(r, hb)] = vC.add(1)
        tr_h[r] = t
    p.barrier()
    mm_hist = {}
    ev_hist = {}
    gi = 0
    for ns in range(4):
        bufw = q2s[ns % 2]
        if ns >= 2:
            p.wait("gpsimd", "tC", mm_hist[(ns - 2) * NT + NT - 1])
        tw = _dma(p, "gpsimd", "gL", bufw[:].rearrange("p (t n) -> p t n", n=512),
                  q2hr[:, :, 512 * ns:512 * (ns + 1)])
        p.wait("tensor", "gL", tw)
        for r in range(NT):
            pb = psm[gi % 2]
            if gi >= 2:
                p.on("tensor", lambda e, t=ev_hist[gi - 2], h=hv: e.wait_ge(h, t))
            for et in range(NE):
                last = (et == NE - 1)
                def mmf(e, r=r, et=et, pb=pb, bufw=bufw, last=last):
                    rr = e.matmul(pb, qfTa[:, E * r + 128 * et:E * r + 128 * (et + 1)],
                                  bufw[:, 512 * et:512 * (et + 1)], start=(et == 0), stop=last)
                    if last:
                        rr.then_inc(ht, 1)
                p.on("tensor", mmf)
            tm_ = tC.add(1)
            mm_hist[gi] = tm_
            p.wait("vector", "tC", tm_)
            p.on("vector", lambda e, pb=pb, r=r, ns=ns, h=hv: e.scalar_tensor_tensor(
                out=x2a[:, E * r + 512 * ns:E * r + 512 * (ns + 1)], in0=pb, scalar=dq_c[:],
                in1=x2a[:, E * r + 512 * ns:E * r + 512 * (ns + 1)],
                op0=ALU.mult, op1=ALU.add).then_inc(h, 1))
            ev_hist[gi] = vC.add(1)
            gi += 1
    p.wait("sync", "vC", p.ctr["vC"].v)
    for r in range(NT):
        _dma(p, "sync", "yS", x2[128 * r:128 * (r + 1), :], x2a[:, E * r:E * (r + 1)])
    p.wait("sync", "yS", p.ctr["yS"].v)
    p.barrier()
    s2.close()

    s3 = contextlib.ExitStack()
    xlnT2 = p.sb("xlnT2", [128, NE * T], BF16, stack=s3)
    wk1b = p.sb("wk1b", [128, NE * 128], BF16, stack=s3)
    wq1b = p.sb("wq1b", [128, NE * 128], BF16, stack=s3)
    wk2b = p.sb("wk2b", [128, E2], BF16, stack=s3)
    wvb = [p.sb(f"wvb{j}", [128, NE * 128], BF16, stack=s3) for j in range(2)]
    k1T = p.sb("k1T", [128, T], BF16, stack=s3)
    k1tmp = p.sb("k1tmp", [128, 128], BF16, stack=s3)
    q1sb = p.sb("q1sb", [128, D], F32, stack=s3)
    f_sb = p.sb("f_sb", [128, T], F32, stack=s3)
    a_sb = [p.sb(f"a_sb{j}", [128, T], F32, stack=s3) for j in range(2)]
    kg_sb = [p.sb(f"kg_sb{j}", [128, T], F32, stack=s3) for j in range(2)]
    v_f = p.sb("v_f", [128, T], F32, stack=s3)
    b_sb = p.sb("b_sb", [128, T], F32, stack=s3)
    ac_sb = [p.sb(f"ac_sb{j}", [128, T], BF16, stack=s3) for j in range(2)]
    hl_sb = [p.sb(f"hl_sb{j}", [128, T], BF16, stack=s3) for j in range(2)]
    pssm = PSB[:, 0:128]
    pstb1 = PSTB[:, 0:128]

    wk1r = wk1.rearrange("(t p) n -> p t n", p=128)
    wq1r = wq1.rearrange("(t p) n -> p t n", p=128)
    wvr = wv.rearrange("(t p) n -> p t n", p=128)
    _dma(p, "gpsimd", "gL", wk1b[:].rearrange("p (t n) -> p t n", n=128), wk1r)
    _dma(p, "gpsimd", "gL", wq1b[:].rearrange("p (t n) -> p t n", n=128), wq1r)
    twk = _dma(p, "gpsimd", "gL", wk2b[:], wk2)

    ap_h.clear(); tr_h.clear(); ev_h.clear()
    for r in range(NT):
        ln_tr_tile(r, x2[128 * r:128 * (r + 1), :],
                   lambda r, hb: xlnT2[:].rearrange("p (t c) -> p t c", c=T)[:, 8 * hb:8 * (hb + 1), 128 * r:128 * (r + 1)])
    p.barrier()

    p.wait("tensor", "gL", twk)
    for r in range(NT):
        for et in range(NE):
            last = (et == NE - 1)
            def mk1(e, r=r, et=et, last=last):
                rr = e.matmul(pssm, xlnT2[:, T * et + 128 * r:T * et + 128 * (r + 1)],
                              wk1b[:, 128 * et:128 * (et + 1)], start=(et == 0), stop=last)
                if last:
                    rr.then_inc(ht, 1)
            p.on("tensor", mk1)
        t1_ = tC.add(1)
        p.wait("vector", "tC", t1_)
        p.on("vector", lambda e, h=hv: e.tensor_copy(out=k1tmp[:], in_=pssm).then_inc(h, 1))
        tev = vC.add(1)
        p.wait("tensor", "vC", tev)
        p.on("tensor", lambda e, h=ht: e.transpose(pstb1, k1tmp[:], ident_b[:]).then_inc(h, 1))
        t2_ = tC.add(1)
        p.wait("vector", "tC", t2_)
        p.on("vector", lambda e, r=r, h=hv: e.tensor_copy(out=k1T[:, 128 * r:128 * (r + 1)],
                                                          in_=pstb1).then_inc(h, 1))
        tev2 = vC.add(1)
        p.on("tensor", lambda e, t=tev2, h=hv: e.wait_ge(h, t))
        for et in range(NE):
            last = (et == NE - 1)
            def mq1(e, r=r, et=et, last=last):
                rr = e.matmul(pssm, xlnT2[:, T * et + 128 * r:T * et + 128 * (r + 1)],
                              wq1b[:, 128 * et:128 * (et + 1)], start=(et == 0), stop=last)
                if last:
                    rr.then_inc(ht, 1)
            p.on("tensor", mq1)
        t3_ = tC.add(1)
        p.wait("vector", "tC", t3_)
        p.on("vector", lambda e, h=hv: e.tensor_copy(out=q1sb[:], in_=pssm).then_inc(h, 1))
        tev3 = vC.add(1)
        p.wait("sync", "vC", tev3)
        tst = _dma(p, "sync", "yS", q1o[128 * r:128 * (r + 1), :], q1sb[:])
        p.wait("sync", "yS", tst)
        p.on("tensor", lambda e, t=tev3, h=hv: e.wait_ge(h, t))
    p.barrier()

    psraw = PSB[:, 0:1024]
    psv1 = [PSA[:, 0:1024], PSA[:, 1024:2048]]
    raw_h, sigf_h, sig_h, vmm_h, vf_h, scan_h, st_h = {}, {}, {}, {}, {}, {}, {}
    for ch in range(NCH):
        bufw = wvb[ch % 2]
        if ch >= 2:
            p.wait("gpsimd", "tC", vmm_h[ch - 2])
        tw = _dma(p, "gpsimd", "gL", bufw[:].rearrange("p (t n) -> p t n", n=128),
                  wvr[:, :, 128 * ch:128 * (ch + 1)])
        if ch >= 1:
            p.wait("tensor", "sC", sigf_h[ch - 1])
        for half in range(2):
            lasth = (half == 1)
            def mraw(e, ch=ch, half=half, lasth=lasth):
                rr = e.matmul(psraw[:, 512 * half:512 * (half + 1)], wk2b[:, 128 * ch:128 * (ch + 1)],
                              k1T[:, 512 * half:512 * (half + 1)], start=True, stop=True)
                if lasth:
                    rr.then_inc(ht, 1)
            p.on("tensor", mraw)
        raw_h[ch] = tC.add(1)
        p.wait("tensor", "gL", tw)
        if ch >= 2:
            p.wait("tensor", "vC", vf_h[ch - 2])
        for half in range(2):
            for et in range(NE):
                last = (et == NE - 1)
                def mv2(e, ch=ch, et=et, half=half, bufw=bufw, last=last, lasth=(half == 1)):
                    rr = e.matmul(psv1[ch % 2][:, 512 * half:512 * (half + 1)],
                                  bufw[:, 128 * et:128 * (et + 1)],
                                  xlnT2[:, T * et + 512 * half:T * et + 512 * (half + 1)],
                                  start=(et == 0), stop=last)
                    if last and lasth:
                        rr.then_inc(ht, 1)
                p.on("tensor", mv2)
        vmm_h[ch] = tC.add(1)
        p.wait("scalar", "tC", raw_h[ch])
        if ch >= 2:
            p.wait("scalar", "vC", scan_h[ch - 2])
        p.on("scalar", lambda e, ch=ch, h=hs: e.activation(out=f_sb[:], in_=psraw,
                                                           func=AF.Sigmoid).then_inc(h, 1))
        sigf_h[ch] = sC.add(1)
        p.wait("scalar", "sC", sigf_h[ch])
        p.on("scalar", lambda e, ch=ch: e.activation(out=a_sb[ch % 2][:], in_=f_sb[:], func=AF.Sigmoid))
        p.on("scalar", lambda e, ch=ch, h=hs: e.activation(out=kg_sb[ch % 2][:], in_=f_sb[:],
                                                           func=AF.Sigmoid, scale=-1.0).then_inc(h, 1))
        sig_h[ch] = sC.add(1)
        p.wait("vector", "tC", vmm_h[ch])
        p.on("vector", lambda e, ch=ch, h=hv: e.tensor_copy(out=v_f[:], in_=psv1[ch % 2]).then_inc(h, 1))
        vf_h[ch] = vC.add(1)
        p.wait("vector", "vC", vf_h[ch])
        p.wait("vector", "sC", sig_h[ch])
        p.on("vector", lambda e, ch=ch, h=hv: e.tensor_mul(out=b_sb[:], in0=kg_sb[ch % 2][:],
                                                           in1=v_f[:]).then_inc(h, 1))
        tb_ = vC.add(1)
        p.wait("vector", "vC", tb_)
        if ch >= 2:
            p.wait("vector", "yS", st_h[ch - 2])
        p.on("vector", lambda e, ch=ch: e.tensor_tensor_scan(
            out=ac_sb[ch % 2][:], data0=a_sb[ch % 2][:], data1=a_sb[ch % 2][:],
            initial=1.0, op0=ALU.mult, op1=ALU.bypass))
        p.on("vector", lambda e, ch=ch, h=hv: e.tensor_tensor_scan(
            out=hl_sb[ch % 2][:], data0=a_sb[ch % 2][:], data1=b_sb[:],
            initial=0.0, op0=ALU.mult, op1=ALU.add).then_inc(h, 1))
        scan_h[ch] = vC.add(1)
        p.wait("sync", "vC", scan_h[ch])
        _dma(p, "sync", "yS", acum[128 * ch:128 * (ch + 1), :], ac_sb[ch % 2][:])
        st_h[ch] = _dma(p, "sync", "yS", hloc[128 * ch:128 * (ch + 1), :], hl_sb[ch % 2][:])

    p.wait("sync", "yS", p.ctr["yS"].v)
    p.wait("gpsimd", "gS", p.ctr["gS"].v)
    s3.close()

    p.emit()
    stack.close()
    return nc



def build_B():
    nc = bass.Bass("TRN2", target_bir_lowering=False)
    stack = contextlib.ExitStack()
    p = P(nc, stack)

    x2 = nc.dram_tensor("x2", [T, E], F32, kind="ExternalInput").ap()
    acum = nc.dram_tensor("acum", [E2, T], BF16, kind="ExternalInput").ap()
    hloc = nc.dram_tensor("hloc", [E2, T], BF16, kind="ExternalInput").ap()
    q1t = nc.dram_tensor("q1t", [D, T], F32, kind="ExternalInput").ap()
    predA = nc.dram_tensor("predA", [128, 96], F32, kind="ExternalInput").ap()
    predH = nc.dram_tensor("predH", [128, 96], F32, kind="ExternalInput").ap()
    wq2 = nc.dram_tensor("wq2", [D, E2], F32, kind="ExternalInput").ap()
    wo = nc.dram_tensor("wo", [E2, E], F32, kind="ExternalInput").ap()
    nwr = nc.dram_tensor("nwr", [128, NCH], F32, kind="ExternalInput").ap()
    wrr = nc.dram_tensor("wrr", [128, NE], F32, kind="ExternalInput").ap()
    brr = nc.dram_tensor("brr", [1, 1], F32, kind="ExternalInput").ap()
    identf = nc.dram_tensor("identf", [128, 128], F32, kind="ExternalInput").ap()

    x3 = nc.dram_tensor("x3", [T, E], F32, kind="ExternalOutput").ap()
    xlnT3o = nc.dram_tensor("xlnT3o", [128, NE * T], BF16, kind="ExternalOutput").ap()
    scso = nc.dram_tensor("scso", [1, T], F32, kind="ExternalOutput").ap()

    ident_f = p.sb("ident_f", [128, 128], F32)
    eps5 = p.sb("eps5", [128, 1], F32)
    xbuf = [p.sb(f"xbuf{i}", [128, E], F32) for i in range(2)]
    xln_f = p.sb("xln_f", [128, E], F32)
    xln_f2 = p.sb("xln_f2", [128, E], F32)
    sc2 = mk_sc2(p, "l0")
    sc = {
        "bn24": p.sb("bn24", [128, 24], F32),
        "mv": p.sb("mv", [128, 2], F32),
        "sd": p.sb("sd", [128, 1], F32),
        "rstd": p.sb("rstd", [128, 1], F32),
        "negmr": p.sb("negmr", [128, 1], F32),
        "eps": eps5,
    }
    PSA = p.ps("PSA", [128, 2048], F32)
    PSB = p.ps("PSB", [128, 1024], F32)

    hv, hs, ht = p.sems["vC"], p.sems["sC"], p.sems["tC"]
    vC, sC, tC = p.ctr["vC"], p.ctr["sC"], p.ctr["tC"]

    pA = p.sb("pA", [128, 96], F32)
    pH = p.sb("pH", [128, 96], F32)
    hin = p.sb("hin", [128, NCH], F32)
    q1f = p.sb("q1f", [128, T], F32)
    q1b = p.sb("q1b", [128, T], BF16)
    wq2b = p.sb("wq2b", [128, E2], BF16)
    nwc = p.sb("nwc", [128, NCH], F32)
    wrb = p.sb("wrb", [128, NE], BF16)
    brc = p.sb("brc", [1, 1], F32)
    onesb = p.sb("onesb", [128, 1], BF16)
    onesrow = p.sb("onesrow", [1, 128], BF16)
    acb = [p.sb(f"acb{i}", [128, T], BF16) for i in range(2)]
    hlb = [p.sb(f"hlb{i}", [128, T], BF16) for i in range(2)]
    h_sb = p.sb("h_sb", [128, T], BF16)
    qg_sb = p.sb("qg_sb", [128, T], BF16)
    o2_sb = p.sb("o2_sb", [128, T], BF16)
    out_sb = p.sb("out_sb", [128, NCH * T], BF16)
    ssqf = p.sb("ssqf", [1, T], F32)
    ssqb = p.sb("ssqb", [1, T], BF16)
    var_c = p.sb("var_c", [128, NT], F32)
    sd_c = p.sb("sd_c", [128, NT], F32)
    rstd_c = p.sb("rstd_c", [128, NT], F32)

    _dma(p, "sync", "yL", ident_f[:], identf)
    _dma(p, "sync", "yL", pA[:], predA)
    _dma(p, "sync", "yL", pH[:], predH)
    _dma(p, "sync", "yL", q1f[:], q1t)
    _dma(p, "sync", "yL", nwc[:], nwr)
    t_br = _dma(p, "sync", "yL", brc[:], brr)
    _dma(p, "gpsimd", "gL", wq2b[:], wq2)
    t_wr = _dma(p, "gpsimd", "gL", wrb[:], wrr)
    p.on("vector", lambda e: e.memset(eps5[:], 1e-5))
    p.on("vector", lambda e: e.memset(onesb[:], 1.0))
    p.on("vector", lambda e: e.memset(onesrow[:], 1.0))
    p.wait("vector", "yL", t_br)
    p.wait("vector", "gL", t_wr)
    p.wait("tensor", "yL", t_br)
    p.wait("tensor", "gL", t_wr)
    p.on("vector", lambda e, h=hv: e.tensor_copy(out=q1b[:], in_=q1f[:]).then_inc(h, 1))
    p.wait("vector", "vC", vC.add(1))
    p.on("vector", lambda e, h=hv: e.memset(hin[:], 0.0).then_inc(h, 1))
    p.wait("vector", "vC", vC.add(1))
    for j in range(3):
        p.on("vector", lambda e, j=j, h=hv: e.tensor_mul(out=hin[:], in0=pA[:, 32 * j:32 * (j + 1)],
                                                         in1=hin[:]).then_inc(h, 1))
        p.wait("vector", "vC", vC.add(1))
        p.on("vector", lambda e, j=j, h=hv: e.tensor_add(out=hin[:], in0=hin[:],
                                                         in1=pH[:, 32 * j:32 * (j + 1)]).then_inc(h, 1))
        p.wait("vector", "vC", vC.add(1))
    p.barrier()

    psg = PSA[:, 0:1024]
    psr1 = PSB[0:1, 0:512]
    psr2 = PSB[0:1, 512:1024]
    t_ldprev = 0
    for ch in range(NCH):
        ab = acb[ch % 2]
        hb = hlb[ch % 2]
        if ch >= 2:
            p.wait("sync", "vC", t_hdone)
        _dma(p, "sync", "yL", ab[:], acum[128 * ch:128 * (ch + 1), :])
        tld = _dma(p, "sync", "yL", hb[:], hloc[128 * ch:128 * (ch + 1), :])
        if ch > 0:
            p.wait("tensor", "sC", t_silu)
        for half in range(2):
            lasth = (half == 1)
            def mmg(e, ch=ch, half=half, lasth=lasth):
                rr = e.matmul(psg[:, 512 * half:512 * (half + 1)], wq2b[:, 128 * ch:128 * (ch + 1)],
                              q1b[:, 512 * half:512 * (half + 1)], start=True, stop=True)
                if lasth:
                    rr.then_inc(ht, 1)
            p.on("tensor", mmg)
        tqg = tC.add(1)
        if ch > 0:
            p.wait("scalar", "vC", t_out)
        p.wait("scalar", "tC", tqg)
        p.on("scalar", lambda e, h=hs: e.activation(out=qg_sb[:], in_=psg, func=AF.Silu).then_inc(h, 1))
        t_silu = sC.add(1)
        p.wait("vector", "yL", tld)
        p.on("vector", lambda e, ch=ch, ab=ab, hb=hb, h=hv: e.scalar_tensor_tensor(
            out=h_sb[:], in0=ab[:], scalar=hin[:, ch:ch + 1], in1=hb[:],
            op0=ALU.mult, op1=ALU.add).then_inc(h, 1))
        t_hdone = vC.add(1)
        p.wait("vector", "vC", t_hdone)
        p.wait("vector", "sC", t_silu)
        p.on("vector", lambda e, ch=ch, h=hv: e.tensor_mul(
            out=out_sb[:, T * ch:T * (ch + 1)], in0=qg_sb[:], in1=h_sb[:]).then_inc(h, 1))
        t_out = vC.add(1)
        p.wait("scalar", "vC", t_out)
        p.on("scalar", lambda e, ch=ch, h=hs: e.activation(
            out=o2_sb[:], in_=out_sb[:, T * ch:T * (ch + 1)], func=AF.Square).then_inc(h, 1))
        t_o2 = sC.add(1)
        p.wait("tensor", "sC", t_o2)
        for half in range(2):
            lasth = (half == 1)
            def mms(e, ch=ch, half=half, lasth=lasth):
                rr = e.matmul((psr1 if half == 0 else psr2), onesb[:],
                              o2_sb[:, 512 * half:512 * (half + 1)],
                              start=(ch == 0), stop=(ch == NCH - 1))
                if lasth:
                    rr.then_inc(ht, 1)
            p.on("tensor", mms)
        t_ssq = tC.add(1)
        p.on("scalar", lambda e, t=t_ssq, h=ht: e.wait_ge(h, t))
    p.barrier()

    p.on("vector", lambda e, h=hv: e.tensor_copy(out=ssqf[0:1, 0:512], in_=psr1).then_inc(h, 1))
    p.on("vector", lambda e, h=hv: e.tensor_copy(out=ssqf[0:1, 512:1024], in_=psr2).then_inc(h, 1))
    p.wait("vector", "vC", vC.add(2))
    p.on("vector", lambda e, h=hv: e.tensor_copy(out=ssqb[:], in_=ssqf[:]).then_inc(h, 1))
    tssb = vC.add(1)
    p.wait("tensor", "vC", tssb)
    pscol = PSB[:, 0:128]
    for r in range(NT):
        p.on("tensor", lambda e, r=r, h=ht: e.matmul(pscol[:, 0:1], ssqb[0:1, 128 * r:128 * (r + 1)],
                                                     onesb[0:1, 0:1], start=True, stop=True).then_inc(h, 1))
        tcol = tC.add(1)
        p.wait("vector", "tC", tcol)
        p.on("vector", lambda e, r=r, h=hv: e.tensor_scalar(
            out=var_c[:, r:r + 1], in0=pscol[:, 0:1], scalar1=float(1.0 / E2), scalar2=1e-6,
            op0=ALU.mult, op1=ALU.add).then_inc(h, 1))
        tvar = vC.add(1)
        p.on("tensor", lambda e, t=tvar, h=hv: e.wait_ge(h, t))
    p.wait("scalar", "vC", vC.v)
    p.on("scalar", lambda e, h=hs: e.activation(out=sd_c[:], in_=var_c[:], func=AF.Sqrt).then_inc(h, 1))
    p.wait("vector", "sC", sC.add(1))
    p.on("vector", lambda e, h=hv: e.reciprocal(out=rstd_c[:], in_=sd_c[:]).then_inc(h, 1))
    trst = vC.add(1)
    p.barrier()

    wos = p.sb("wos", [128, NCH * 512], BF16)
    wor = wo.rearrange("(t p) n -> p t n", p=128)
    psm = [PSA[:, 0:512], PSA[:, 512:1024]]
    for n in range(4):
        if n > 0:
            p.wait("gpsimd", "tC", tC.v)
        tw = _dma(p, "gpsimd", "gL", wos[:].rearrange("p (t n) -> p t n", n=512),
                  wor[:, :, 512 * n:512 * (n + 1)])
        p.wait("vector", "gL", tw)
        for c in range(NCH):
            lastc = (c == NCH - 1)
            def scw(e, c=c, lastc=lastc):
                rr = e.tensor_single_scalar(out=wos[:, 512 * c:512 * (c + 1)],
                                            in_=wos[:, 512 * c:512 * (c + 1)],
                                            scalar=nwc[:, c:c + 1], op=ALU.mult)
                if lastc:
                    rr.then_inc(hv, 1)
            p.on("vector", scw)
        tnw = vC.add(1)
        p.wait("tensor", "vC", tnw)
        tevs = {}
        for r in range(NT):
            buf = xbuf[r % 2]
            tld = _dma(p, "sync", "yL", buf[:, 512 * n:512 * (n + 1)],
                       x2[128 * r:128 * (r + 1), 512 * n:512 * (n + 1)])
            pb = psm[r % 2]
            if r >= 2:
                p.on("tensor", lambda e, t=tevs[r - 2], h=hv: e.wait_ge(h, t))
            for c in range(NCH):
                lastc = (c == NCH - 1)
                def mmo(e, r=r, c=c, pb=pb, lastc=lastc):
                    rr = e.matmul(pb, out_sb[:, T * c + 128 * r:T * c + 128 * (r + 1)],
                                  wos[:, 512 * c:512 * (c + 1)], start=(c == 0), stop=lastc)
                    if lastc:
                        rr.then_inc(ht, 1)
                p.on("tensor", mmo)
            tm_ = tC.add(1)
            p.wait("vector", "tC", tm_)
            p.wait("vector", "yL", tld)
            p.on("vector", lambda e, pb=pb, buf=buf, n=n, r=r, h=hv: e.scalar_tensor_tensor(
                out=buf[:, 512 * n:512 * (n + 1)], in0=pb, scalar=rstd_c[:, r:r + 1],
                in1=buf[:, 512 * n:512 * (n + 1)], op0=ALU.mult, op1=ALU.add).then_inc(h, 1))
            tev = vC.add(1)
            tevs[r] = tev
            p.wait("sync", "vC", tev)
            tst = _dma(p, "sync", "yS", x3[128 * r:128 * (r + 1), 512 * n:512 * (n + 1)],
                       buf[:, 512 * n:512 * (n + 1)])
        p.wait("tensor", "vC", tev)
    p.wait("sync", "yS", p.ctr["yS"].v)
    p.barrier()

    xlnT3 = p.sb("xlnT3", [128, NE * T], BF16)
    xln2 = [xln_f, xln_f2]
    p.on("vector", lambda e: e.memset(sc2["eps"][:], 1e-5))
    pst_half = [PSA[:, 0:1024], PSA[:, 1024:2048]]
    ap_h, tr_h, ev_h = {}, {}, {}
    for i in range(NT):
        par = i % 2
        buf = xbuf[par]
        if i >= 2:
            p.wait("sync", "sC", ap_h[i - 2])
        tld = _dma(p, "sync", "yL", buf[:], x3[128 * i:128 * (i + 1), :])
        p.wait("vector", "yL", tld)
        p.wait("scalar", "yL", tld)
        if i >= 2:
            p.wait("scalar", "tC", tr_h[i - 2])
        ap = emit_ln2(p, buf, xln2[par][:], sc2, par)
        ap_h[i] = ap
        p.wait("tensor", "sC", ap)
        for hb in range(2):
            ph = pst_half[hb]
            if i >= 1:
                p.on("tensor", lambda e, t=ev_h[(i - 1, hb)], h=hv: e.wait_ge(h, t))
            for j in range(8):
                et = 8 * hb + j
                lastj = (j == 7)
                def tr1(e, et=et, j=j, par=par, ph=ph, lastj=lastj):
                    rr = e.transpose(ph[:, 128 * j:128 * (j + 1)],
                                     xln2[par][:, 128 * et:128 * (et + 1)], ident_f[:])
                    if lastj:
                        rr.then_inc(ht, 1)
                p.on("tensor", tr1)
            t = tC.add(1)
            p.wait("vector", "tC", t)
            dv = xlnT3[:].rearrange("p (t c) -> p t c", c=T)[:, 8 * hb:8 * (hb + 1), 128 * i:128 * (i + 1)]
            p.on("vector", lambda e, dv=dv, ph=ph, h=hv: e.tensor_copy(
                out=dv, in_=ph.rearrange("p (t c) -> p t c", c=128)).then_inc(h, 1))
            ev_h[(i, hb)] = vC.add(1)
        tr_h[i] = t
    ev_last = ev_h[(NT - 1, 1)]
    p.wait("sync", "vC", ev_last)
    p.on("tensor", lambda e, t=ev_last, h=hv: e.wait_ge(h, t))
    tso = _dma(p, "sync", "yS", xlnT3o, xlnT3[:])
    for half in range(2):
        for et in range(NE):
            laste = (et == NE - 1)
            def mmsc(e, half=half, et=et, laste=laste):
                rr = e.matmul((psr1 if half == 0 else psr2), wrb[:, et:et + 1],
                              xlnT3[:, T * et + 512 * half:T * et + 512 * (half + 1)],
                              start=(et == 0), stop=laste)
                if laste and half == 1:
                    rr.then_inc(ht, 1)
            p.on("tensor", mmsc)
    tsc_ = tC.add(1)
    p.wait("vector", "tC", tsc_)
    p.on("vector", lambda e, h=hv: e.tensor_single_scalar(out=ssqf[0:1, 0:512], in_=psr1,
                                                          scalar=brc[0:1, 0:1], op=ALU.add).then_inc(h, 1))
    p.on("vector", lambda e, h=hv: e.tensor_single_scalar(out=ssqf[0:1, 512:1024], in_=psr2,
                                                          scalar=brc[0:1, 0:1], op=ALU.add).then_inc(h, 1))
    tsf = vC.add(2)
    p.wait("sync", "vC", tsf)
    _dma(p, "sync", "yS", scso, ssqf[:])
    p.wait("sync", "yS", p.ctr["yS"].v)
    p.emit()
    stack.close()
    return nc


def build_C():
    nc = bass.Bass("TRN2", target_bir_lowering=False)
    stack = contextlib.ExitStack()
    p = P(nc, stack)

    x3 = nc.dram_tensor("x3", [T, E], F32, kind="ExternalInput").ap()
    xlnT3 = nc.dram_tensor("xlnT3", [128, NE * T], BF16, kind="ExternalInput").ap()
    scall = nc.dram_tensor("scall", [1, 4096], F32, kind="ExternalInput").ap()
    msc = nc.dram_tensor("msc", [128, NT], F32, kind="ExternalInput").ap()
    w1 = nc.dram_tensor("w1", [E, DF], F32, kind="ExternalInput").ap()
    b1r = nc.dram_tensor("b1r", [128, NF], F32, kind="ExternalInput").ap()
    w2 = nc.dram_tensor("w2", [DF, E], F32, kind="ExternalInput").ap()
    b2r = nc.dram_tensor("b2r", [1, E], F32, kind="ExternalInput").ap()
    y = nc.dram_tensor("y", [T, E], F32, kind="ExternalOutput").ap()

    PSA = p.ps("PSA", [128, 2048], F32)
    PSB = p.ps("PSB", [128, 1024], F32)
    hv, hs, ht = p.sems["vC"], p.sems["sC"], p.sems["tC"]
    vC, sC, tC = p.ctr["vC"], p.ctr["sC"], p.ctr["tC"]

    xlt = p.sb("xlt", [128, NE * T], BF16)
    w1b = [p.sb(f"w1b{i}", [128, NE * 128], BF16) for i in range(2)]
    xbuf = [p.sb(f"xbuf{i}", [128, E], F32) for i in range(2)]
    tmp = p.sb("tmp", [128, 512], F32)
    b1c = p.sb("b1c", [128, NF], F32)
    b2b = p.sb("b2b", [1, E], BF16)
    b2f = p.sb("b2f", [1, E], F32)
    mscb = p.sb("mscb", [128, NT], F32)
    cnt = p.sb("cnt", [128, NT], F32)
    msel = p.sb("msel", [128, NT], F32)
    onesf = p.sb("onesf", [1, 128], F32)
    onesrow = p.sb("onesrow", [1, 128], BF16)
    sC1 = contextlib.ExitStack()
    scrow = p.sb("scrow", [1, 4096], F32, stack=sC1)
    J = p.sb("J", [128, 4096], F32, stack=sC1)
    junk = p.sb("junk", [128, 4096], F32, stack=sC1)

    tld0 = _dma(p, "sync", "yL", xlt[:], xlnT3)
    _dma(p, "sync", "yL", scrow[:], scall)
    _dma(p, "sync", "yL", mscb[:], msc)
    _dma(p, "sync", "yL", b1c[:], b1r)
    t_c0 = _dma(p, "sync", "yL", b2f[:], b2r)
    p.on("vector", lambda e: e.memset(onesf[:], 1.0))
    p.on("vector", lambda e: e.memset(onesrow[:], 1.0))
    p.wait("vector", "yL", t_c0)
    p.on("vector", lambda e, h=hv: e.tensor_copy(out=b2b[:], in_=b2f[:]).then_inc(h, 1))
    p.wait("vector", "vC", vC.add(1))

    psj = PSB[:, 0:512]
    for s8 in range(8):
        p.on("tensor", lambda e, s8=s8, h=ht: e.matmul(psj, onesf[:], scrow[0:1, 512 * s8:512 * (s8 + 1)],
                                                       start=True, stop=True).then_inc(h, 1))
        tj = tC.add(1)
        p.wait("vector", "tC", tj)
        p.on("vector", lambda e, s8=s8, h=hv: e.tensor_copy(out=J[:, 512 * s8:512 * (s8 + 1)],
                                                            in_=psj).then_inc(h, 1))
        tjc = vC.add(1)
        p.on("tensor", lambda e, t=tjc, h=hv: e.wait_ge(h, t))
    p.wait("vector", "vC", vC.v)
    for r in range(NT):
        p.on("vector", lambda e, r=r, h=hv: e.tensor_scalar(
            out=junk[:], in0=J[:], scalar1=mscb[:, r:r + 1], scalar2=0.0,
            op0=ALU.is_gt, op1=ALU.add, accum_out=cnt[:, r:r + 1]).then_inc(h, 1))
    p.wait("vector", "vC", vC.add(NT))
    p.on("vector", lambda e, h=hv: e.tensor_single_scalar(out=msel[:], in_=cnt[:], scalar=2048.0,
                                                          op=ALU.is_lt).then_inc(h, 1))
    tmsel = vC.add(1)
    p.barrier()
    sC1.close()

    h1g = p.sb("h1g", [128, 32 * T], BF16)
    oacc = p.sb("oacc", [128, NT * E], BF16)
    w2b = [p.sb(f"w2b{j}", [128, 32 * 256], BF16) for j in range(2)]
    w1r = w1.rearrange("(t p) f -> p t f", p=128)
    w2r = w2.rearrange("(t p) n -> p t n", p=128)
    psh = [PSA[:, 0:1024], PSA[:, 1024:2048]]
    psw = [PSB[:, 0:256], PSB[:, 512:768]]
    th_hist = {}
    gelu_hist = {}
    for fh in range(2):
        for ft in range(32):
            fg = 32 * fh + ft
            bufw = w1b[ft % 2]
            if fg >= 2:
                p.wait("gpsimd", "tC", th_hist[fg - 2])
            tw = _dma(p, "gpsimd", "gL", bufw[:].rearrange("p (t n) -> p t n", n=128),
                      w1r[:, :, 128 * fg:128 * (fg + 1)])
            p.wait("tensor", "gL", tw)
            if fg == 0:
                p.wait("tensor", "yL", tld0)
            if fg >= 2:
                p.wait("tensor", "sC", gelu_hist[fg - 2])
            ph = psh[fg % 2]
            for half in range(2):
                for et in range(NE):
                    laste = (et == NE - 1)
                    def mmh(e, half=half, et=et, bufw=bufw, ph=ph, laste=laste, lasth=(half == 1)):
                        rr = e.matmul(ph[:, 512 * half:512 * (half + 1)],
                                      bufw[:, 128 * et:128 * (et + 1)],
                                      xlt[:, T * et + 512 * half:T * et + 512 * (half + 1)],
                                      start=(et == 0), stop=laste)
                        if laste and lasth:
                            rr.then_inc(ht, 1)
                    p.on("tensor", mmh)
            th_ = tC.add(1)
            th_hist[fg] = th_
            p.wait("scalar", "tC", th_)
            p.on("scalar", lambda e, ft=ft, fg=fg, ph=ph, h=hs: e.activation(
                out=h1g[:, T * ft:T * (ft + 1)], in_=ph, func=AF.Gelu_apprx_tanh,
                bias=b1c[:, fg:fg + 1]).then_inc(h, 1))
            gelu_hist[fg] = sC.add(1)
        t_gelu_all = gelu_hist[32 * fh + 31]
        w2mm_hist = {}
        ev_hist = {}
        gi = 0
        for n in range(8):
            bufw2 = w2b[n % 2]
            if n >= 2:
                p.wait("gpsimd", "tC", w2mm_hist[n - 2])
            tw2 = _dma(p, "gpsimd", "gL", bufw2[:].rearrange("p (t n) -> p t n", n=256),
                       w2r[:, 32 * fh:32 * (fh + 1), 256 * n:256 * (n + 1)])
            p.wait("tensor", "gL", tw2)
            if n == 0:
                p.wait("tensor", "sC", t_gelu_all)
            for r in range(NT):
                pb = psw[gi % 2]
                if fh == 1:
                    buf = xbuf[r % 2]
                    tld = _dma(p, "sync", "yL", buf[:, 256 * n:256 * (n + 1)],
                               x3[128 * r:128 * (r + 1), 256 * n:256 * (n + 1)])
                if gi >= 2:
                    p.on("tensor", lambda e, t=ev_hist[gi - 2], h=hv: e.wait_ge(h, t))
                for c in range(32):
                    lastc = (c == 31)
                    stop = lastc and (fh == 0)
                    def mmw2(e, r=r, c=c, pb=pb, bufw2=bufw2, lastc=lastc, stop=stop, fh=fh):
                        rr = e.matmul(pb, h1g[:, T * c + 128 * r:T * c + 128 * (r + 1)],
                                      bufw2[:, 256 * c:256 * (c + 1)], start=(c == 0), stop=stop)
                        if lastc and fh == 0:
                            rr.then_inc(ht, 1)
                    p.on("tensor", mmw2)
                if fh == 1:
                    def mmb2(e, pb=pb, n=n, h=ht):
                        e.matmul(pb, onesrow[:], b2b[0:1, 256 * n:256 * (n + 1)],
                                 start=False, stop=True).then_inc(h, 1)
                    p.on("tensor", mmb2)
                tm_ = tC.add(1)
                w2mm_hist[n] = tm_
                p.wait("vector", "tC", tm_)
                if fh == 0:
                    p.on("vector", lambda e, r=r, n=n, pb=pb, h=hv: e.tensor_copy(
                        out=oacc[:, E * r + 256 * n:E * r + 256 * (n + 1)], in_=pb).then_inc(h, 1))
                    tev = vC.add(1)
                else:
                    p.wait("vector", "yL", tld)
                    p.on("vector", lambda e, r=r, n=n, pb=pb, h=hv: e.tensor_tensor(
                        out=tmp[:, 0:256], in0=pb, in1=oacc[:, E * r + 256 * n:E * r + 256 * (n + 1)],
                        op=ALU.add).then_inc(h, 1))
                    p.wait("vector", "vC", vC.add(1))
                    p.on("vector", lambda e, r=r, n=n, buf=buf, h=hv: e.scalar_tensor_tensor(
                        out=buf[:, 256 * n:256 * (n + 1)], in0=tmp[:, 0:256], scalar=msel[:, r:r + 1],
                        in1=buf[:, 256 * n:256 * (n + 1)], op0=ALU.mult, op1=ALU.add).then_inc(h, 1))
                    tev = vC.add(1)
                    p.wait("sync", "vC", tev)
                    tst = _dma(p, "sync", "yS", y[128 * r:128 * (r + 1), 256 * n:256 * (n + 1)],
                               buf[:, 256 * n:256 * (n + 1)])
                ev_hist[gi] = tev
                gi += 1
    p.wait("sync", "yS", p.ctr["yS"].v)
    p.emit()
    stack.close()
    return nc


_CACHE = {}


def _consts():
    identf = np.eye(128, dtype=np.float32)
    identb = np.eye(128, dtype=BF)
    return identf, identb, None, None


def _amask(is_start):
    am = np.zeros((40, 128, 128), dtype=np.float32)
    jj, ii = np.mgrid[0:128, 0:128]
    for r in range(8):
        am[r * 5 + 0][jj < ii] = NEG
        am[r * 5 + 4][jj > ii] = NEG
        if is_start:
            for m in range(5):
                if m < 4 - r:
                    am[r * 5 + m][:, :] = NEG
    return np.ascontiguousarray(am.astype(BF))


def run_A(X, inputs):
    identf, identb, _, _ = _consts()
    am_st, am_n = _amask(True), _amask(False)
    if "A" not in _CACHE:
        _CACHE["A"] = build_A()
    sclb = np.full((128, 1), np.float32(inputs["L0_ffn_scale"]), np.float32)
    scl1b = np.full((128, 1), np.float32(inputs["L0_ffn_scale1"]), np.float32)
    in_maps = []
    for c in range(8):
        b, k = c // 4, c % 4
        x0 = np.ascontiguousarray(X[b, T * k:T * (k + 1)])
        xhv = X[b, T * k - HALO:T * k] if k > 0 else np.zeros((HALO, E), np.float32)
        in_maps.append({
            "x0": x0, "xh": np.ascontiguousarray(xhv),
            "wqkv": np.ascontiguousarray(np.asarray(inputs["L0_attn_w_qkv"], np.float32)),
            "wout": np.ascontiguousarray(np.asarray(inputs["L0_attn_w_out"], np.float32)),
            "in2": np.ascontiguousarray(np.asarray(inputs["L0_ffn_input2"], np.float32)),
            "wk1": np.ascontiguousarray(np.asarray(inputs["L1_attn_Wk1"], np.float32)),
            "wq1": np.ascontiguousarray(np.asarray(inputs["L1_attn_Wq1"], np.float32)),
            "wk2": np.ascontiguousarray(np.asarray(inputs["L1_attn_Wk2"], np.float32)),
            "wv": np.ascontiguousarray(np.asarray(inputs["L1_attn_Wv"], np.float32)),
            "identf": identf, "identb": identb, "amask": am_st if k == 0 else am_n,
            "sclb": sclb, "scl1b": scl1b,
        })
    res = run_bass_kernel_spmd(_CACHE["A"], in_maps, core_ids=list(range(8)))
    return res.results


def run_B(outA, inputs):
    if "B" not in _CACHE:
        _CACHE["B"] = build_B()
    identf, _, _, _ = _consts()
    wq2 = np.ascontiguousarray(np.asarray(inputs["L1_attn_Wq2"], np.float32))
    wo = np.ascontiguousarray(np.asarray(inputs["L1_attn_Wo"], np.float32))
    nwr = np.ascontiguousarray(np.asarray(inputs["L1_attn_norm_w"], np.float32).reshape(NCH, 128).T)
    wrr = np.ascontiguousarray(np.asarray(inputs["L1_ffn_w_r"], np.float32).reshape(NE, 128).T)
    brr = np.full((1, 1), np.float32(inputs["L1_ffn_b_r"]), np.float32)
    in_maps = []
    for c in range(8):
        b, k = c // 4, c % 4
        predA = np.zeros((128, 96), np.float32)
        predH = np.zeros((128, 96), np.float32)
        predA[:, 0:32 * (3 - k)] = 1.0
        for idx, j in enumerate(range(k)):
            col = 32 * (3 - k + idx)
            aj = np.asarray(outA[4 * b + j]["acum"])[:, T - 1].astype(np.float32)
            hj = np.asarray(outA[4 * b + j]["hloc"])[:, T - 1].astype(np.float32)
            predA[:, col:col + 32] = aj.reshape(NCH, 128).T
            predH[:, col:col + 32] = hj.reshape(NCH, 128).T
        in_maps.append({
            "x2": np.ascontiguousarray(outA[c]["x2"]),
            "acum": np.ascontiguousarray(outA[c]["acum"]),
            "hloc": np.ascontiguousarray(outA[c]["hloc"]),
            "q1t": np.ascontiguousarray(outA[c]["q1o"].T),
            "predA": predA, "predH": predH,
            "wq2": wq2, "wo": wo, "nwr": nwr, "wrr": wrr, "brr": brr,
            "identf": identf,
        })
    res = run_bass_kernel_spmd(_CACHE["B"], in_maps, core_ids=list(range(8)))
    return res.results


def run_C(outB, inputs):
    if "C" not in _CACHE:
        _CACHE["C"] = build_C()
    w1 = np.ascontiguousarray(np.asarray(inputs["L1_ffn_w1"], np.float32))
    w2 = np.ascontiguousarray(np.asarray(inputs["L1_ffn_w2"], np.float32))
    b1r = np.ascontiguousarray(np.asarray(inputs["L1_ffn_b1"], np.float32).reshape(NF, 128).T)
    b2r = np.ascontiguousarray(np.asarray(inputs["L1_ffn_b2"], np.float32).reshape(1, E))
    in_maps = []
    for c in range(8):
        b, k = c // 4, c % 4
        scall = np.concatenate([outB[4 * b + j]["scso"][0] for j in range(4)]).reshape(1, 4096)
        msc = np.ascontiguousarray(outB[c]["scso"][0].reshape(NT, 128).T)
        in_maps.append({
            "x3": np.ascontiguousarray(outB[c]["x3"]),
            "xlnT3": np.ascontiguousarray(outB[c]["xlnT3o"]),
            "scall": np.ascontiguousarray(scall), "msc": msc,
            "w1": w1, "b1r": b1r, "w2": w2, "b2r": b2r,
        })
    res = run_bass_kernel_spmd(_CACHE["C"], in_maps, core_ids=list(range(8)))
    return res.results


def kernel(L0_attn_x, L0_attn_w_qkv, L0_attn_w_out, L0_ffn_input2, L0_ffn_scale,
           L0_ffn_scale1, L1_attn_Wq1, L1_attn_Wq2, L1_attn_Wk1, L1_attn_Wk2,
           L1_attn_Wv, L1_attn_Wo, L1_attn_norm_w, L1_ffn_w_r, L1_ffn_b_r,
           L1_ffn_w1, L1_ffn_b1, L1_ffn_w2, L1_ffn_b2):
    inputs = dict(L0_attn_w_qkv=L0_attn_w_qkv, L0_attn_w_out=L0_attn_w_out,
                  L0_ffn_input2=L0_ffn_input2, L0_ffn_scale=L0_ffn_scale,
                  L0_ffn_scale1=L0_ffn_scale1, L1_attn_Wq1=L1_attn_Wq1,
                  L1_attn_Wq2=L1_attn_Wq2, L1_attn_Wk1=L1_attn_Wk1,
                  L1_attn_Wk2=L1_attn_Wk2, L1_attn_Wv=L1_attn_Wv,
                  L1_attn_Wo=L1_attn_Wo, L1_attn_norm_w=L1_attn_norm_w,
                  L1_ffn_w_r=L1_ffn_w_r, L1_ffn_b_r=L1_ffn_b_r,
                  L1_ffn_w1=L1_ffn_w1, L1_ffn_b1=L1_ffn_b1,
                  L1_ffn_w2=L1_ffn_w2, L1_ffn_b2=L1_ffn_b2)
    X = np.ascontiguousarray(np.asarray(L0_attn_x, dtype=np.float32))
    B, S, _ = X.shape
    outA = run_A(X, inputs)
    outB = run_B(outA, inputs)
    outC = run_C(outB, inputs)
    out = np.empty((B, S, E), np.float32)
    for c in range(8):
        b, k = c // 4, c % 4
        out[b, T * k:T * (k + 1)] = outC[c]["y"]
    return out
